# version 06
# baseline (speedup 1.0000x reference)
import math
import numpy as np
import ml_dtypes
from contextlib import ExitStack

import concourse.bass as bass
import concourse.mybir as mybir
from concourse.bass_utils import run_bass_kernel_spmd

F32 = mybir.dt.float32
BF16 = mybir.dt.bfloat16
AF = mybir.ActivationFunctionType
ALU = mybir.AluOpType
AX = mybir.AxisListType

NCORES = 8
T = 2048
D = 1024
NT = T // 128
NEG = -60000.0
BF16NP = ml_dtypes.bfloat16


class B:

    def __init__(self):
        self.nc = bass.Bass("TRN2", target_bir_lowering=False)
        self.es = ExitStack()
        self.sems = {}

    def inp(self, name, shape, dtype=F32):
        return self.nc.dram_tensor(name, list(shape), dtype, kind="ExternalInput").ap()

    def outp(self, name, shape, dtype=F32):
        return self.nc.dram_tensor(name, list(shape), dtype, kind="ExternalOutput").ap()

    def sb(self, name, shape, dtype=F32):
        return self.es.enter_context(self.nc.sbuf_tensor(name, list(shape), dtype))

    def ps(self, name, shape, dtype=F32):
        return self.es.enter_context(self.nc.psum_tensor(name, list(shape), dtype))

    def sem(self, name):
        if name in self.sems:
            return name
        h = self.es.enter_context(self.nc.semaphore(name))
        self.sems[name] = [h, 0]
        return name

    def h(self, name):
        return self.sems[name][0]

    def inc(self, instr, name, k=1):
        s = self.sems[name]
        instr.then_inc(s[0], k)
        s[1] += k
        return s[1]

    def bump(self, name, k=1):
        self.sems[name][1] += k

    def val(self, name):
        return self.sems[name][1]

    def wait(self, eng, name, v=None):
        s = self.sems[name]
        eng.wait_ge(s[0], s[1] if v is None else v)

    def begin(self):
        self.block_cm = self.nc.Block()
        self.block = self.block_cm.__enter__()
        return self.block

    def finish(self):
        self.block_cm.__exit__(None, None, None)
        self.es.close()
        return self.nc


def ident_setup(b, blk, done, idents=True):
    if idents:
        b.ident_bf = b.sb("identbf", [128, 128], BF16)
        b.ident_f32 = b.sb("identf32", [128, 128], F32)
    b.ones_row = b.sb("onesrow", [1, 512], BF16)
    b.ones_col = b.sb("onescol", [128, 1], BF16)
    b.ones_col_f = b.sb("onescolf", [128, 1], F32)

    @blk.gpsimd
    def _(g):
        if idents:
            g.memset(b.ident_bf[:], 0.0)
            g.affine_select(out=b.ident_bf[:], in_=b.ident_bf[:],
                            compare_op=ALU.not_equal, fill=1.0, base=0,
                            pattern=[[-1, 128]], channel_multiplier=1)
            g.memset(b.ident_f32[:], 0.0)
            g.affine_select(out=b.ident_f32[:], in_=b.ident_f32[:],
                            compare_op=ALU.not_equal, fill=1.0, base=0,
                            pattern=[[-1, 128]], channel_multiplier=1)
        g.memset(b.ones_row[:], 1.0)
        g.memset(b.ones_col_f[:], 1.0)
        b.inc(g.memset(b.ones_col[:], 1.0), done)


def gemm(b, blk, tag, groups, psums, wait_list=(), done=None):
    smm = b.sem(tag + "_m")
    sev = done if done is not None else b.sem(tag + "_e")
    sbase = b.val(sev)
    nb = len(psums)

    @blk.tensor
    def _(t):
        for w, v in wait_list:
            b.wait(t, w, v)
        for gi, g in enumerate(groups):
            for w, v in g.get("waits", ()):
                t.wait_ge(b.h(w), v)
            if gi >= nb:
                t.wait_ge(b.h(sev), sbase + gi - nb + 1)
            pa = g["pslice"](psums[gi % nb])
            mms = g["mms"]
            n = len(mms)
            for mi, (lh, rh) in enumerate(mms):
                ins = t.matmul(pa, lh, rh, start=(mi == 0), stop=(mi == n - 1))
            ins.then_inc(b.h(smm), 1)
        b.bump(smm, len(groups))

    engs = {}
    for gi, g in enumerate(groups):
        engs.setdefault(g["evac_eng"], []).append(gi)
    for eng_name, gis in engs.items():
        def mk(eng_name=eng_name, gis=gis):
            def f(e):
                for w, v in wait_list:
                    b.wait(e, w, v)
                for gi in gis:
                    g = groups[gi]
                    e.wait_ge(b.h(smm), gi + 1)
                    last = g["evac"](e, g["pslice"](psums[gi % nb]))
                    last.then_inc(b.h(sev), 1)
            return f
        getattr(blk, eng_name)(mk())
    b.bump(sev, len(groups))


def dma_transpose_tiles(b, blk, tag, tiles, queue, wait_list=(), done=None,
                        waits=None):
    def body(e):
        for w, v in wait_list:
            b.wait(e, w, v)
        for i, (src, dst) in enumerate(tiles):
            if waits is not None and waits[i]:
                for w, v in waits[i]:
                    e.wait_ge(b.h(w), v)
            b.inc(e.dma_start(out=dst, in_=src, transpose=True), done, 16)
    getattr(blk, queue)(body)


def ln_emit(b, blk, tag, ntiles, d, get_x, wait_for_tile, done, eps=1e-5,
            rms=False, post=None):
    SQ = b.sb(tag + "_sq", [128, 2 * d], F32)
    ST = b.sb(tag + "_st", [128, 8 * ntiles], F32)
    ssq = b.sem(tag + "_q")
    sred = b.sem(tag + "_r")
    srstd = b.sem(tag + "_s")
    sflu = b.sem(tag + "_f")
    svc = b.sem(tag + "_v")

    @blk.scalar
    def _(s):
        for i in range(ntiles):
            for w, v in wait_for_tile(i):
                b.wait(s, w, v)
            buf = SQ[:, (i % 2) * d:(i % 2 + 1) * d]
            s.activation(out=buf, in_=get_x(i), func=AF.Square).then_inc(b.h(ssq), 1)
            st = ST[:, i * 8:(i + 1) * 8]
            s.wait_ge(b.h(sred), i + 1)
            s.activation(out=st[:, 6:7], in_=st[:, 2:3],
                         func=AF.Ln).then_inc(b.h(sflu), 1)
            s.wait_ge(b.h(sflu), i + 1)
            s.activation(out=st[:, 3:4], in_=st[:, 6:7], func=AF.Exp,
                         scale=-0.5).then_inc(b.h(srstd), 1)
        b.bump(ssq, ntiles)
        b.bump(sflu, ntiles)
        b.bump(srstd, ntiles)

    @blk.vector
    def _(v):
        for i in range(ntiles):
            xi = get_x(i)
            buf = SQ[:, (i % 2) * d:(i % 2 + 1) * d]
            st = ST[:, i * 8:(i + 1) * 8]
            v.wait_ge(b.h(ssq), i + 1)
            if not rms:
                v.reduce_sum(st[:, 1:2], buf, axis=AX.X)
                v.reduce_sum(st[:, 0:1], xi, axis=AX.X).then_inc(b.h(svc), 1)
                v.wait_ge(b.h(svc), 3 * i + 1)
                v.tensor_scalar_mul(st[:, 0:1], st[:, 0:1], 1.0 / d).then_inc(b.h(svc), 1)
                v.wait_ge(b.h(svc), 3 * i + 2)
                v.tensor_mul(st[:, 4:5], st[:, 0:1], st[:, 0:1])
                v.tensor_scalar(st[:, 5:6], st[:, 1:2], 1.0 / d, float(eps),
                                ALU.mult, ALU.add).then_inc(b.h(svc), 1)
                v.wait_ge(b.h(svc), 3 * i + 3)
                v.tensor_sub(st[:, 2:3], st[:, 5:6], st[:, 4:5]).then_inc(b.h(sred), 1)
            else:
                v.reduce_sum(st[:, 1:2], buf, axis=AX.X).then_inc(b.h(svc), 1)
                v.wait_ge(b.h(svc), 3 * i + 1)
                v.tensor_scalar(st[:, 2:3], st[:, 1:2], 1.0 / d, float(eps),
                                ALU.mult, ALU.add).then_inc(b.h(sred), 1)
                v.engine_nop().then_inc(b.h(svc), 1)
                v.engine_nop().then_inc(b.h(svc), 1)
            v.wait_ge(b.h(srstd), i + 1)
            last = post(v, i, st)
            last.then_inc(b.h(done), 1)
        b.bump(sred, ntiles)
        b.bump(svc, 3 * ntiles)
        b.bump(done, ntiles)


def ln_apply(v, xi, ui, st, rms=False):
    if not rms:
        return v.tensor_scalar(ui, xi, st[:, 0:1], st[:, 3:4], ALU.subtract, ALU.mult)
    return v.tensor_scalar(ui, xi, st[:, 3:4], None, ALU.mult)


def ln_fm_emit(b, blk, tag, ntiles, d, get_x, wait_for_tile, fm_dst, done,
               eps=1e-5, rms=False, u_dst=None, gamma_b=None, beta_b=None):
    nch = d // 128
    UBR = b.sb(tag + "_ub", [128, 2 * d], BF16)
    SQ = b.sb(tag + "_sq", [128, 2 * d], F32)
    ST = b.sb(tag + "_st", [128, 8 * ntiles], F32)
    ssq = b.sem(tag + "_q")
    sred = b.sem(tag + "_r")
    srstd = b.sem(tag + "_s")
    sflu = b.sem(tag + "_f")
    svc = b.sem(tag + "_v")
    sln = b.sem(tag + "_a")
    base_done = b.val(done)

    @blk.scalar
    def _(s):
        for i in range(ntiles):
            for w, v in wait_for_tile(i):
                b.wait(s, w, v)
            buf = SQ[:, (i % 2) * d:(i % 2 + 1) * d]
            s.activation(out=buf, in_=get_x(i), func=AF.Square).then_inc(b.h(ssq), 1)
            st = ST[:, i * 8:(i + 1) * 8]
            s.wait_ge(b.h(sred), i + 1)
            s.activation(out=st[:, 6:7], in_=st[:, 2:3],
                         func=AF.Ln).then_inc(b.h(sflu), 1)
            s.wait_ge(b.h(sflu), i + 1)
            s.activation(out=st[:, 3:4], in_=st[:, 6:7], func=AF.Exp,
                         scale=-0.5).then_inc(b.h(srstd), 1)
            if i >= 1:
                s.wait_ge(b.h(sln), i)
                for c in range(nch):
                    s.dma_start(out=fm_dst(i - 1, c),
                                in_=UBR[:, ((i - 1) % 2) * d + c * 128:
                                        ((i - 1) % 2) * d + (c + 1) * 128],
                                transpose=True).then_inc(b.h(done), 16)
        s.wait_ge(b.h(sln), ntiles)
        for c in range(nch):
            s.dma_start(out=fm_dst(ntiles - 1, c),
                        in_=UBR[:, ((ntiles - 1) % 2) * d + c * 128:
                                ((ntiles - 1) % 2) * d + (c + 1) * 128],
                        transpose=True).then_inc(b.h(done), 16)
        b.bump(ssq, ntiles)
        b.bump(sflu, ntiles)
        b.bump(srstd, ntiles)
        b.bump(done, 16 * nch * ntiles)

    @blk.vector
    def _(v):
        for i in range(ntiles):
            xi = get_x(i)
            buf = SQ[:, (i % 2) * d:(i % 2 + 1) * d]
            st = ST[:, i * 8:(i + 1) * 8]
            v.wait_ge(b.h(ssq), i + 1)
            if not rms:
                v.reduce_sum(st[:, 1:2], buf, axis=AX.X)
                v.reduce_sum(st[:, 0:1], xi, axis=AX.X).then_inc(b.h(svc), 1)
                v.wait_ge(b.h(svc), 3 * i + 1)
                v.tensor_scalar_mul(st[:, 0:1], st[:, 0:1], 1.0 / d).then_inc(b.h(svc), 1)
                v.wait_ge(b.h(svc), 3 * i + 2)
                v.tensor_mul(st[:, 4:5], st[:, 0:1], st[:, 0:1])
                v.tensor_scalar(st[:, 5:6], st[:, 1:2], 1.0 / d, float(eps),
                                ALU.mult, ALU.add).then_inc(b.h(svc), 1)
                v.wait_ge(b.h(svc), 3 * i + 3)
                v.tensor_sub(st[:, 2:3], st[:, 5:6], st[:, 4:5]).then_inc(b.h(sred), 1)
            else:
                v.reduce_sum(st[:, 1:2], buf, axis=AX.X).then_inc(b.h(svc), 1)
                v.wait_ge(b.h(svc), 3 * i + 1)
                v.tensor_scalar(st[:, 2:3], st[:, 1:2], 1.0 / d, float(eps),
                                ALU.mult, ALU.add).then_inc(b.h(sred), 1)
                v.engine_nop().then_inc(b.h(svc), 1)
                v.engine_nop().then_inc(b.h(svc), 1)
            v.wait_ge(b.h(srstd), i + 1)
            ub = UBR[:, (i % 2) * d:(i % 2 + 1) * d]
            if i >= 2:
                v.wait_ge(b.h(done), base_done + 16 * nch * (i - 1))
            if u_dst is not None:
                ln_apply(v, xi, u_dst(i), st, rms=rms)
                if gamma_b is not None:
                    v.tensor_mul(u_dst(i), u_dst(i), gamma_b)
                if beta_b is not None:
                    v.tensor_add(u_dst(i), u_dst(i), beta_b)
                ins = v.tensor_copy(ub, u_dst(i))
            else:
                if gamma_b is None and beta_b is None:
                    ins = ln_apply(v, xi, ub, st, rms=rms)
                else:
                    ln_apply(v, xi, ub, st, rms=rms)
                    if gamma_b is not None:
                        ins = v.tensor_mul(ub, ub, gamma_b)
                    if beta_b is not None:
                        ins = v.tensor_add(ub, ub, beta_b)
            ins.then_inc(b.h(sln), 1)
        b.bump(sred, ntiles)
        b.bump(svc, 3 * ntiles)
        b.bump(sln, ntiles)
    return dict(sln=sln)


def shard_rows(x, core):
    bidx, qc = core // 4, core % 4
    return np.ascontiguousarray(x.reshape(2, 8192, -1)[bidx, qc * T:(qc + 1) * T])


def shard_rows_halo(x, core, halo):
    bidx, qc = core // 4, core % 4
    s = qc * T
    out = np.zeros((T + halo, x.shape[-1]), np.float32)
    lo = max(0, s - halo)
    out[halo - (s - lo):] = x.reshape(2, 8192, -1)[bidx, lo:s + T]
    return out


def unshard(parts):
    a = np.stack(parts).reshape(2, 8192, -1)
    return a


def bfc(x):
    return np.ascontiguousarray(x).astype(BF16NP)


def a0_masks_np():
    m = np.zeros((6, 128, 512), np.float32)
    jd = np.arange(128)[:, None]
    iq = np.arange(512)[None, :]
    for r in range(6):
        diff = 256 - 128 * r + iq - jd
        m[r] = ((diff >= 0) & (diff <= 255)).astype(np.float32)
    return m.astype(BF16NP)


def a0_flagbias_np(qc):
    f = np.zeros((4, 6, 128), np.float32)
    if qc == 0:
        f[0, 0, :] = NEG
        f[0, 1, :] = NEG
    return f


HALO0 = 256
NTE0 = (T + HALO0) // 128
TE0 = T + HALO0


def mk_a0qkv(stage=4):
    b = B()
    xin = b.inp("xin", [TE0, D])
    wq = b.inp("wq", [D, D])
    wk = b.inp("wk", [256, D])
    wv = b.inp("wv", [256, D])
    bqv = b.inp("bqv", [8, 128])
    bkv = b.inp("bkv", [2, 128])
    bvv = b.inp("bvv", [256])
    oqf = b.outp("qf", [8, 128, T], BF16)
    okf = b.outp("kf", [2, 128, TE0], BF16)
    ovt = b.outp("vt", [TE0, 256], BF16)

    XR = b.sb("XR", [128, 4 * D])
    UF = b.sb("UF", [128, 8 * TE0], BF16)
    QF = b.sb("QF", [128, 8 * T], BF16)
    KF = b.sb("KF", [128, 2 * TE0], BF16)
    VT = b.sb("VT", [128, NTE0 * 256], BF16)
    WN = b.sb("WN", [128, 8 * D], BF16)
    WQT = b.sb("WQT", [128, 8 * D], BF16)
    WKN = b.sb("WKN", [128, 2 * D], BF16)
    WKT = b.sb("WKT", [128, 8 * 256], BF16)
    WVN = b.sb("WVN", [128, 2 * D], BF16)
    WVT = b.sb("WVT", [128, 8 * 256], BF16)
    BQ = b.sb("BQ", [128, 8], F32)
    BKB = b.sb("BKB", [128, 2], F32)
    BVB = b.sb("BVB", [1, 256], BF16)
    BVBc = b.sb("BVBc", [128, 256], F32)
    BQS = b.sb("BQS", [8, 128], F32)
    BKS = b.sb("BKS", [2, 128], F32)

    PS = [b.ps(f"ps{i}", [128, 512], F32) for i in range(3)]
    PSB = b.ps("psb", [128, 384], F32)

    blk = b.begin()
    sini = b.sem("sini")
    sld = b.sem("sld")
    sw = b.sem("sw")
    swt = b.sem("swt")
    sln = b.sem("sln")
    str_ = b.sem("str")
    sbias = b.sem("sbias")
    sqkv = b.sem("sqkv")
    sst = b.sem("sst")

    ident_setup(b, blk, sini)

    xt = xin.rearrange("(n p) d -> n p d", p=128)
    b.sem("lnU_a")

    @blk.sync
    def _(e):
        b.inc(e.dma_start(out=BQS[:], in_=bqv[:, :]), sld, 16)
        b.inc(e.dma_start(out=BKS[:], in_=bkv[:, :]), sld, 16)
        for i in range(NTE0):
            if i >= 4:
                e.wait_ge(b.h("lnU_a"), i - 3)
            b.inc(e.dma_start(out=XR[:, (i % 4) * D:(i % 4 + 1) * D], in_=xt[i]),
                  sld, 16)

    @blk.gpsimd
    def _(g):
        b.inc(g.dma_start(out=WN[:].rearrange("p (c i) -> p c i", i=D),
                          in_=wq.rearrange("(c p) i -> p c i", p=128)), sw, 16)
        b.inc(g.dma_start(out=WKN[:].rearrange("p (c i) -> p c i", i=D),
                          in_=wk.rearrange("(c p) i -> p c i", p=128)), sw, 16)
        b.inc(g.dma_start(out=WVN[:].rearrange("p (c i) -> p c i", i=D),
                          in_=wv.rearrange("(c p) i -> p c i", p=128)), sw, 16)
        b.inc(g.dma_start(out=BVB[0:1, :], in_=bvv[None, :]), sw, 16)

    sbq = b.sem("sbq")
    sbqc = b.sem("sbqc")

    @blk.tensor
    def _(t):
        b.wait(t, sini)
        b.wait(t, sld, 32)
        b.wait(t, sw, 64)
        t.transpose(PSB[:, 0:8], BQS[:], b.ident_f32[0:8, 0:8])
        t.nop().then_inc(b.h(sbq), 1)
        t.wait_ge(b.h(sbqc), 1)
        t.transpose(PSB[:, 8:10], BKS[:], b.ident_f32[0:2, 0:2])
        t.nop().then_inc(b.h(sbq), 1)
        t.wait_ge(b.h(sbqc), 2)
        t.matmul(PSB[:, 128:256], b.ones_row[0:1, 0:128], BVB[0:1, 0:128],
                 start=True, stop=True).then_inc(b.h(sbq), 1)
        t.wait_ge(b.h(sbqc), 3)
        t.matmul(PSB[:, 256:384], b.ones_row[0:1, 0:128], BVB[0:1, 128:256],
                 start=True, stop=True).then_inc(b.h(sbq), 1)
    b.bump(sbq, 4)

    @blk.vector
    def _(v):
        v.wait_ge(b.h(sbq), 1)
        v.tensor_copy(BQ[:], PSB[:, 0:8]).then_inc(b.h(sbqc), 1)
        v.wait_ge(b.h(sbq), 2)
        v.tensor_copy(BKB[:], PSB[:, 8:10]).then_inc(b.h(sbqc), 1)
        v.wait_ge(b.h(sbq), 3)
        v.tensor_copy(BVBc[:, 0:128], PSB[:, 128:256]).then_inc(b.h(sbqc), 1)
        v.wait_ge(b.h(sbq), 4)
        b.inc(v.tensor_copy(BVBc[:, 128:256], PSB[:, 256:384]), sbias)
    b.bump(sbqc, 3)

    def get_x(i):
        return XR[:, (i % 4) * D:(i % 4 + 1) * D]

    lns = ln_fm_emit(b, blk, "lnU", NTE0, D, get_x,
                     lambda i: [(sld, 32 + 16 * (i + 1)), (sini, 1)],
                     lambda i, c: UF[:, c * TE0 + i * 128:c * TE0 + (i + 1) * 128],
                     str_)

    if stage == 1:
        dbg = b.outp("dbg", [128, 8 * TE0], BF16)

        @blk.sync
        def _(e):
            b.wait(e, str_)
            b.inc(e.dma_start(out=dbg, in_=UF[:]), sst, 16)
            b.wait(e, sst)
        return b.finish()

    wtiles = []
    for co in range(8):
        for ci in range(8):
            wtiles.append((WN[:, co * D + ci * 128:co * D + (ci + 1) * 128],
                           WQT[:, ci * D + co * 128:ci * D + (co + 1) * 128]))
    for co in range(2):
        for ci in range(8):
            wtiles.append((WKN[:, co * D + ci * 128:co * D + (ci + 1) * 128],
                           WKT[:, ci * 256 + co * 128:ci * 256 + (co + 1) * 128]))
    for co in range(2):
        for ci in range(8):
            wtiles.append((WVN[:, co * D + ci * 128:co * D + (ci + 1) * 128],
                           WVT[:, ci * 256 + co * 128:ci * 256 + (co + 1) * 128]))
    dma_transpose_tiles(b, blk, "trW", wtiles, "scalar",
                        [(sw, None)], swt)

    if stage == 2:
        dbg = b.outp("dbg", [128, 8 * D], BF16)

        @blk.sync
        def _(e):
            b.wait(e, swt)
            b.wait(e, str_)
            b.inc(e.dma_start(out=dbg, in_=WQT[:]), sst, 16)
            b.wait(e, sst)
        return b.finish()

    groups = []
    for h in range(8):
        for g4 in range(4):
            c0 = HALO0 + g4 * 512
            mms = [(WQT[:, ci * D + h * 128:ci * D + (h + 1) * 128],
                    UF[:, ci * TE0 + c0:ci * TE0 + c0 + 512]) for ci in range(8)]

            def ev(e, pa, h=h, g4=g4):
                return e.activation(out=QF[:, h * T + g4 * 512:h * T + (g4 + 1) * 512],
                                    in_=pa, func=AF.Identity, bias=BQ[:, h:h + 1])
            groups.append(dict(mms=mms, pslice=lambda p: p[:, 0:512],
                               evac_eng="scalar", evac=ev))
    for kh in range(2):
        for g0 in range(0, TE0, 512):
            wdt = min(512, TE0 - g0)
            mms = [(WKT[:, ci * 256 + kh * 128:ci * 256 + (kh + 1) * 128],
                    UF[:, ci * TE0 + g0:ci * TE0 + g0 + wdt]) for ci in range(8)]

            def ev(e, pa, kh=kh, g0=g0, wdt=wdt):
                return e.activation(out=KF[:, kh * TE0 + g0:kh * TE0 + g0 + wdt],
                                    in_=pa, func=AF.Identity, bias=BKB[:, kh:kh + 1])
            groups.append(dict(mms=mms, pslice=lambda p, wdt=wdt: p[:, 0:wdt],
                               evac_eng="scalar", evac=ev))
    for tt in range(NTE0):
        mms = [(UF[:, ci * TE0 + tt * 128:ci * TE0 + (tt + 1) * 128],
                WVT[:, ci * 256:(ci + 1) * 256]) for ci in range(8)]

        def ev(e, pa, tt=tt):
            return e.tensor_add(VT[:, tt * 256:(tt + 1) * 256], pa, BVBc[:])
        groups.append(dict(mms=mms, pslice=lambda p: p[:, 0:256],
                           evac_eng="vector", evac=ev))

    gemm(b, blk, "proj", groups, PS,
         wait_list=[(swt, None), (str_, None), (sbias, None)], done=sqkv)

    @blk.sync
    def _(e):
        b.wait(e, sqkv)
        b.inc(e.dma_start(out=oqf.rearrange("h p t -> p h t"),
                          in_=QF[:].rearrange("p (h t) -> p h t", t=T)), sst, 16)
        b.inc(e.dma_start(out=okf.rearrange("h p t -> p h t"),
                          in_=KF[:].rearrange("p (h t) -> p h t", t=TE0)), sst, 16)
        b.inc(e.dma_start(out=ovt.rearrange("(n p) d -> p n d", p=128),
                          in_=VT[:].rearrange("p (n d) -> p n d", d=256)), sst, 16)
        b.wait(e, sst)

    return b.finish()


def mk_a0at(stage=4):
    b = B()
    qf = b.inp("qf", [8, 128, T], BF16)
    kf = b.inp("kf", [2, 128, TE0], BF16)
    vt = b.inp("vt", [TE0, 256], BF16)
    masks = b.inp("masks", [6, 128, 512], BF16)
    flagb = b.inp("flagb", [128, 24])
    sinkb = b.inp("sinkb", [1])
    wo = b.inp("wo", [D, D])
    bov = b.inp("bov", [D])
    xown = b.inp("xown", [T, D])
    xout = b.outp("xout", [T, D])

    QF = b.sb("QF", [128, 8 * T], BF16)
    KF = b.sb("KF", [128, 2 * TE0], BF16)
    VT = b.sb("VT", [128, NTE0 * 256], BF16)
    MSK = b.sb("MSK", [128, 6 * 512], BF16)
    FLG = b.sb("FLG", [128, 24], F32)
    SK = b.sb("SK", [1, 4], F32)
    ET = b.sb("ET", [128, 4 * 512], BF16)
    AVH = b.sb("AVH", [128, T], BF16)
    DNR = b.sb("DNR", [1, 1024], F32)
    RRB = b.sb("RRB", [1, 512], BF16)
    WN = b.sb("WN", [128, 8 * D], BF16)
    WOT = b.sb("WOT", [128, 8 * D], BF16)
    BOB = b.sb("BOB", [128, D], F32)
    BOR = b.sb("BOR", [1, D], BF16)
    O = b.sb("O", [128, NT * D])
    XR = b.sb("XR", [128, 4 * D])
    TMP = b.sb("TMP", [128, 2 * D])

    PJ = [b.ps(f"pj{i}", [128, 512], F32) for i in range(3)]
    PSA = b.ps("psa", [128, 512], F32)
    PSD = b.ps("psd", [1, 512], F32)
    PSR = b.ps("psr", [128, 512], F32)
    PO = [b.ps(f"po{i}", [128, 512], F32) for i in range(2)]

    blk = b.begin()
    sini = b.sem("sini")
    sld = b.sem("sld")
    sw = b.sem("sw")
    swt = b.sem("swt")
    sbias = b.sem("sbias")
    ssc = b.sem("ssc")
    sex = b.sem("sex")
    smk = b.sem("smk")
    savr = b.sem("savr")
    seva = b.sem("seva")
    sdn = b.sem("sdn")
    srr = b.sem("srr")
    sbr = b.sem("sbr")
    snm = b.sem("snm")
    swom = b.sem("swom")
    swo = b.sem("swo")
    sflu2 = b.sem("sflu2")
    sfin = b.sem("sfin")
    sst = b.sem("sst")

    ident_setup(b, blk, sini)

    @blk.sync
    def _(e):
        b.inc(e.dma_start(out=QF[:].rearrange("p (h t) -> p h t", t=T),
                          in_=qf.rearrange("h p t -> p h t")), sld, 16)
        b.inc(e.dma_start(out=KF[:].rearrange("p (h t) -> p h t", t=TE0),
                          in_=kf.rearrange("h p t -> p h t")), sld, 16)
        b.inc(e.dma_start(out=VT[:].rearrange("p (n d) -> p n d", d=256),
                          in_=vt.rearrange("(n p) d -> p n d", p=128)), sld, 16)
        b.inc(e.dma_start(out=MSK[:].rearrange("p (n d) -> p n d", d=512),
                          in_=masks.rearrange("n p d -> p n d")), sld, 16)
        b.inc(e.dma_start(out=FLG[:], in_=flagb[:, :]), sld, 16)
        b.inc(e.dma_start(out=SK[0:1, 0:1], in_=sinkb[None, :]), sld, 16)
    SLD0 = b.val("sld")

    @blk.gpsimd
    def _(g):
        b.inc(g.dma_start(out=WN[:].rearrange("p (c i) -> p c i", i=D),
                          in_=wo.rearrange("(c p) i -> p c i", p=128)), sw, 16)
        b.inc(g.dma_start(out=BOR[0:1, :], in_=bov[None, :]), sw, 16)

    @blk.scalar
    def _(s):
        b.wait(s, sld, SLD0)
        b.inc(s.activation(out=SK[0:1, 1:2], in_=SK[0:1, 0:1], func=AF.Exp), sbias)

    sbo = b.sem("sbo")

    @blk.tensor
    def _(t):
        b.wait(t, sini)
        b.wait(t, sld, SLD0)
        b.wait(t, sw, 32)
        t.matmul(PSR[:, 0:512], b.ones_row[0:1, 0:128], BOR[0:1, 0:512],
                 start=True, stop=True).then_inc(b.h(sbo), 1)
        t.matmul(PSA[:, 0:512], b.ones_row[0:1, 0:128], BOR[0:1, 512:1024],
                 start=True, stop=True).then_inc(b.h(sbo), 1)
    b.bump(sbo, 2)

    @blk.vector
    def _(v):
        v.wait_ge(b.h(sbo), 1)
        v.tensor_copy(BOB[:, 0:512], PSR[:, 0:512])
        v.wait_ge(b.h(sbo), 2)
        b.inc(v.tensor_copy(BOB[:, 512:1024], PSA[:, 0:512]), sbias)

    wtiles = []
    for co in range(8):
        for ci in range(8):
            wtiles.append((WN[:, co * D + ci * 128:co * D + (ci + 1) * 128],
                           WOT[:, ci * D + co * 128:ci * D + (co + 1) * 128]))
    dma_transpose_tiles(b, blk, "trW", wtiles, "scalar",
                        [(sw, None)], swt)

    if stage == 1:
        dbg = b.outp("dbg", [128, 8 * D], BF16)

        @blk.sync
        def _(e):
            b.wait(e, swt)
            b.wait(e, sbias, 2)
            b.inc(e.dma_start(out=dbg, in_=WOT[:]), sst, 16)
            b.wait(e, sst)
        return b.finish()

    NG = 4
    NHG = (8 if stage >= 3 else 1) * NG
    NH_USED = 8 if stage >= 3 else 1
    inv = 1.0 / math.sqrt(128.0)

    @blk.tensor
    def _(t):
        b.wait(t, sld, SLD0)
        b.wait(t, sbias, 2)
        b.wait(t, swt, b.val(swt))

        def sc(hg, h, g, kh, r):
            glob = hg * 6 + r
            if glob >= 2:
                t.wait_ge(b.h(sex), glob - 1)
            jd = 4 * g + r
            t.matmul(PJ[glob % 2][:, 0:512],
                     KF[:, kh * TE0 + jd * 128:kh * TE0 + (jd + 1) * 128],
                     QF[:, h * T + g * 512:h * T + (g + 1) * 512],
                     start=True, stop=True).then_inc(b.h(ssc), 1)

        def av(hg, h, g, kh, r):
            glob = hg * 6 + r
            jd = 4 * g + r
            t.wait_ge(b.h(smk), glob + 1)
            if r == 0 and hg >= 1:
                t.wait_ge(b.h(seva), hg)
                t.wait_ge(b.h(sdn), hg)
            t.matmul(PSA[:, 0:512],
                     VT[:, jd * 256 + kh * 128:jd * 256 + (kh + 1) * 128],
                     ET[:, (glob % 2) * 512:(glob % 2 + 1) * 512],
                     start=(r == 0), stop=(r == 5), skip_group_check=True)
            ins = t.matmul(PSD[0:1, 0:512], b.ones_col[:, 0:1],
                           ET[:, (glob % 2) * 512:(glob % 2 + 1) * 512],
                           start=(r == 0), stop=(r == 5), skip_group_check=True)
            ins.then_inc(b.h(savr), 1)

        for hg in range(NHG):
            h, g = divmod(hg, NG)
            kh = h // 4
            sc(hg, h, g, kh, 0)
            sc(hg, h, g, kh, 1)
            av(hg, h, g, kh, 0)
            for r in range(2, 6):
                sc(hg, h, g, kh, r)
                av(hg, h, g, kh, r - 1)
            av(hg, h, g, kh, 5)
            t.wait_ge(b.h(srr), hg + 1)
            if hg >= 1:
                t.wait_ge(b.h(snm), hg)
            t.matmul(PSR[:, 0:512], b.ones_row[0:1, 0:128], RRB[0:1, 0:512],
                     start=True, stop=True).then_inc(b.h(sbr), 1)
            if g == NG - 1 and stage >= 3:
                t.wait_ge(b.h(snm), hg + 1)
                for tt in range(NT):
                    for og in range(2):
                        gi = h * NT * 2 + tt * 2 + og
                        if gi >= 2:
                            t.wait_ge(b.h(swo), gi - 1)
                        t.matmul(PO[gi % 2][:, 0:512],
                                 AVH[:, tt * 128:(tt + 1) * 128],
                                 WOT[:, h * D + og * 512:h * D + (og + 1) * 512],
                                 start=True, stop=True).then_inc(b.h(swom), 1)
        b.bump(ssc, NHG * 6)
        b.bump(savr, NHG * 6)
        b.bump(sbr, NHG)
        if stage >= 3:
            b.bump(swom, 8 * NT * 2)

    @blk.scalar
    def _(s):
        for hg in range(NHG):
            h, g = divmod(hg, NG)
            for r in range(6):
                glob = hg * 6 + r
                s.wait_ge(b.h(ssc), glob + 1)
                if glob >= 2:
                    s.wait_ge(b.h(savr), glob - 1)
                s.activation(out=ET[:, (glob % 2) * 512:(glob % 2 + 1) * 512],
                             in_=PJ[glob % 2][:, 0:512], func=AF.Exp, scale=inv,
                             bias=FLG[:, g * 6 + r:g * 6 + r + 1]).then_inc(b.h(sex), 1)
            if g == 0 and h >= 1 and stage >= 3:
                s.wait_ge(b.h(swom), h * NT * 2)
            s.wait_ge(b.h(savr), (hg + 1) * 6)
            s.activation(out=AVH[:, g * 512:(g + 1) * 512], in_=PSA[:, 0:512],
                         func=AF.Copy).then_inc(b.h(seva), 1)
            s.wait_ge(b.h(sdn), hg + 1)
            if hg >= 1:
                s.wait_ge(b.h(sbr), hg)
            s.activation(out=DNR[0:1, 512:1024], in_=DNR[0:1, 0:512],
                         func=AF.Ln).then_inc(b.h(sflu2), 1)
            s.wait_ge(b.h(sflu2), hg + 1)
            s.activation(out=RRB[0:1, 0:512], in_=DNR[0:1, 512:1024], func=AF.Exp,
                         scale=-1.0).then_inc(b.h(srr), 1)
        b.bump(sex, NHG * 6)
        b.bump(seva, NHG)
        b.bump(sflu2, NHG)
        b.bump(srr, NHG)

    @blk.vector
    def _(v):
        for hg in range(NHG):
            h, g = divmod(hg, NG)
            for r in range(6):
                glob = hg * 6 + r
                v.wait_ge(b.h(sex), glob + 1)
                v.tensor_mul(ET[:, (glob % 2) * 512:(glob % 2 + 1) * 512],
                             ET[:, (glob % 2) * 512:(glob % 2 + 1) * 512],
                             MSK[:, r * 512:(r + 1) * 512]).then_inc(b.h(smk), 1)
            v.wait_ge(b.h(savr), (hg + 1) * 6)
            if hg >= 1:
                v.wait_ge(b.h(srr), hg)
            v.tensor_scalar(DNR[0:1, 0:512], PSD[0:1, 0:512], SK[0:1, 1:2],
                            None, ALU.add).then_inc(b.h(sdn), 1)
            v.wait_ge(b.h(sbr), hg + 1)
            v.wait_ge(b.h(seva), hg + 1)
            v.tensor_mul(AVH[:, g * 512:(g + 1) * 512],
                         AVH[:, g * 512:(g + 1) * 512],
                         PSR[:, 0:512]).then_inc(b.h(snm), 1)
            if g == NG - 1 and stage >= 3:
                for tt in range(NT):
                    for og in range(2):
                        gi = h * NT * 2 + tt * 2 + og
                        v.wait_ge(b.h(swom), gi + 1)
                        dst = O[:, tt * D + og * 512:tt * D + (og + 1) * 512]
                        if h == 0:
                            ins = v.tensor_add(dst, PO[gi % 2][:, 0:512],
                                               BOB[:, og * 512:(og + 1) * 512])
                        else:
                            ins = v.tensor_add(dst, PO[gi % 2][:, 0:512], dst)
                        ins.then_inc(b.h(swo), 1)
        b.bump(smk, NHG * 6)
        b.bump(sdn, NHG)
        b.bump(snm, NHG)
        if stage >= 3:
            b.bump(swo, 8 * NT * 2)

    if stage == 2:
        dbg = b.outp("dbg", [128, T], BF16)

        @blk.sync
        def _(e):
            b.wait(e, snm, NHG)
            b.inc(e.dma_start(out=dbg, in_=AVH[:]), sst, 16)
            b.wait(e, sst)
        return b.finish()

    if stage == 3:
        dbg = b.outp("dbg", [128, NT * D], F32)

        @blk.sync
        def _(e):
            b.wait(e, swo)
            b.inc(e.dma_start(out=dbg, in_=O[:]), sst, 16)
            b.wait(e, sst)
        return b.finish()

    nwo_done = b.val("swo")

    def get_o(i):
        return O[:, i * D:(i + 1) * D]

    def post(v, i, st):
        tmp = TMP[:, (i % 2) * D:(i % 2 + 1) * D]
        if i >= 2:
            v.wait_ge(b.h(sst), 16 * (i - 1))
        ln_apply(v, get_o(i), tmp, st)
        v.wait_ge(b.h(sld), SLD0 + 16 * (i + 1))
        return v.tensor_add(tmp, tmp, XR[:, (i % 4) * D:(i % 4 + 1) * D])

    ln_emit(b, blk, "lnO", NT, D, get_o,
            lambda i: [(swo, nwo_done)], sfin, post=post)

    xot = xout.rearrange("(n p) d -> n p d", p=128)
    xit = xown.rearrange("(n p) d -> n p d", p=128)

    @blk.sync
    def _(e):
        for i in range(NT):
            if i >= 4:
                b.wait(e, sfin, i - 3)
            b.inc(e.dma_start(out=XR[:, (i % 4) * D:(i % 4 + 1) * D], in_=xit[i]),
                  sld, 16)
            if i >= 4:
                b.inc(e.dma_start(out=xot[i - 4],
                                  in_=TMP[:, ((i - 4) % 2) * D:((i - 4) % 2 + 1) * D]),
                      sst, 16)
        for i in range(NT - 4, NT):
            b.wait(e, sfin, i + 1)
            b.inc(e.dma_start(out=xot[i],
                              in_=TMP[:, (i % 2) * D:(i % 2 + 1) * D]), sst, 16)
        b.wait(e, sst)

    return b.finish()


def emit_bcast(b, blk, tag, pairs, psum, wait_list=()):
    sm = b.sem(tag + "_m")
    sc = b.sem(tag + "_c")

    @blk.tensor
    def _(t):
        for w, v in wait_list:
            b.wait(t, w, v)
        for i, (row, dst, wd) in enumerate(pairs):
            if i >= 1:
                t.wait_ge(b.h(sc), i)
            t.matmul(psum[:, 0:wd], b.ones_row[0:1, 0:128], row,
                     start=True, stop=True).then_inc(b.h(sm), 1)
    b.bump(sm, len(pairs))

    @blk.vector
    def _(v):
        for i, (row, dst, wd) in enumerate(pairs):
            v.wait_ge(b.h(sm), i + 1)
            v.tensor_copy(dst, psum[:, 0:wd]).then_inc(b.h(sc), 1)
    b.bump(sc, len(pairs))
    return sc


def mk_preln():
    b = B()
    xin = b.inp("xin", [T, D])
    uout = b.outp("uout", [T, D])
    XR = b.sb("XR", [128, 4 * D])
    TMP = b.sb("TMP", [128, 2 * D])
    blk = b.begin()
    sld = b.sem("sld")
    sfin = b.sem("sfin")
    sst = b.sem("sst")
    b.sem("pl_a")

    def get_x(i):
        return XR[:, (i % 4) * D:(i % 4 + 1) * D]

    def post(v, i, st):
        tmp = TMP[:, (i % 2) * D:(i % 2 + 1) * D]
        if i >= 2:
            v.wait_ge(b.h(sst), 16 * (i - 1))
        return ln_apply(v, get_x(i), tmp, st)

    ln_emit(b, blk, "lnP", NT, D, get_x,
            lambda i: [(sld, 16 * (i + 1))], sfin, post=post)

    xit = xin.rearrange("(n p) d -> n p d", p=128)
    uot = uout.rearrange("(n p) d -> n p d", p=128)

    @blk.sync
    def _(e):
        for i in range(NT):
            if i >= 4:
                b.wait(e, sfin, i - 3)
            b.inc(e.dma_start(out=XR[:, (i % 4) * D:(i % 4 + 1) * D], in_=xit[i]),
                  sld, 16)
            if i >= 4:
                b.inc(e.dma_start(out=uot[i - 4],
                                  in_=TMP[:, ((i - 4) % 2) * D:((i - 4) % 2 + 1) * D]),
                      sst, 16)
        for i in range(NT - 4, NT):
            b.wait(e, sfin, i + 1)
            b.inc(e.dma_start(out=uot[i], in_=TMP[:, (i % 2) * D:(i % 2 + 1) * D]),
                  sst, 16)
        b.wait(e, sst)
    return b.finish()


def rot_perm_np(hd=64):
    P = np.zeros((128, 128), np.float32)
    for d in range(128):
        r = d % hd
        if r < hd // 2:
            P[d + hd // 2, d] = -1.0
        else:
            P[d - hd // 2, d] = 1.0
    return P.astype(BF16NP)


def rope_tables_np(core, hd=64, theta=10000.0):
    qc = core % 4
    pos = np.arange(qc * T, (qc + 1) * T, dtype=np.float32)
    inv = 1.0 / theta ** (np.arange(0, hd, 2, dtype=np.float32) / hd)
    ang = pos[:, None] * inv[None, :]
    cosh = np.concatenate([np.cos(ang), np.cos(ang)], -1)
    sinh = np.concatenate([np.sin(ang), np.sin(ang)], -1)
    cos = np.tile(cosh.T, (2, 1)).astype(np.float32)
    sin = np.tile(sinh.T, (2, 1)).astype(np.float32)
    return np.ascontiguousarray(cos), np.ascontiguousarray(sin)


def mk_f0qkv():
    b = B()
    uin = b.inp("uin", [T, D])
    ln1 = b.inp("ln1", [D])
    wq = b.inp("wq", [D, D])
    wk = b.inp("wk", [D, 512])
    wv = b.inp("wv", [D, 512])
    prot = b.inp("prot", [128, 128], BF16)
    rcos = b.inp("rcos", [128, T])
    rsin = b.inp("rsin", [128, T])
    oqf = b.outp("qf", [8, 128, T], BF16)
    okf = b.outp("kf", [4, 128, T], BF16)
    ovt = b.outp("vt", [T, 520], BF16)

    UR = b.sb("UR", [128, 4 * D])
    HF = b.sb("HF", [128, 8 * T], BF16)
    QF = b.sb("QF", [128, 8 * T], BF16)
    KF = b.sb("KF", [128, 4 * T], BF16)
    VT = b.sb("VT", [128, NT * 520], BF16)
    WQN = b.sb("WQN", [128, 8 * D], BF16)
    WKN = b.sb("WKN", [128, 8 * 512], BF16)
    WVN = b.sb("WVN", [128, 8 * 512], BF16)
    G1B = b.sb("G1B", [128, D], F32)
    G1R = b.sb("G1R", [1, D], BF16)
    RC = b.sb("RC", [128, T], F32)
    RS = b.sb("RS", [128, T], F32)
    PROT = b.sb("PROT", [128, 128], BF16)
    TR1 = b.sb("TR1", [128, 512], F32)
    TR2 = b.sb("TR2", [128, 512], F32)

    PS = [b.ps(f"ps{i}", [128, 512], F32) for i in range(3)]
    PSB = b.ps("psb", [128, 512], F32)
    PR = [b.ps(f"pr{i}", [128, 512], F32) for i in range(2)]

    blk = b.begin()
    sini = b.sem("sini")
    sld = b.sem("sld")
    sw = b.sem("sw")
    sproj = b.sem("sproj")
    srot = b.sem("srot")
    srotd = b.sem("srotd")
    sst = b.sem("sst")
    b.sem("lnF_a")

    ident_setup(b, blk, sini)
    uit = uin.rearrange("(n p) d -> n p d", p=128)

    @blk.sync
    def _(e):
        b.inc(e.dma_start(out=RC[:], in_=rcos[:, :]), sld, 16)
        b.inc(e.dma_start(out=RS[:], in_=rsin[:, :]), sld, 16)
        b.inc(e.dma_start(out=PROT[:], in_=prot[:, :]), sld, 16)
        for i in range(NT):
            if i >= 4:
                e.wait_ge(b.h("lnF_a"), i - 3)
            b.inc(e.dma_start(out=UR[:, (i % 4) * D:(i % 4 + 1) * D], in_=uit[i]),
                  sld, 16)

    @blk.gpsimd
    def _(g):
        b.inc(g.dma_start(out=WQN[:].rearrange("p (c o) -> p c o", o=D),
                          in_=wq.rearrange("(c p) o -> p c o", p=128)), sw, 16)
        b.inc(g.dma_start(out=WKN[:].rearrange("p (c o) -> p c o", o=512),
                          in_=wk.rearrange("(c p) o -> p c o", p=128)), sw, 16)
        b.inc(g.dma_start(out=WVN[:].rearrange("p (c o) -> p c o", o=512),
                          in_=wv.rearrange("(c p) o -> p c o", p=128)), sw, 16)
        b.inc(g.dma_start(out=G1R[0:1, :], in_=ln1[None, :]), sw, 16)
        g.memset(VT[:].rearrange("p (n h c) -> p n h c", h=8, c=65)[:, :, :, 64:65], 1.0)
        b.inc(g.engine_nop(), sini)

    sgb = emit_bcast(b, blk, "g1", [(G1R[0:1, 0:512], G1B[:, 0:512], 512)], PSB,
                     wait_list=[(sw, None), (sini, None)])
    sgb2 = emit_bcast(b, blk, "g2", [(G1R[0:1, 512:1024], G1B[:, 512:1024], 512)],
                      PSB, wait_list=[(sgb, 1), (sini, None)])

    lns = ln_fm_emit(b, blk, "lnF", NT, D,
                     lambda i: UR[:, (i % 4) * D:(i % 4 + 1) * D],
                     lambda i: [(sld, 48 + 16 * (i + 1)), (sgb2, 1)],
                     lambda i, c: HF[:, c * T + i * 128:c * T + (i + 1) * 128],
                     b.sem("shf"), rms=True, eps=1e-6, gamma_b=G1B[:])
    shf = "shf"

    groups = []
    for co in range(8):
        for g4 in range(4):
            mms = [(WQN[:, ci * D + co * 128:ci * D + (co + 1) * 128],
                    HF[:, ci * T + g4 * 512:ci * T + (g4 + 1) * 512]) for ci in range(8)]

            def ev(e, pa, co=co, g4=g4):
                return e.activation(out=QF[:, co * T + g4 * 512:co * T + (g4 + 1) * 512],
                                    in_=pa, func=AF.Copy)
            groups.append(dict(mms=mms, pslice=lambda p: p[:, 0:512],
                               evac_eng="scalar", evac=ev))
    for co in range(4):
        for g4 in range(4):
            mms = [(WKN[:, ci * 512 + co * 128:ci * 512 + (co + 1) * 128],
                    HF[:, ci * T + g4 * 512:ci * T + (g4 + 1) * 512]) for ci in range(8)]

            def ev(e, pa, co=co, g4=g4):
                return e.activation(out=KF[:, co * T + g4 * 512:co * T + (g4 + 1) * 512],
                                    in_=pa, func=AF.Copy)
            groups.append(dict(mms=mms, pslice=lambda p: p[:, 0:512],
                               evac_eng="scalar", evac=ev))
    for tt in range(NT):
        mms = [(HF[:, ci * T + tt * 128:ci * T + (tt + 1) * 128],
                WVN[:, ci * 512:(ci + 1) * 512]) for ci in range(8)]

        def ev(e, pa, tt=tt):
            dst = VT[:, tt * 520:(tt + 1) * 520].rearrange(
                "p (h c) -> p h c", c=65)[:, :, 0:64]
            return e.tensor_copy(dst, pa.rearrange("p (h c) -> p h c", c=64))
        groups.append(dict(mms=mms, pslice=lambda p: p[:, 0:512],
                           evac_eng="vector", evac=ev))

    gemm(b, blk, "proj", groups, PS, wait_list=[(sw, 64), (shf, None)], done=sproj)

    NPIECES = (8 + 4) * 4

    @blk.tensor
    def _(t):
        b.wait(t, sproj)
        b.wait(t, sld, 48)
        for i in range(NPIECES):
            if i >= 2:
                t.wait_ge(b.h(srotd), i - 1)
            c, g4 = divmod(i, 4)
            buf = QF if c < 8 else KF
            cc = c if c < 8 else c - 8
            t.matmul(PR[i % 2][:, 0:512], PROT[:],
                     buf[:, cc * T + g4 * 512:cc * T + (g4 + 1) * 512],
                     start=True, stop=True).then_inc(b.h(srot), 1)
    b.bump(srot, NPIECES)

    @blk.vector
    def _(v):
        for i in range(NPIECES):
            c, g4 = divmod(i, 4)
            buf = QF if c < 8 else KF
            cc = c if c < 8 else c - 8
            sl = buf[:, cc * T + g4 * 512:cc * T + (g4 + 1) * 512]
            v.wait_ge(b.h(srot), i + 1)
            v.tensor_mul(TR1[:], PR[i % 2][:, 0:512], RS[:, g4 * 512:(g4 + 1) * 512])
            v.tensor_mul(TR2[:], sl, RC[:, g4 * 512:(g4 + 1) * 512])
            v.tensor_add(sl, TR1[:], TR2[:]).then_inc(b.h(srotd), 1)
    b.bump(srotd, NPIECES)

    @blk.sync
    def _(e):
        b.wait(e, srotd)
        b.wait(e, sproj)
        b.wait(e, sini)
        b.inc(e.dma_start(out=oqf.rearrange("h p t -> p h t"),
                          in_=QF[:].rearrange("p (h t) -> p h t", t=T)), sst, 16)
        b.inc(e.dma_start(out=okf.rearrange("h p t -> p h t"),
                          in_=KF[:].rearrange("p (h t) -> p h t", t=T)), sst, 16)
        b.inc(e.dma_start(out=ovt.rearrange("(n p) c -> p n c", p=128),
                          in_=VT[:].rearrange("p (n c) -> p n c", c=520)), sst, 16)
        b.wait(e, sst)
    return b.finish()


def mk_gattn(nheads, kv_per_head, causal, hd=64, nglob=64, wo_shape=(1024, 1024),
             has_bias=False, gbounds=None):
    S = 8192
    nkv = nheads // kv_per_head
    NGRc = 4
    ng_list = list(gbounds) if gbounds is not None else [nglob] * NGRc
    dtl = 4 if causal else 0
    cnt = [n + dtl for n in ng_list]
    TOT_H = sum(cnt)
    goff = [0]
    for n in ng_list:
        goff.append(goff[-1] + n)
    b = B()
    qh = b.inp("qh", [nheads, hd, T], BF16)
    kglob = b.inp("kglob", [nkv, hd, S], BF16)
    vglob = b.inp("vglob", [nkv, S, hd + 1], BF16)
    uin = b.inp("uin", [T, D])
    wo = b.inp("wo", [wo_shape[0], wo_shape[1]])
    flags = b.inp("flags", [128, goff[-1]])
    obias = b.inp("obias", [D]) if has_bias else None
    if causal:
        dmask = b.inp("dmask", [4, 128, 512], BF16)
        kd = b.inp("kd", [nkv, hd, T], BF16)
        vd = b.inp("vd", [nkv, T, hd + 1], BF16)
    u1out = b.outp("u1", [T, D])

    IN = nheads * hd
    nic = IN // 128
    hper = 128 // hd

    QH = b.sb("QH", [hd, 2 * T], BF16)
    KH = b.sb("KH", [hd, 2 * S], BF16)
    VH = b.sb("VH", [128, 2 * 64 * (hd + 1)], BF16)
    KDH = b.sb("KDH", [hd, 2 * T], BF16) if causal else None
    VDH = b.sb("VDH", [128, 2 * NT * (hd + 1)], BF16) if causal else None
    MSK = b.sb("MSK", [128, 4 * 512], BF16) if causal else None
    FLG = b.sb("FLG", [128, goff[-1]], F32)
    ET = b.sb("ET", [128, 4 * 512], BF16)
    AVF = b.sb("AVF", [128, nic * T], BF16)
    DNR = b.sb("DNR", [1, 2 * 512], F32)
    RRB = b.sb("RRB", [1, 512], BF16)
    WON = b.sb("WON", [128, nic * D], BF16)
    BOB = b.sb("BOB", [128, D], F32) if has_bias else None
    BOR = b.sb("BOR", [1, D], BF16) if has_bias else None
    UR = b.sb("UR", [128, 4 * D])
    TMP = b.sb("TMP", [128, 2 * D])

    PJ = [b.ps(f"pj{i}", [128, 512], F32) for i in range(3)]
    PSA = b.ps("psa", [hd + 1, 512], F32)
    PSR = b.ps("psr", [128, 512], F32)
    PO = [b.ps(f"po{i}", [128, 512], F32) for i in range(2)]
    PSB = b.ps("psb", [128, 512], F32)

    blk = b.begin()
    sini = b.sem("sini")
    sld = b.sem("sld")
    skv = b.sem("skv")
    sq = b.sem("sq")
    sw = b.sem("sw")
    ssc = b.sem("ssc")
    sex = b.sem("sex")
    smk = b.sem("smk")
    savr = b.sem("savr")
    seva = b.sem("seva")
    sdn = b.sem("sdn")
    sflu = b.sem("sfl2")
    srr = b.sem("srr")
    sbr = b.sem("sbr")
    snm = b.sem("snm")
    sbias = b.sem("sbias")
    swo = b.sem("swo")
    sfin = b.sem("sfin")
    sst = b.sem("sst")

    ident_setup(b, blk, sini)
    NGR = 4
    cum = [0]
    for h in range(nheads):
        for g in range(NGR):
            cum.append(cum[-1] + cnt[g])

    @blk.sync
    def _(e):
        b.inc(e.dma_start(out=FLG[:], in_=flags[:, :]), sld, 16)
        if causal:
            b.inc(e.dma_start(out=MSK[:].rearrange("p (n d) -> p n d", d=512),
                              in_=dmask.rearrange("n p d -> p n d")), sld, 16)
        for h in range(nheads):
            kh = h // kv_per_head
            if h % kv_per_head == 0:
                if kh >= 2:
                    e.wait_ge(b.h(savr), (h - kv_per_head) * TOT_H)
                b.inc(e.dma_start(out=KH[:, (kh % 2) * S:(kh % 2 + 1) * S],
                                  in_=kglob[kh]), skv, 16)
                b.inc(e.dma_start(
                    out=VH[:, (kh % 2) * 64 * (hd + 1):(kh % 2 + 1) * 64 * (hd + 1)]
                        .rearrange("p (n c) -> p n c", c=hd + 1),
                    in_=vglob[kh].rearrange("(n p) c -> p n c", p=128)), skv, 16)
                if causal:
                    b.inc(e.dma_start(out=KDH[:, (kh % 2) * T:(kh % 2 + 1) * T],
                                      in_=kd[kh]), skv, 16)
                    b.inc(e.dma_start(
                        out=VDH[:, (kh % 2) * NT * (hd + 1):(kh % 2 + 1) * NT * (hd + 1)]
                            .rearrange("p (n c) -> p n c", c=hd + 1),
                        in_=vd[kh].rearrange("(n p) c -> p n c", p=128)), skv, 16)
            if h >= 2:
                e.wait_ge(b.h(savr), (h - 1) * TOT_H)
            b.inc(e.dma_start(out=QH[:, (h % 2) * T:(h % 2 + 1) * T], in_=qh[h]),
                  sq, 16)

    @blk.gpsimd
    def _(g):
        b.inc(g.dma_start(out=WON[:].rearrange("p (c o) -> p c o", o=D),
                          in_=wo.rearrange("(c p) o -> p c o", p=128)), sw, 16)
        if has_bias:
            b.inc(g.dma_start(out=BOR[0:1, :], in_=obias[None, :]), sw, 16)

    if has_bias:
        sbb = emit_bcast(b, blk, "bo", [(BOR[0:1, 0:512], BOB[:, 0:512], 512),
                                        (BOR[0:1, 512:1024], BOB[:, 512:1024], 512)],
                         PSB, wait_list=[(sw, None), (sini, None)])

    nkvld = 2 if not causal else 4

    def av_r(t, h, g, kh, r):
        hg = h * NGR + g
        glob = cum[hg] + r
        ng = ng_list[g]
        ks = kh % 2
        if causal and r >= ng:
            t.wait_ge(b.h(smk), hg * 4 + (r - ng) + 1)
        else:
            t.wait_ge(b.h(sex), glob + 1)
        if r == 0 and hg >= 1:
            t.wait_ge(b.h(seva), hg)
            t.wait_ge(b.h(sdn), hg)
        if r < ng:
            lh = VH[:, ks * 64 * (hd + 1) + r * (hd + 1):
                    ks * 64 * (hd + 1) + (r + 1) * (hd + 1)]
        else:
            jr = r - ng
            lh = VDH[:, ks * NT * (hd + 1) + (4 * g + jr) * (hd + 1):
                     ks * NT * (hd + 1) + (4 * g + jr + 1) * (hd + 1)]
        ins = t.matmul(PSA[:, 0:512], lh, ET[:, (glob % 4) * 512:(glob % 4 + 1) * 512],
                       start=(r == 0), stop=(r == cnt[g] - 1), skip_group_check=True)
        ins.then_inc(b.h(savr), 1)

    @blk.tensor
    def _(t):
        b.wait(t, sld, b.val(sld))
        b.wait(t, sini)
        for h in range(nheads):
            kh = h // kv_per_head
            t.wait_ge(b.h(skv), (kh + 1) * nkvld * 16)
            t.wait_ge(b.h(sq), (h + 1) * 16)
            ks = (kh % 2)
            for g in range(NGR):
                hg = h * NGR + g
                ng = ng_list[g]
                for r in range(cnt[g]):
                    glob = cum[hg] + r
                    if glob >= 3:
                        t.wait_ge(b.h(sex), glob - 2)
                    if r < ng:
                        lh = KH[:, ks * S + r * 128:ks * S + (r + 1) * 128]
                    else:
                        jr = r - ng
                        lh = KDH[:, ks * T + (4 * g + jr) * 128:
                                 ks * T + (4 * g + jr + 1) * 128]
                    t.matmul(PJ[glob % 3][:, 0:512], lh,
                             QH[:, (h % 2) * T + g * 512:(h % 2) * T + (g + 1) * 512],
                             start=True, stop=True).then_inc(b.h(ssc), 1)
                    if r >= 2:
                        av_r(t, h, g, kh, r - 2)
                av_r(t, h, g, kh, cnt[g] - 2)
                av_r(t, h, g, kh, cnt[g] - 1)
                t.wait_ge(b.h(srr), hg + 1)
                if hg >= 1:
                    t.wait_ge(b.h(snm), hg)
                t.matmul(PSR[:, 0:512], b.ones_row[0:1, 0:128], RRB[0:1, 0:512],
                         start=True, stop=True).then_inc(b.h(sbr), 1)
        b.bump(ssc, nheads * TOT_H)
        b.bump(savr, nheads * TOT_H)
        b.bump(sbr, nheads * NGR)

    inv = 1.0 / math.sqrt(float(hd))

    @blk.scalar
    def _(s):
        for h in range(nheads):
            for g in range(NGR):
                hg = h * NGR + g
                ng = ng_list[g]
                for r in range(cnt[g]):
                    glob = cum[hg] + r
                    s.wait_ge(b.h(ssc), glob + 1)
                    if glob >= 4:
                        s.wait_ge(b.h(savr), glob - 3)
                    kwargs = {}
                    if causal and r < ng:
                        kwargs["bias"] = FLG[:, goff[g] + r:goff[g] + r + 1]
                    s.activation(out=ET[:, (glob % 4) * 512:(glob % 4 + 1) * 512],
                                 in_=PJ[glob % 3][:, 0:512], func=AF.Exp,
                                 scale=inv, **kwargs).then_inc(b.h(sex), 1)
                s.wait_ge(b.h(savr), cum[hg + 1])
                co = h // hper
                ro = (h % hper) * hd
                s.activation(out=AVF[ro:ro + hd, co * T + g * 512:co * T + (g + 1) * 512],
                             in_=PSA[0:hd, 0:512], func=AF.Copy).then_inc(b.h(seva), 1)
                s.wait_ge(b.h(sdn), hg + 1)
                if hg >= 1:
                    s.wait_ge(b.h(sbr), hg)
                s.activation(out=DNR[0:1, 512:1024], in_=DNR[0:1, 0:512],
                             func=AF.Ln).then_inc(b.h(sflu), 1)
                s.wait_ge(b.h(sflu), hg + 1)
                s.activation(out=RRB[0:1, 0:512], in_=DNR[0:1, 512:1024], func=AF.Exp,
                             scale=-1.0).then_inc(b.h(srr), 1)
        b.bump(sex, nheads * TOT_H)
        b.bump(seva, nheads * NGR)
        b.bump(sflu, nheads * NGR)
        b.bump(srr, nheads * NGR)

    @blk.vector
    def _(v):
        for h in range(nheads):
            co = h // hper
            ro = (h % hper) * hd
            for g in range(NGR):
                hg = h * NGR + g
                if causal:
                    for jr in range(4):
                        glob = cum[hg] + ng_list[g] + jr
                        v.wait_ge(b.h(sex), glob + 1)
                        v.tensor_mul(ET[:, (glob % 4) * 512:(glob % 4 + 1) * 512],
                                     ET[:, (glob % 4) * 512:(glob % 4 + 1) * 512],
                                     MSK[:, jr * 512:(jr + 1) * 512]).then_inc(b.h(smk), 1)
                v.wait_ge(b.h(savr), cum[hg + 1])
                if hg >= 1:
                    v.wait_ge(b.h(srr), hg)
                v.tensor_copy(DNR[0:1, 0:512], PSA[hd:hd + 1, 0:512]).then_inc(b.h(sdn), 1)
                v.wait_ge(b.h(sbr), hg + 1)
                v.wait_ge(b.h(seva), hg + 1)
                v.tensor_mul(AVF[ro:ro + hd, co * T + g * 512:co * T + (g + 1) * 512],
                             AVF[ro:ro + hd, co * T + g * 512:co * T + (g + 1) * 512],
                             PSR[ro:ro + hd, 0:512]).then_inc(b.h(snm), 1)
        if causal:
            b.bump(smk, nheads * NGR * 4)
        b.bump(sdn, nheads * NGR)
        b.bump(snm, nheads * NGR)

    groups = []
    for tt in range(NT):
        for og in range(2):
            mms = [(AVF[:, ci * T + tt * 128:ci * T + (tt + 1) * 128],
                    WON[:, ci * D + og * 512:ci * D + (og + 1) * 512])
                   for ci in range(nic)]

            def ev(e, pa, tt=tt, og=og):
                e.wait_ge(b.h(sld), b.val(sld) + 16 * (tt + 1))
                dst = TMP[:, (tt % 2) * D + og * 512:(tt % 2) * D + (og + 1) * 512]
                if tt >= 2:
                    e.wait_ge(b.h(sst), 16 * (tt - 1))
                if has_bias:
                    e.tensor_add(dst, pa, BOB[:, og * 512:(og + 1) * 512])
                    return e.tensor_add(dst, dst,
                                        UR[:, (tt % 4) * D + og * 512:(tt % 4) * D + (og + 1) * 512])
                return e.tensor_add(dst, pa,
                                    UR[:, (tt % 4) * D + og * 512:(tt % 4) * D + (og + 1) * 512])
            groups.append(dict(mms=mms, pslice=lambda p: p[:, 0:512],
                               evac_eng="vector", evac=ev))
    wl = [(snm, None), (sw, None)]
    if has_bias:
        wl.append((sbb, 2))
    gemm(b, blk, "wog", groups, PO, wait_list=wl, done=sfin)

    uit = uin.rearrange("(n p) d -> n p d", p=128)
    uot = u1out.rearrange("(n p) d -> n p d", p=128)

    @blk.sync
    def _(e):
        base = b.val(sld)
        for i in range(NT):
            if i >= 4:
                e.wait_ge(b.h(sfin), 2 * (i - 3))
            b.inc(e.dma_start(out=UR[:, (i % 4) * D:(i % 4 + 1) * D], in_=uit[i]),
                  sld, 16)
            if i >= 2:
                e.wait_ge(b.h(sfin), 2 * (i - 1))
                b.inc(e.dma_start(out=uot[i - 2],
                                  in_=TMP[:, ((i - 2) % 2) * D:((i - 2) % 2 + 1) * D]),
                      sst, 16)
        for i in range(NT - 2, NT):
            e.wait_ge(b.h(sfin), 2 * (i + 1))
            b.inc(e.dma_start(out=uot[i], in_=TMP[:, (i % 2) * D:(i % 2 + 1) * D]),
                  sst, 16)
        b.wait(e, sst)
    return b.finish()


def emit_top2(b, blk, tag, LG, W8, ntiles, E, wait_list=()):
    SC = b.sb(tag + "_sc", [128, 8 * ntiles], F32)
    TM = b.sb(tag + "_tm", [128, 2 * E], F32)
    sv = b.sem(tag + "_v")
    ssg = b.sem(tag + "_g")
    sdone = b.sem(tag + "_d")

    @blk.vector
    def _(v):
        nv = 0
        for i in range(ntiles):
            for w, val in wait_list:
                b.wait(v, w, val)
            lg = LG[:, i * E:(i + 1) * E]
            st = SC[:, i * 8:(i + 1) * 8]
            m1, m2, d12, w1, w2 = (st[:, 0:1], st[:, 1:2], st[:, 2:3],
                                   st[:, 3:4], st[:, 4:5])
            mask1 = TM[:, 0:E]
            l2 = TM[:, E:2 * E]
            v.reduce_max(m1, lg, axis=AX.X).then_inc(b.h(sv), 1); nv += 1
            v.wait_ge(b.h(sv), nv)
            v.tensor_scalar(mask1, lg, m1, None, ALU.is_equal).then_inc(b.h(sv), 1); nv += 1
            v.wait_ge(b.h(sv), nv)
            v.scalar_tensor_tensor(l2, mask1, -1e30, lg, ALU.mult, ALU.add).then_inc(b.h(sv), 1); nv += 1
            v.wait_ge(b.h(sv), nv)
            v.reduce_max(m2, l2, axis=AX.X).then_inc(b.h(sv), 1); nv += 1
            v.wait_ge(b.h(sv), nv)
            v.tensor_scalar(l2, l2, m2, None, ALU.is_equal)
            v.tensor_sub(d12, m1, m2).then_inc(b.h(sv), 1); nv += 1
            v.wait_ge(b.h(sv), nv)
            v.wait_ge(b.h(ssg), i + 1)
            v.tensor_scalar(w2, w1, -1.0, 1.0, ALU.mult, ALU.add).then_inc(b.h(sv), 1); nv += 1
            v.wait_ge(b.h(sv), nv)
            v.tensor_scalar(mask1, mask1, w1, None, ALU.mult).then_inc(b.h(sv), 1); nv += 1
            v.wait_ge(b.h(sv), nv)
            v.scalar_tensor_tensor(W8[:, i * E:(i + 1) * E], l2, w2, mask1,
                                   ALU.mult, ALU.add).then_inc(b.h(sdone), 1)
        b.bump(sv, 7 * ntiles)
        b.bump(sdone, ntiles)

    @blk.scalar
    def _(s):
        for i in range(ntiles):
            st = SC[:, i * 8:(i + 1) * 8]
            s.wait_ge(b.h(sv), 5 * 1 + 7 * i)
            s.activation(out=st[:, 3:4], in_=st[:, 2:3],
                         func=AF.Sigmoid).then_inc(b.h(ssg), 1)
        b.bump(ssg, ntiles)
    return sdone


def mk_f0moe():
    b = B()
    u1 = b.inp("u1", [T, D])
    x1 = b.inp("x1", [T, D])
    ln2 = b.inp("ln2", [D])
    router = b.inp("router", [D, 8])
    wg = b.inp("wg", [8, D, 3584])
    wu = b.inp("wu", [8, D, 3584])
    wd = b.inp("wd", [8, 3584, D])
    x2out = b.outp("x2", [T, D])

    MOE = b.sb("MOE", [128, NT * D])
    H2F = b.sb("H2F", [128, 8 * T], BF16)
    ACT = b.sb("ACT", [128, 4 * T], BF16)
    WGC = b.sb("WGC", [128, 2 * 8 * 512], BF16)
    WUC = b.sb("WUC", [128, 2 * 8 * 512], BF16)
    WDC = b.sb("WDC", [128, 2 * 4 * D], BF16)
    G2B = b.sb("G2B", [128, D], BF16)
    G2R = b.sb("G2R", [1, D], BF16)
    RWN = b.sb("RWN", [128, 8 * 8], BF16)
    LG = b.sb("LG", [128, NT * 8], F32)
    W8 = b.sb("W8", [128, NT * 8], F32)
    SG = b.sb("SG", [128, 2 * 512], BF16)
    UP = b.sb("UP", [128, 2 * 512], BF16)
    XR = b.sb("XR", [128, 2 * D])
    TMP = b.sb("TMP", [128, 2 * D])

    PS = [b.ps(f"ps{i}", [128, 512], F32) for i in range(2)]
    PD = [b.ps(f"pd{i}", [128, 512], F32) for i in range(2)]
    PSB = b.ps("psb", [128, 512], F32)
    PRT = b.ps("prt", [128, 128], F32)

    blk = b.begin()
    sini = b.sem("sini")
    sld = b.sem("sld")
    sw = b.sem("sw")
    swg = b.sem("swg")
    shf = b.sem("shf")
    srt = b.sem("srt")
    srtc = b.sem("srtc")
    sgm = b.sem("sgm")
    sge = b.sem("sge")
    sac = b.sem("sac")
    sdm = b.sem("sdm")
    smo = b.sem("smo")
    sfin = b.sem("sfin")
    sst = b.sem("sst")

    ident_setup(b, blk, sini, idents=False)
    u1t = u1.rearrange("(n p) d -> n p d", p=128)

    @blk.sync
    def _(e):
        b.inc(e.dma_start(out=MOE[:].rearrange("p (n d) -> p n d", d=D),
                          in_=u1.rearrange("(n p) d -> p n d", p=128)), sld, 16)

    @blk.gpsimd
    def _(g):
        b.inc(g.dma_start(out=RWN[:].rearrange("p (c o) -> p c o", o=8),
                          in_=router.rearrange("(c p) o -> p c o", p=128)), sw, 16)
        b.inc(g.dma_start(out=G2R[0:1, :], in_=ln2[None, :]), sw, 16)
        for em in range(56):
            e_, mc = divmod(em, 7)
            sl = em % 2
            if em >= 2:
                g.wait_ge(b.h(sgm), 32 * (em - 1))
            b.inc(g.dma_start(
                out=WGC[:, sl * 8 * 512:(sl + 1) * 8 * 512]
                    .rearrange("p (c m) -> p c m", m=512),
                in_=wg[e_].rearrange("(c p) m -> p c m", p=128)[:, :, mc * 512:(mc + 1) * 512]),
                swg, 16)
            b.inc(g.dma_start(
                out=WUC[:, sl * 8 * 512:(sl + 1) * 8 * 512]
                    .rearrange("p (c m) -> p c m", m=512),
                in_=wu[e_].rearrange("(c p) m -> p c m", p=128)[:, :, mc * 512:(mc + 1) * 512]),
                swg, 16)
            if em >= 2:
                g.wait_ge(b.h(sdm), 32 * (em - 1))
            b.inc(g.dma_start(
                out=WDC[:, (em % 2) * 4 * D:(em % 2 + 1) * 4 * D]
                    .rearrange("p (c o) -> p c o", o=D),
                in_=wd[e_].rearrange("(c p) o -> p c o", p=128)[:, 4 * mc:4 * (mc + 1), :]),
                swg, 16)

    sgb = emit_bcast(b, blk, "g2a", [(G2R[0:1, 0:512], G2B[:, 0:512], 512)], PSB,
                     wait_list=[(sw, 32), (sini, None)])
    sgb2 = emit_bcast(b, blk, "g2b", [(G2R[0:1, 512:1024], G2B[:, 512:1024], 512)],
                      PSB, wait_list=[(sgb, 1)])

    ln_fm_emit(b, blk, "lnM", NT, D,
               lambda i: MOE[:, i * D:(i + 1) * D],
               lambda i: [(sld, 16), (sgb2, 1)],
               lambda i, c: H2F[:, c * T + i * 128:c * T + (i + 1) * 128],
               shf, rms=True, eps=1e-6, gamma_b=G2B[:])

    @blk.tensor
    def _(t):
        b.wait(t, shf)
        b.wait(t, sw, 32)
        for tt in range(NT):
            if tt >= 1:
                t.wait_ge(b.h(srtc), tt)
            for ci in range(8):
                ins = t.matmul(PRT[:, 0:8],
                               H2F[:, ci * T + tt * 128:ci * T + (tt + 1) * 128],
                               RWN[:, ci * 8:(ci + 1) * 8],
                               start=(ci == 0), stop=(ci == 7))
            ins.then_inc(b.h(srt), 1)
    b.bump(srt, NT)

    @blk.vector
    def _(v):
        for tt in range(NT):
            v.wait_ge(b.h(srt), tt + 1)
            v.tensor_copy(LG[:, tt * 8:(tt + 1) * 8], PRT[:, 0:8]).then_inc(b.h(srtc), 1)
    b.bump(srtc, NT)

    sw8 = emit_top2(b, blk, "t2", LG, W8, NT, 8,
                    wait_list=[(srtc, None)])

    @blk.tensor
    def _(t):
        b.wait(t, shf)
        gg = 0
        dd = 0
        for em in range(56):
            e_, mc = divmod(em, 7)
            sl = em % 2
            t.wait_ge(b.h(swg), 48 * em + 32)
            for msub in range(4):
                for tg in range(4):
                    for kind in range(2):
                        if gg >= 2:
                            t.wait_ge(b.h(sge), gg - 1)
                        W = WGC if kind == 0 else WUC
                        for ci in range(8):
                            ins = t.matmul(
                                PS[gg % 2][:, 0:512],
                                W[:, sl * 8 * 512 + ci * 512 + msub * 128:
                                  sl * 8 * 512 + ci * 512 + (msub + 1) * 128],
                                H2F[:, ci * T + tg * 512:ci * T + (tg + 1) * 512],
                                start=(ci == 0), stop=(ci == 7))
                        ins.then_inc(b.h(sgm), 1)
                        gg += 1
            t.wait_ge(b.h(swg), 48 * (em + 1))
            t.wait_ge(b.h(sac), 16 * (em + 1))
            for tt in range(NT):
                for dg in range(2):
                    if dd >= 2:
                        t.wait_ge(b.h(smo), dd - 1)
                    for msub in range(4):
                        ins = t.matmul(
                            PD[dd % 2][:, 0:512],
                            ACT[:, msub * T + tt * 128:(tt + 1) * 128 + msub * T],
                            WDC[:, (em % 2) * 4 * D + msub * D + dg * 512:
                                (em % 2) * 4 * D + msub * D + (dg + 1) * 512],
                            start=(msub == 0), stop=(msub == 3))
                    ins.then_inc(b.h(sdm), 1)
                    dd += 1
    b.bump(sgm, 56 * 32)
    b.bump(sdm, 56 * 32)

    @blk.scalar
    def _(s):
        gg = 0
        for em in range(56):
            for msub in range(4):
                for tg in range(4):
                    for kind in range(2):
                        s.wait_ge(b.h(sgm), gg + 1)
                        if gg // 2 >= 2:
                            s.wait_ge(b.h(sac), gg // 2 - 1)
                        if kind == 0:
                            s.activation(out=SG[:, (gg // 2 % 2) * 512:(gg // 2 % 2 + 1) * 512],
                                         in_=PS[gg % 2][:, 0:512],
                                         func=AF.Silu).then_inc(b.h(sge), 1)
                        else:
                            s.activation(out=UP[:, (gg // 2 % 2) * 512:(gg // 2 % 2 + 1) * 512],
                                         in_=PS[gg % 2][:, 0:512],
                                         func=AF.Copy).then_inc(b.h(sge), 1)
                        gg += 1
    b.bump(sge, 56 * 32)

    @blk.vector
    def _(v):
        aa = 0
        dd = 0
        for em in range(56):
            e_, mc = divmod(em, 7)
            sl = em % 2
            for msub in range(4):
                for tg in range(4):
                    v.wait_ge(b.h(sge), 2 * aa + 2)
                    if em >= 1 and aa % 16 == 0:
                        v.wait_ge(b.h(sdm), 32 * em)
                    v.tensor_mul(ACT[:, msub * T + tg * 512:msub * T + (tg + 1) * 512],
                                 SG[:, (aa % 2) * 512:(aa % 2 + 1) * 512],
                                 UP[:, (aa % 2) * 512:(aa % 2 + 1) * 512]).then_inc(b.h(sac), 1)
                    aa += 1
            if em == 0:
                v.wait_ge(b.h(sw8), NT)
            for tt in range(NT):
                for dg in range(2):
                    v.wait_ge(b.h(sdm), dd + 1)
                    v.scalar_tensor_tensor(
                        MOE[:, tt * D + dg * 512:tt * D + (dg + 1) * 512],
                        PD[dd % 2][:, 0:512], W8[:, tt * 8 + e_:tt * 8 + e_ + 1],
                        MOE[:, tt * D + dg * 512:tt * D + (dg + 1) * 512],
                        ALU.mult, ALU.add).then_inc(b.h(smo), 1)
                    dd += 1
    b.bump(sac, 56 * 16)
    b.bump(smo, 56 * 32)

    def get_o(i):
        return MOE[:, i * D:(i + 1) * D]

    def post(v, i, st):
        tmp = TMP[:, (i % 2) * D:(i % 2 + 1) * D]
        if i >= 2:
            v.wait_ge(b.h(sst), 16 * (i - 1))
        ln_apply(v, get_o(i), tmp, st)
        v.wait_ge(b.h(sld), 16 + 16 * (i + 1))
        return v.tensor_add(tmp, tmp, XR[:, (i % 2) * D:(i % 2 + 1) * D])

    ln_emit(b, blk, "lnO", NT, D, get_o,
            lambda i: [(smo, 56 * 32)], sfin, post=post)

    x1t = x1.rearrange("(n p) d -> n p d", p=128)
    xot = x2out.rearrange("(n p) d -> n p d", p=128)

    @blk.sync
    def _(e):
        for i in range(NT):
            if i >= 2:
                b.wait(e, sfin, i - 1)
            b.inc(e.dma_start(out=XR[:, (i % 2) * D:(i % 2 + 1) * D], in_=x1t[i]),
                  sld, 16)
            if i >= 2:
                b.wait(e, sfin, i - 1)
                b.inc(e.dma_start(out=xot[i - 2],
                                  in_=TMP[:, ((i - 2) % 2) * D:((i - 2) % 2 + 1) * D]),
                      sst, 16)
        for i in range(NT - 2, NT):
            b.wait(e, sfin, i + 1)
            b.inc(e.dma_start(out=xot[i], in_=TMP[:, (i % 2) * D:(i % 2 + 1) * D]),
                  sst, 16)
        b.wait(e, sst)
    return b.finish()


def f0_flags_np(qc, nglob=64):
    f = np.zeros((4, nglob, 128), np.float32)
    for g in range(4):
        I0 = qc * T + g * 512
        for jt in range(nglob):
            j = jt * 128 + np.arange(128)
            f[g, jt] = np.where(j >= I0, NEG, 0.0).astype(np.float32)
    return np.ascontiguousarray(f.transpose(2, 0, 1).reshape(128, 4 * nglob))


def diag_masks_np():
    m = np.zeros((4, 128, 512), np.float32)
    jd = np.arange(128)[:, None]
    iq = np.arange(512)[None, :]
    for jr in range(4):
        m[jr] = (128 * jr + jd <= iq).astype(np.float32)
    return m.astype(BF16NP)


def run_f0(z, x1):
    ncp = mk_preln()
    imp = [dict(xin=shard_rows(x1, c)) for c in range(NCORES)]
    rp = run_bass_kernel_spmd(ncp, imp, core_ids=list(range(NCORES))).results
    u = [rp[c]["uout"] for c in range(NCORES)]

    ncq = mk_f0qkv()
    prot = rot_perm_np()
    imq = []
    for c in range(NCORES):
        rc, rs = rope_tables_np(c)
        imq.append(dict(uin=u[c], ln1=z["L0_ffn_ln1_w"], wq=z["L0_ffn_wq"],
                        wk=z["L0_ffn_wk"], wv=z["L0_ffn_wv"], prot=prot,
                        rcos=rc, rsin=rs))
    rq = run_bass_kernel_spmd(ncq, imq, core_ids=list(range(NCORES))).results

    ASSIGN = {0: [0, 7, 8, 15], 1: [1, 6, 9, 14], 2: [2, 5, 10, 13],
              3: [3, 4, 11, 12]}
    F0_GB = [16, 32, 48, 64]
    nca = mk_gattn(16, 2, True, gbounds=F0_GB)
    dmask = diag_masks_np()
    ima = []
    for c in range(NCORES):
        bidx, qc = c // 4, c % 4
        ks = [rq[bidx * 4 + q]["kf"] for q in range(4)]
        kglob = np.concatenate(ks, axis=2).reshape(4, 2, 64, 8192)
        kglob = np.ascontiguousarray(kglob.transpose(0, 1, 2, 3).reshape(8, 64, 8192))
        vs = np.concatenate([rq[bidx * 4 + q]["vt"] for q in range(4)], axis=0)
        vglob = np.ascontiguousarray(
            vs.reshape(8192, 8, 65).transpose(1, 0, 2))
        qf_all = [np.asarray(rq[bidx * 4 + q]["qf"]).reshape(16, 64, T)
                  for q in range(4)]
        ub = np.concatenate([np.asarray(u[bidx * 4 + q]) for q in range(4)])
        gl = ASSIGN[qc]
        kd = np.ascontiguousarray(np.concatenate(
            [kglob[:, :, gi * 512:(gi + 1) * 512] for gi in gl], axis=2))
        vd = np.ascontiguousarray(np.concatenate(
            [vglob[:, gi * 512:(gi + 1) * 512, :] for gi in gl], axis=1))
        qh = np.ascontiguousarray(np.concatenate(
            [qf_all[gi // 4][:, :, (gi % 4) * 512:(gi % 4 + 1) * 512] for gi in gl],
            axis=2))
        uin = np.ascontiguousarray(np.concatenate(
            [ub[gi * 512:(gi + 1) * 512] for gi in gl]))
        flg = np.zeros((128, sum(F0_GB)), np.float32)
        off = 0
        for s, gi in enumerate(gl):
            flg[:, off + 4 * gi:off + F0_GB[s]] = NEG
            off += F0_GB[s]
        ima.append(dict(qh=qh, kglob=kglob, vglob=vglob, uin=uin,
                        wo=z["L0_ffn_wo"], flags=flg,
                        dmask=dmask, kd=kd, vd=vd))
    ra = run_bass_kernel_spmd(nca, ima, core_ids=list(range(NCORES))).results
    u1 = []
    for bidx in range(2):
        ub1 = np.zeros((8192, D), np.float32)
        for qc in range(4):
            r_ = np.asarray(ra[bidx * 4 + qc]["u1"])
            for s, gi in enumerate(ASSIGN[qc]):
                ub1[gi * 512:(gi + 1) * 512] = r_[s * 512:(s + 1) * 512]
        for qc in range(4):
            u1.append(np.ascontiguousarray(ub1[qc * T:(qc + 1) * T]))

    ncm = mk_f0moe()
    imm = [dict(u1=u1[c], x1=shard_rows(x1, c), ln2=z["L0_ffn_ln2_w"],
                router=z["L0_ffn_router_w"], wg=z["L0_ffn_w_gate"],
                wu=z["L0_ffn_w_up"], wd=z["L0_ffn_w_down"]) for c in range(NCORES)]
    rm = run_bass_kernel_spmd(ncm, imm, core_ids=list(range(NCORES))).results
    return unshard([rm[c]["x2"] for c in range(NCORES)])


def mk_f2moe(nem_lim=None):
    b = B()
    u = b.inp("u", [T, D])
    x6 = b.inp("x6", [T, D])
    gate_w = b.inp("gate_w", [D, 8])
    gate_b = b.inp("gate_b", [8])
    ew1 = b.inp("ew1", [9, D, 4096])
    eb1t = b.inp("eb1t", [128, 9 * 32])
    ew2 = b.inp("ew2", [9, 4096, D])
    eb2r = b.inp("eb2r", [9 * D])
    x7out = b.outp("x7", [T, D])
    dbgw = b.outp("dbgw", [128, NT * 8])
    dbgl = b.outp("dbgl", [128, NT * 8])

    MOE = b.sb("MOE", [128, NT * D])
    UF = b.sb("UF", [128, 8 * T], BF16)
    ACT = b.sb("ACT", [128, 4 * T], BF16)
    W1C = b.sb("W1C", [128, 2 * 8 * 512], BF16)
    W2C = b.sb("W2C", [128, 4 * D], BF16)
    EB1 = b.sb("EB1", [128, 9 * 32], F32)
    EB2 = b.sb("EB2", [1, 9 * D], BF16)
    GWN = b.sb("GWN", [128, 8 * 8], BF16)
    GBB = b.sb("GBB", [128, 8], F32)
    GBR = b.sb("GBR", [1, 8], BF16)
    LG = b.sb("LG", [128, NT * 8], F32)
    W8 = b.sb("W8", [128, NT * 8], F32)
    XR = b.sb("XR", [128, 2 * D])
    TMP = b.sb("TMP", [128, 2 * D])

    PS = [b.ps(f"ps{i}", [128, 512], F32) for i in range(2)]
    PD = [b.ps(f"pd{i}", [128, 512], F32) for i in range(2)]
    PSB = b.ps("psb", [128, 512], F32)
    PRT = b.ps("prt", [128, 128], F32)

    blk = b.begin()
    sini = b.sem("sini")
    sld = b.sem("sld")
    sw = b.sem("sw")
    swg = b.sem("swg")
    suf = b.sem("suf")
    srt = b.sem("srt")
    srtc = b.sem("srtc")
    sgm = b.sem("sgm")
    sge = b.sem("sge")
    sdm = b.sem("sdm")
    smo = b.sem("smo")
    sfin = b.sem("sfin")
    sst = b.sem("sst")

    ident_setup(b, blk, sini)
    NE = 9
    NMC = 8
    NEM = NE * NMC if nem_lim is None else nem_lim

    @blk.sync
    def _(e):
        b.inc(e.dma_start(out=MOE[:].rearrange("p (n d) -> p n d", d=D),
                          in_=u.rearrange("(n p) d -> p n d", p=128)), sld, 16)

    @blk.gpsimd
    def _(g):
        b.inc(g.dma_start(out=GWN[:].rearrange("p (c o) -> p c o", o=8),
                          in_=gate_w.rearrange("(c p) o -> p c o", p=128)), sw, 16)
        b.inc(g.dma_start(out=GBR[0:1, :], in_=gate_b[None, :]), sw, 16)
        b.inc(g.dma_start(out=EB1[:], in_=eb1t[:, :]), sw, 16)
        b.inc(g.dma_start(out=EB2[0:1, :], in_=eb2r[None, :]), sw, 16)
        for em in range(NEM):
            e_, mc = divmod(em, NMC)
            sl = em % 2
            if em >= 2:
                g.wait_ge(b.h(sgm), 16 * (em - 1))
            b.inc(g.dma_start(
                out=W1C[:, sl * 8 * 512:(sl + 1) * 8 * 512]
                    .rearrange("p (c m) -> p c m", m=512),
                in_=ew1[e_].rearrange("(c p) m -> p c m", p=128)[:, :, mc * 512:(mc + 1) * 512]),
                swg, 16)
            if em >= 1:
                g.wait_ge(b.h(sdm), 32 * em)
            b.inc(g.dma_start(
                out=W2C[:].rearrange("p (c o) -> p c o", o=D),
                in_=ew2[e_].rearrange("(c p) o -> p c o", p=128)[:, 4 * mc:4 * (mc + 1), :]),
                swg, 16)

    cast_fm_emit(b, blk, "cfU", NT, D,
                 lambda i: MOE[:, i * D:(i + 1) * D],
                 lambda i: [(sld, 16)],
                 lambda i, c: UF[:, c * T + i * 128:c * T + (i + 1) * 128], suf)

    sgb = b.sem("sgb")

    @blk.tensor
    def _(t):
        b.wait(t, suf)
        b.wait(t, sw, 64)
        b.wait(t, sini)
        for tt in range(NT):
            if tt >= 1:
                t.wait_ge(b.h(srtc), tt)
            for ci in range(8):
                t.matmul(PRT[:, 0:8],
                         UF[:, ci * T + tt * 128:ci * T + (tt + 1) * 128],
                         GWN[:, ci * 8:(ci + 1) * 8],
                         start=(ci == 0), stop=False)
            ins = t.matmul(PRT[:, 0:8], b.ones_row[0:1, 0:128], GBR[0:1, 0:8],
                           start=False, stop=True)
            ins.then_inc(b.h(srt), 1)
    b.bump(srt, NT)

    @blk.vector
    def _(v):
        for tt in range(NT):
            v.wait_ge(b.h(srt), tt + 1)
            v.tensor_copy(LG[:, tt * 8:(tt + 1) * 8], PRT[:, 0:8]).then_inc(b.h(srtc), 1)
    b.bump(srtc, NT)

    sw8 = emit_top2(b, blk, "t2", LG, W8, NT, 8, wait_list=[(srtc, None)])

    @blk.tensor
    def _(t):
        b.wait(t, suf)
        gg = 0
        dd = 0
        for em in range(NEM):
            e_, mc = divmod(em, NMC)
            sl = em % 2
            t.wait_ge(b.h(swg), 32 * em + 16)
            for msub in range(4):
                for tg in range(4):
                    if gg >= 2:
                        t.wait_ge(b.h(sge), gg - 1)
                    for ci in range(8):
                        ins = t.matmul(
                            PS[gg % 2][:, 0:512],
                            W1C[:, sl * 8 * 512 + ci * 512 + msub * 128:
                                sl * 8 * 512 + ci * 512 + (msub + 1) * 128],
                            UF[:, ci * T + tg * 512:ci * T + (tg + 1) * 512],
                            start=(ci == 0), stop=(ci == 7))
                    ins.then_inc(b.h(sgm), 1)
                    gg += 1
            t.wait_ge(b.h(swg), 32 * (em + 1))
            t.wait_ge(b.h(sge), gg)
            for tt in range(NT):
                for dg in range(2):
                    if dd >= 2:
                        t.wait_ge(b.h(smo), dd - 1)
                    last_mc = (mc == NMC - 1)
                    for msub in range(4):
                        ins = t.matmul(
                            PD[dd % 2][:, 0:512],
                            ACT[:, msub * T + tt * 128:msub * T + (tt + 1) * 128],
                            W2C[:, msub * D + dg * 512:msub * D + (dg + 1) * 512],
                            start=(msub == 0), stop=(msub == 3 and not last_mc))
                    if last_mc:
                        ins = t.matmul(PD[dd % 2][:, 0:512], b.ones_row[0:1, 0:128],
                                       EB2[0:1, e_ * D + dg * 512:e_ * D + (dg + 1) * 512],
                                       start=False, stop=True)
                    ins.then_inc(b.h(sdm), 1)
                    dd += 1
    b.bump(sgm, NEM * 16)
    b.bump(sdm, NEM * 32)

    @blk.scalar
    def _(s):
        gg = 0
        for em in range(NEM):
            e_, mc = divmod(em, NMC)
            for msub in range(4):
                for tg in range(4):
                    s.wait_ge(b.h(sgm), gg + 1)
                    if em >= 1 and gg % 16 == 0:
                        s.wait_ge(b.h(sdm), 32 * em)
                    s.activation(out=ACT[:, msub * T + tg * 512:msub * T + (tg + 1) * 512],
                                 in_=PS[gg % 2][:, 0:512], func=AF.Gelu_apprx_tanh,
                                 bias=EB1[:, e_ * 32 + mc * 4 + msub:
                                          e_ * 32 + mc * 4 + msub + 1]).then_inc(b.h(sge), 1)
                    gg += 1
    b.bump(sge, NEM * 16)

    DBGD = b.sb("DBGD", [128, 4 * D], BF16) if nem_lim is not None else None

    @blk.vector
    def _(v):
        dd = 0
        for em in range(NEM):
            e_, mc = divmod(em, NMC)
            if em == 0:
                v.wait_ge(b.h(sw8), NT)
            for tt in range(NT):
                for dg in range(2):
                    v.wait_ge(b.h(sdm), dd + 1)
                    dst = MOE[:, tt * D + dg * 512:tt * D + (dg + 1) * 512]
                    if DBGD is not None and em == 0 and tt < 4:
                        v.tensor_copy(DBGD[:, tt * D + dg * 512:tt * D + (dg + 1) * 512],
                                      PD[dd % 2][:, 0:512])
                    if e_ == 8:
                        v.tensor_add(dst, PD[dd % 2][:, 0:512], dst).then_inc(b.h(smo), 1)
                    else:
                        v.scalar_tensor_tensor(dst, PD[dd % 2][:, 0:512],
                                               W8[:, tt * 8 + e_:tt * 8 + e_ + 1],
                                               dst, ALU.mult, ALU.add).then_inc(b.h(smo), 1)
                    dd += 1
    b.bump(smo, NEM * 32)

    if nem_lim is not None:
        dbgm = b.outp("dbgm", [128, NT * D])
        dbga = b.outp("dbga", [128, 4 * T], BF16)
        dbgd = b.outp("dbgd", [128, 4 * D], BF16)

        @blk.sync
        def _(e):
            b.wait(e, smo, NEM * 32)
            b.inc(e.dma_start(out=dbgm, in_=MOE[:]), sst, 16)
            b.inc(e.dma_start(out=dbga, in_=ACT[:]), sst, 16)
            b.inc(e.dma_start(out=dbgd, in_=DBGD[:]), sst, 16)
            b.wait(e, "t2_d", NT)
            b.inc(e.dma_start(out=dbgw, in_=W8[:]), sst, 16)
            b.inc(e.dma_start(out=dbgl, in_=LG[:]), sst, 16)
            b.wait(e, sst)
        return b.finish()

    dbgm2 = b.outp("dbgm2", [128, NT * D])

    @blk.sync
    def _(e):
        b.wait(e, smo, NEM * 32)
        b.inc(e.dma_start(out=dbgm2, in_=MOE[:]), b.sem("sdbg"), 16)

    def get_o(i):
        return MOE[:, i * D:(i + 1) * D]

    def post(v, i, st):
        tmp = TMP[:, (i % 2) * D:(i % 2 + 1) * D]
        if i >= 2:
            v.wait_ge(b.h(sst), 32 + 16 * (i - 1))
        ln_apply(v, get_o(i), tmp, st)
        v.wait_ge(b.h(sld), 16 + 16 * (i + 1))
        return v.tensor_add(tmp, tmp, XR[:, (i % 2) * D:(i % 2 + 1) * D])

    ln_emit(b, blk, "lnO", NT, D, get_o,
            lambda i: [(smo, NEM * 32)], sfin, post=post)

    x6t = x6.rearrange("(n p) d -> n p d", p=128)
    xot = x7out.rearrange("(n p) d -> n p d", p=128)

    @blk.sync
    def _(e):
        b.wait(e, "t2_d", NT)
        b.inc(e.dma_start(out=dbgw, in_=W8[:]), sst, 16)
        b.inc(e.dma_start(out=dbgl, in_=LG[:]), sst, 16)
        for i in range(NT):
            if i >= 2:
                b.wait(e, sfin, i - 1)
            b.inc(e.dma_start(out=XR[:, (i % 2) * D:(i % 2 + 1) * D], in_=x6t[i]),
                  sld, 16)
            if i >= 2:
                b.inc(e.dma_start(out=xot[i - 2],
                                  in_=TMP[:, ((i - 2) % 2) * D:((i - 2) % 2 + 1) * D]),
                      sst, 16)
        for i in range(NT - 2, NT):
            b.wait(e, sfin, i + 1)
            b.inc(e.dma_start(out=xot[i], in_=TMP[:, (i % 2) * D:(i % 2 + 1) * D]),
                  sst, 16)
        b.wait(e, sst)
    return b.finish()


def cast_fm_emit(b, blk, tag, ntiles, d, get_x, wait_for_tile, fm_dst, done,
                 ubr=None):
    nch = d // 128
    UBR = ubr if ubr is not None else b.sb(tag + "_ub", [128, 2 * d], BF16)
    scp = b.sem(tag + "_c")
    base_done = b.val(done)

    @blk.vector
    def _(v):
        for i in range(ntiles):
            for w, val in wait_for_tile(i):
                b.wait(v, w, val)
            if i >= 2:
                v.wait_ge(b.h(done), base_done + 16 * nch * (i - 1))
            v.tensor_copy(UBR[:, (i % 2) * d:(i % 2 + 1) * d],
                          get_x(i)).then_inc(b.h(scp), 1)
        b.bump(scp, ntiles)

    @blk.scalar
    def _(s):
        for i in range(ntiles):
            s.wait_ge(b.h(scp), i + 1)
            for c in range(nch):
                s.dma_start(out=fm_dst(i, c),
                            in_=UBR[:, (i % 2) * d + c * 128:(i % 2) * d + (c + 1) * 128],
                            transpose=True).then_inc(b.h(done), 16)
        b.bump(done, 16 * nch * ntiles)


def mk_a3qkv():
    b = B()
    uin = b.inp("uin", [T, D])
    wqkvT = b.inp("wqkvT", [D, 3072])
    bqkv = b.inp("bqkv", [3072])
    qkb = b.inp("qkb", [128, 32])
    oq = b.outp("oq", [16, 64, T], BF16)
    ok = b.outp("ok", [16, 64, T], BF16)
    ovt = b.outp("vt", [T, 16 * 65], BF16)

    UR = b.sb("UR", [128, 2 * D])
    UF = b.sb("UF", [128, 8 * T], BF16)
    WT = b.sb("WT", [128, 8 * 3072], BF16)
    QF = b.sb("QF", [128, 8 * T], BF16)
    KFB = b.sb("KFB", [128, 8 * T], BF16)
    VT = b.sb("VT", [128, NT * 16 * 65], BF16)
    BR = b.sb("BR", [1, 3072], BF16)

    PS = [b.ps(f"ps{i}", [128, 512], F32) for i in range(3)]

    blk = b.begin()
    sini = b.sem("sini")
    sld = b.sem("sld")
    sw = b.sem("sw")
    swt = b.sem("swt")
    suf = b.sem("suf")
    sproj = b.sem("sproj")
    sst = b.sem("sst")
    b.sem("cfU_c")

    ident_setup(b, blk, sini)
    uit = uin.rearrange("(n p) d -> n p d", p=128)

    @blk.sync
    def _(e):
        for i in range(NT):
            if i >= 2:
                e.wait_ge(b.h("cfU_c"), i - 1)
            b.inc(e.dma_start(out=UR[:, (i % 2) * D:(i % 2 + 1) * D], in_=uit[i]),
                  sld, 16)

    @blk.gpsimd
    def _(g):
        b.inc(g.dma_start(out=WT[:].rearrange("p (c o) -> p c o", o=3072),
                          in_=wqkvT.rearrange("(c p) o -> p c o", p=128)), sw, 16)
        b.inc(g.dma_start(out=BR[0:1, :], in_=bqkv[None, :]), sw, 16)
        g.memset(VT[:].rearrange("p (n h c) -> p n h c", h=16, c=65)[:, :, :, 64:65], 1.0)
        b.inc(g.engine_nop(), sini)

    cast_fm_emit(b, blk, "cfU", NT, D,
                 lambda i: UR[:, (i % 2) * D:(i % 2 + 1) * D],
                 lambda i: [(sld, 16 * (i + 1))],
                 lambda i, c: UF[:, c * T + i * 128:c * T + (i + 1) * 128], suf)

    QKB = b.sb("QKB", [128, 32], F32)

    @blk.sync
    def _(e):
        b.inc(e.dma_start(out=QKB[:], in_=qkb[:, :]), sld, 16)
    SLDB = b.val("sld")

    groups = []
    for h in range(16):
        for kind in range(2):
            base_o = 192 * h + 64 * kind
            ro = (h % 2) * 64
            for g4 in range(4):
                mms = [(WT[:, ci * 3072 + base_o:ci * 3072 + base_o + 64],
                        UF[:, ci * T + g4 * 512:ci * T + (g4 + 1) * 512])
                       for ci in range(8)]

                def ev(e, pa, h=h, kind=kind, g4=g4, ro=ro):
                    dstbuf = QF if kind == 0 else KFB
                    return e.activation(
                        out=dstbuf[ro:ro + 64, (h // 2) * T + g4 * 512:
                                   (h // 2) * T + (g4 + 1) * 512],
                        in_=pa, func=AF.Identity,
                        bias=QKB[ro:ro + 64, kind * 16 + h:kind * 16 + h + 1])
                groups.append(dict(mms=mms, pslice=lambda p: p[0:64, 0:512],
                                   evac_eng="scalar", evac=ev))
    for tt in range(NT):
        for h in range(16):
            base_o = 192 * h + 128
            mms = [(UF[:, ci * T + tt * 128:ci * T + (tt + 1) * 128],
                    WT[:, ci * 3072 + base_o:ci * 3072 + base_o + 64])
                   for ci in range(8)]
            mms.append((b.ones_row[0:1, 0:128], BR[0:1, base_o:base_o + 64]))

            def ev(e, pa, tt=tt, h=h):
                return e.tensor_copy(
                    VT[:, tt * 16 * 65 + h * 65:tt * 16 * 65 + h * 65 + 64], pa)
            groups.append(dict(mms=mms, pslice=lambda p: p[:, 0:64],
                               evac_eng="vector", evac=ev))

    gemm(b, blk, "proj", groups, PS,
         wait_list=[(suf, None), (sld, SLDB), (sw, 32)], done=sproj)

    @blk.sync
    def _(e):
        b.wait(e, sproj)
        b.wait(e, sini)
        b.inc(e.dma_start(out=oq.rearrange("a p t -> p a t"),
                          in_=QF[:].rearrange("p (a t) -> p a t", t=T)), sst, 16)
        b.inc(e.dma_start(out=ok.rearrange("a p t -> p a t"),
                          in_=KFB[:].rearrange("p (a t) -> p a t", t=T)), sst, 16)
        b.inc(e.dma_start(out=ovt.rearrange("(n p) c -> p n c", p=128),
                          in_=VT[:].rearrange("p (n c) -> p n c", c=16 * 65)), sst, 16)
        b.wait(e, sst)
    return b.finish()



def mk_postln():
    b = B()
    o = b.inp("o", [T, D])
    xprev = b.inp("xprev", [T, D])
    xout = b.outp("xout", [T, D])
    OR_ = b.sb("OR", [128, 4 * D])
    XR = b.sb("XR", [128, 4 * D])
    TMP = b.sb("TMP", [128, 2 * D])
    blk = b.begin()
    sld = b.sem("sld")
    slx = b.sem("slx")
    sfin = b.sem("sfin")
    sst = b.sem("sst")

    def get_o(i):
        return OR_[:, (i % 4) * D:(i % 4 + 1) * D]

    def post(v, i, st):
        tmp = TMP[:, (i % 2) * D:(i % 2 + 1) * D]
        if i >= 2:
            v.wait_ge(b.h(sst), 16 * (i - 1))
        ln_apply(v, get_o(i), tmp, st)
        v.wait_ge(b.h(slx), 16 * (i + 1))
        return v.tensor_add(tmp, tmp, XR[:, (i % 4) * D:(i % 4 + 1) * D])

    ln_emit(b, blk, "lnP", NT, D, get_o,
            lambda i: [(sld, 16 * (i + 1))], sfin, post=post)

    ot = o.rearrange("(n p) d -> n p d", p=128)
    xt = xprev.rearrange("(n p) d -> n p d", p=128)
    xo = xout.rearrange("(n p) d -> n p d", p=128)

    @blk.sync
    def _(e):
        for i in range(NT):
            if i >= 4:
                b.wait(e, sfin, i - 3)
            b.inc(e.dma_start(out=OR_[:, (i % 4) * D:(i % 4 + 1) * D], in_=ot[i]), sld, 16)
            b.inc(e.dma_start(out=XR[:, (i % 4) * D:(i % 4 + 1) * D], in_=xt[i]), slx, 16)
            if i >= 2:
                b.wait(e, sfin, i - 1)
                b.inc(e.dma_start(out=xo[i - 2],
                                  in_=TMP[:, ((i - 2) % 2) * D:((i - 2) % 2 + 1) * D]),
                      sst, 16)
        for i in range(NT - 2, NT):
            b.wait(e, sfin, i + 1)
            b.inc(e.dma_start(out=xo[i], in_=TMP[:, (i % 2) * D:(i % 2 + 1) * D]), sst, 16)
        b.wait(e, sst)
    return b.finish()


def serial_script(b, blk, tag, steps):
    ss = b.sem(tag + "_s")
    per_eng = {}
    for idx, (eng, fn) in enumerate(steps):
        per_eng.setdefault(eng, []).append((idx, fn))
    for eng, lst in per_eng.items():
        def mk(eng=eng, lst=lst):
            def body(e):
                for idx, fn in lst:
                    if idx > 0:
                        e.wait_ge(b.h(ss), idx)
                    fn(e).then_inc(b.h(ss), 1)
            return body
        getattr(blk, eng)(mk())
    b.bump(ss, len(steps))
    return ss


def mk_f3gf():
    b = B()
    gcp = b.inp("gcp", [4, D])
    gw1 = b.inp("gw1", [D, 512])
    gw2 = b.inp("gw2", [512, 256])
    gw3 = b.inp("gw3", [256, 1])
    gb1r = b.inp("gb1r", [512])
    gb2r = b.inp("gb2r", [256])
    gb3 = b.inp("gb3", [1])
    ar = b.inp("ar", [1])
    oc01 = b.outp("oc01", [1, 2])

    GCB = b.sb("GCB", [4, D], BF16)
    ONE4 = b.sb("ONE4", [4, 128], BF16)
    GCRow = b.sb("GCRow", [1, D], F32)
    GCF = b.sb("GCF", [128, 16], BF16)
    GW1N = b.sb("GW1N", [128, 8 * 512], BF16)
    GW2N = b.sb("GW2N", [128, 4 * 256], BF16)
    GW3N = b.sb("GW3N", [128, 2 * 1], BF16)
    GR1 = b.sb("GR1", [1, 512], F32)
    GR1B = b.sb("GR1B", [1, 512], BF16)
    GB1R = b.sb("GB1R", [1, 512], F32)
    GB2R = b.sb("GB2R", [1, 256], F32)
    SCL = b.sb("SCL", [1, 64], F32)
    PS2 = b.ps("ps2", [128, 512], F32)
    PT = b.ps("pt", [128, 512], F32)
    PTB = b.ps("ptb", [128, 256], BF16)

    blk = b.begin()
    sini = b.sem("sini")
    sld = b.sem("sld")
    sw = b.sem("sw")
    sst = b.sem("sst")
    ident_setup(b, blk, sini)

    @blk.sync
    def _(e):
        b.inc(e.dma_start(out=SCL[0:1, 1:2], in_=gb3[None, :]), sld, 16)
        b.inc(e.dma_start(out=SCL[0:1, 2:3], in_=ar[None, :]), sld, 16)
        b.inc(e.dma_start(out=GB1R[0:1, :], in_=gb1r[None, :]), sld, 16)
        b.inc(e.dma_start(out=GB2R[0:1, 0:256], in_=gb2r[None, :]), sld, 16)

    @blk.gpsimd
    def _(g):
        g.memset(ONE4[:], 1.0)
        b.inc(g.dma_start(out=GCB[:], in_=gcp), sw, 16)
        for dst, srcw, o in [(GW1N, gw1, 512), (GW2N, gw2, 256), (GW3N, gw3, 1)]:
            nch = srcw.shape[0] // 128
            b.inc(g.dma_start(out=dst[:, 0:nch * o].rearrange("p (c q) -> p c q", q=o),
                              in_=srcw.rearrange("(c p) q -> p c q", p=128),
                              allow_slow_non_contiguous=(o == 1)), sw, 16)

    steps = []
    steps.append(("tensor", lambda t: (b.wait(t, sw, None), b.wait(t, sld, None),
                  b.wait(t, sini, None),
                  t.matmul(PS2[:, 0:512], ONE4[0:4, 0:128], GCB[0:4, 0:512],
                           start=True, stop=True))[-1]))
    steps.append(("vector", lambda v: v.tensor_scalar_mul(
        GCRow[0:1, 0:512], PS2[0:1, 0:512], 1.0 / 8192.0)))
    steps.append(("tensor", lambda t: t.matmul(
        PS2[:, 0:512], ONE4[0:4, 0:128], GCB[0:4, 512:1024], start=True, stop=True)))
    steps.append(("vector", lambda v: v.tensor_scalar_mul(
        GCRow[0:1, 512:1024], PS2[0:1, 0:512], 1.0 / 8192.0)))
    for c in range(8):
        steps.append(("tensor", lambda t, c=c: (
            t.transpose(PT[:, 0:1], GCRow[0:1, c * 128:(c + 1) * 128],
                        b.ident_f32[0:1, 0:1]), t.nop())[-1]))
        steps.append(("vector", lambda v, c=c: v.tensor_copy(GCF[:, c:c + 1], PT[:, 0:1])))

    def mm_row(t, src_cols, W, nch, width, ps):
        for ci in range(nch):
            ins = t.matmul(ps[0:1, 0:width], src_cols(ci), W(ci),
                           start=(ci == 0), stop=(ci == nch - 1))
        return ins
    steps.append(("tensor", lambda t: mm_row(
        t, lambda ci: GCF[:, ci:ci + 1],
        lambda ci: GW1N[:, ci * 512:(ci + 1) * 512], 8, 512, PS2)))
    steps.append(("vector", lambda v: v.tensor_add(GR1[0:1, :], PS2[0:1, 0:512], GB1R[0:1, :])))
    steps.append(("scalar", lambda s: s.activation(out=GR1[0:1, :], in_=GR1[0:1, :], func=AF.Gelu)))
    steps.append(("vector", lambda v: v.tensor_copy(GR1B[0:1, :], GR1[0:1, :])))
    for c in range(4):
        steps.append(("tensor", lambda t, c=c: (
            t.transpose(PTB[:, 0:1], GR1B[0:1, c * 128:(c + 1) * 128],
                        b.ident_bf[0:1, 0:1]), t.nop())[-1]))
        steps.append(("vector", lambda v, c=c: v.tensor_copy(GCF[:, 8 + c:9 + c], PTB[:, 0:1])))
    steps.append(("tensor", lambda t: mm_row(
        t, lambda ci: GCF[:, 8 + ci:9 + ci],
        lambda ci: GW2N[:, ci * 256:(ci + 1) * 256], 4, 256, PS2)))
    steps.append(("vector", lambda v: v.tensor_add(GR1[0:1, 0:256], PS2[0:1, 0:256], GB2R[0:1, 0:256])))
    steps.append(("scalar", lambda s: s.activation(out=GR1[0:1, 0:256], in_=GR1[0:1, 0:256], func=AF.Gelu)))
    steps.append(("vector", lambda v: v.tensor_copy(GR1B[0:1, 0:256], GR1[0:1, 0:256])))
    for c in range(2):
        steps.append(("tensor", lambda t, c=c: (
            t.transpose(PTB[:, 0:1], GR1B[0:1, c * 128:(c + 1) * 128],
                        b.ident_bf[0:1, 0:1]), t.nop())[-1]))
        steps.append(("vector", lambda v, c=c: v.tensor_copy(GCF[:, 12 + c:13 + c], PTB[:, 0:1])))
    steps.append(("tensor", lambda t: mm_row(
        t, lambda ci: GCF[:, 12 + ci:13 + ci],
        lambda ci: GW3N[:, ci:ci + 1], 2, 1, PS2)))
    steps.append(("vector", lambda v: v.tensor_add(SCL[0:1, 4:5], PS2[0:1, 0:1], SCL[0:1, 1:2])))
    steps.append(("scalar", lambda s: s.activation(out=SCL[0:1, 3:4], in_=SCL[0:1, 4:5], func=AF.Sigmoid)))
    steps.append(("vector", lambda v: v.tensor_scalar_mul(SCL[0:1, 5:6], SCL[0:1, 2:3], 0.6)))
    steps.append(("vector", lambda v: v.tensor_scalar(SCL[0:1, 6:7], SCL[0:1, 3:4], 1.4, -1.0, ALU.mult, ALU.add)))
    steps.append(("vector", lambda v: v.tensor_mul(SCL[0:1, 6:7], SCL[0:1, 6:7], SCL[0:1, 2:3])))
    steps.append(("vector", lambda v: v.tensor_scalar_add(SCL[0:1, 6:7], SCL[0:1, 6:7], 0.1)))
    sgf = serial_script(b, blk, "gf", steps)

    @blk.sync
    def _(e):
        b.wait(e, sgf, None)
        b.inc(e.dma_start(out=oc01, in_=SCL[0:1, 5:7]), sst, 16)
        b.wait(e, sst)
    return b.finish()


def mk_f3cyc():
    b = B()
    sin_ = b.inp("sin", [T, D])
    cw1 = b.inp("cw1", [D, D])
    cw2 = b.inp("cw2", [D, 512])
    cw3 = b.inp("cw3", [512, 256])
    cw4 = b.inp("cw4", [256, 1])
    cb1c = b.inp("cb1c", [128, 8])
    cb2c = b.inp("cb2c", [128, 4])
    cb3c = b.inp("cb3c", [128, 2])
    cb4 = b.inp("cb4", [1])
    c01 = b.inp("c01", [1, 2])
    sout = b.outp("sout", [T, D])

    OUT = b.sb("OUT", [128, NT * D])
    OUTF = b.sb("OUTF", [128, 8 * T], BF16)
    H1F = b.sb("H1F", [128, 8 * T], BF16)
    H2F = b.sb("H2F", [128, 4 * T], BF16)
    H3F = b.sb("H3F", [128, 2 * T], BF16)
    W1N = b.sb("W1N", [128, 8 * D], BF16)
    W2N = b.sb("W2N", [128, 8 * 512], BF16)
    W3N = b.sb("W3N", [128, 4 * 256], BF16)
    W4N = b.sb("W4N", [128, 2 * 1], BF16)
    CB1 = b.sb("CB1", [128, 8], F32)
    CB2 = b.sb("CB2", [128, 4], F32)
    CB3 = b.sb("CB3", [128, 2], F32)
    SCL = b.sb("SCL", [1, 64], F32)
    TF = b.sb("TF", [1, T], F32)
    CPC = b.sb("CPC", [128, NT], F32)
    TW = b.sb("TW", [128, D], F32)
    F3UB = b.sb("F3UB", [128, 2 * D], BF16)

    PS = [b.ps(f"ps{i}", [128, 512], F32) for i in range(3)]
    PT = b.ps("pt", [128, 512], F32)

    blk = b.begin()
    sini = b.sem("sini")
    sld = b.sem("sld")
    sw = b.sem("sw")
    sst = b.sem("sst")
    ident_setup(b, blk, sini)
    sit = sin_.rearrange("(n p) d -> n p d", p=128)
    sot = sout.rearrange("(n p) d -> n p d", p=128)

    @blk.sync
    def _(e):
        b.inc(e.dma_start(out=OUT[:].rearrange("p (n d) -> p n d", d=D),
                          in_=sit.rearrange("n p d -> p n d")), sld, 16)
        b.inc(e.dma_start(out=CB1[:], in_=cb1c[:, :]), sld, 16)
        b.inc(e.dma_start(out=CB2[:, 0:4], in_=cb2c[:, :]), sld, 16)
        b.inc(e.dma_start(out=CB3[:, 0:2], in_=cb3c[:, :]), sld, 16)
        b.inc(e.dma_start(out=SCL[0:1, 0:1], in_=cb4[None, :]), sld, 16)
        b.inc(e.dma_start(out=SCL[0:1, 5:7], in_=c01), sld, 16)
    SLD0 = b.val("sld")

    @blk.gpsimd
    def _(g):
        for dst, srcw, o in [(W1N, cw1, D), (W2N, cw2, 512), (W3N, cw3, 256),
                             (W4N, cw4, 1)]:
            nch = srcw.shape[0] // 128
            b.inc(g.dma_start(out=dst[:, 0:nch * o].rearrange("p (c q) -> p c q", q=o),
                              in_=srcw.rearrange("(c p) q -> p c q", p=128),
                              allow_slow_non_contiguous=(o == 1)), sw, 16)

    suf = b.sem("c_uf")
    cast_fm_emit(b, blk, "c_cf", NT, D,
                 lambda i: OUT[:, i * D:(i + 1) * D],
                 lambda i: [(sld, SLD0)],
                 lambda i, c: OUTF[:, c * T + i * 128:c * T + (i + 1) * 128], suf,
                 ubr=F3UB)
    g1 = []
    for co in range(8):
        for tg in range(4):
            mms = [(W1N[:, ci * D + co * 128:ci * D + (co + 1) * 128],
                    OUTF[:, ci * T + tg * 512:ci * T + (tg + 1) * 512])
                   for ci in range(8)]

            def ev(e, pa, co=co, tg=tg):
                return e.activation(out=H1F[:, co * T + tg * 512:co * T + (tg + 1) * 512],
                                    in_=pa, func=AF.Gelu, bias=CB1[:, co:co + 1])
            g1.append(dict(mms=mms, pslice=lambda p: p[:, 0:512],
                           evac_eng="scalar", evac=ev))
    s1 = b.sem("c_s1")
    gemm(b, blk, "c_g1", g1, PS[0:2],
         wait_list=[(suf, None), (sw, None), (sld, SLD0)], done=s1)
    g2 = []
    for co in range(4):
        for tg in range(4):
            mms = [(W2N[:, ci * 512 + co * 128:ci * 512 + (co + 1) * 128],
                    H1F[:, ci * T + tg * 512:ci * T + (tg + 1) * 512])
                   for ci in range(8)]

            def ev(e, pa, co=co, tg=tg):
                return e.activation(out=H2F[:, co * T + tg * 512:co * T + (tg + 1) * 512],
                                    in_=pa, func=AF.Gelu, bias=CB2[:, co:co + 1])
            g2.append(dict(mms=mms, pslice=lambda p: p[:, 0:512],
                           evac_eng="scalar", evac=ev))
    s2 = b.sem("c_s2")
    gemm(b, blk, "c_g2", g2, PS[0:2], wait_list=[(s1, None)], done=s2)
    g3 = []
    for co in range(2):
        for tg in range(4):
            mms = [(W3N[:, ci * 256 + co * 128:ci * 256 + (co + 1) * 128],
                    H2F[:, ci * T + tg * 512:ci * T + (tg + 1) * 512])
                   for ci in range(4)]

            def ev(e, pa, co=co, tg=tg):
                return e.activation(out=H3F[:, co * T + tg * 512:co * T + (tg + 1) * 512],
                                    in_=pa, func=AF.Gelu, bias=CB3[:, co:co + 1])
            g3.append(dict(mms=mms, pslice=lambda p: p[:, 0:512],
                           evac_eng="scalar", evac=ev))
    s3 = b.sem("c_s3")
    gemm(b, blk, "c_g3", g3, PS[0:2], wait_list=[(s2, None)], done=s3)
    g4 = []
    for tg in range(4):
        mms = [(W4N[:, ci:ci + 1],
                H3F[:, ci * T + tg * 512:ci * T + (tg + 1) * 512])
               for ci in range(2)]

        def ev(e, pa, tg=tg):
            return e.activation(out=TF[0:1, tg * 512:(tg + 1) * 512],
                                in_=pa, func=AF.Sigmoid, bias=SCL[0:1, 0:1])
        g4.append(dict(mms=mms, pslice=lambda p: p[0:1, 0:512],
                       evac_eng="scalar", evac=ev))
    s4 = b.sem("c_s4")
    gemm(b, blk, "c_g4", g4, PS[0:2], wait_list=[(s3, None)], done=s4)

    st2 = []
    for tg in range(4):
        st2.append(("vector", lambda v, tg=tg: (
            b.wait(v, s4, None),
            v.tensor_scalar(TF[0:1, tg * 512:(tg + 1) * 512],
                            TF[0:1, tg * 512:(tg + 1) * 512],
                            SCL[0:1, 5:6], SCL[0:1, 6:7],
                            ALU.mult, ALU.add))[-1]))
    for tt in range(NT):
        st2.append(("tensor", lambda t, tt=tt: (
            b.wait(t, sini, None),
            t.transpose(PT[:, 0:1], TF[0:1, tt * 128:(tt + 1) * 128],
                        b.ident_f32[0:1, 0:1]), t.nop())[-1]))
        st2.append(("vector", lambda v, tt=tt: v.tensor_copy(CPC[:, tt:tt + 1], PT[:, 0:1])))
    scyc = serial_script(b, blk, "ss", st2)
    NS2 = b.val(scyc)

    sup = b.sem("sup")

    @blk.vector
    def _(v):
        for tt in range(NT):
            v.wait_ge(b.h(scyc), NS2)
            ob = OUT[:, tt * D:(tt + 1) * D]
            v.tensor_copy(TW[:, 0:256], ob[:, 512:768])
            v.tensor_scalar_mul(TW[:, 256:512], ob[:, 768:1024], -1.0)
            v.tensor_scalar_mul(TW[:, 512:768], ob[:, 0:256], -1.0)
            v.tensor_copy(TW[:, 768:1024], ob[:, 256:512])
            v.scalar_tensor_tensor(ob, TW[:], CPC[:, tt:tt + 1], ob,
                                   ALU.mult, ALU.add).then_inc(b.h(sup), 1)
        b.bump(sup, NT)

    @blk.sync
    def _(e):
        for tt in range(NT):
            b.wait(e, sup, tt + 1)
            b.inc(e.dma_start(out=sot[tt], in_=OUT[:, tt * D:(tt + 1) * D]), sst, 16)
        b.wait(e, sst)
    return b.finish()


def mk_colsum():
    b = B()
    uin = b.inp("uin", [T, D])
    out = b.outp("out", [1, D])
    UR = b.sb("UR", [128, 4 * D])
    URB = b.sb("URB", [128, 4 * D], BF16)
    ONESF = b.sb("ONESF", [128, 128], BF16)
    RO = b.sb("RO", [1, D])
    PS = b.ps("ps", [128, 512], F32)
    PS2 = b.ps("ps2", [128, 512], F32)
    blk = b.begin()
    sini = b.sem("sini")
    sld = b.sem("sld")
    smm = b.sem("smm")
    scp = b.sem("scp")
    sst = b.sem("sst")
    ident_setup(b, blk, sini)
    uit = uin.rearrange("(n p) d -> n p d", p=128)

    @blk.sync
    def _(e):
        for i in range(NT):
            if i >= 4:
                e.wait_ge(b.h(smm), 2 * (i - 3))
            b.inc(e.dma_start(out=UR[:, (i % 4) * D:(i % 4 + 1) * D], in_=uit[i]),
                  sld, 16)

    sac = b.sem("sac")
    scv = b.sem("scv")

    @blk.gpsimd
    def _(g):
        b.inc(g.memset(ONESF[:], 1.0), sini)

    @blk.tensor
    def _(t):
        b.wait(t, sini)
        for i in range(NT):
            t.wait_ge(b.h(scv), i + 1)
            if i >= 1:
                t.wait_ge(b.h(sac), 2 * i)
            t.matmul(PS[:, 0:512], ONESF[:],
                     URB[:, (i % 4) * D:(i % 4) * D + 512],
                     start=True, stop=True).then_inc(b.h(smm), 1)
            t.matmul(PS2[:, 0:512], ONESF[:],
                     URB[:, (i % 4) * D + 512:(i % 4) * D + 1024],
                     start=True, stop=True).then_inc(b.h(smm), 1)
    b.bump(smm, 2 * NT)

    @blk.vector
    def _(v):
        for i in range(NT):
            v.wait_ge(b.h(sld), 16 * (i + 1))
            v.tensor_copy(URB[:, (i % 4) * D:(i % 4 + 1) * D],
                          UR[:, (i % 4) * D:(i % 4 + 1) * D]).then_inc(b.h(scv), 1)
            v.wait_ge(b.h(smm), 2 * i + 1)
            if i == 0:
                v.tensor_copy(RO[0:1, 0:512], PS[0:1, 0:512]).then_inc(b.h(sac), 1)
            else:
                v.tensor_add(RO[0:1, 0:512], RO[0:1, 0:512],
                             PS[0:1, 0:512]).then_inc(b.h(sac), 1)
            v.wait_ge(b.h(smm), 2 * i + 2)
            if i == 0:
                v.tensor_copy(RO[0:1, 512:1024], PS2[0:1, 0:512]).then_inc(b.h(sac), 1)
            else:
                v.tensor_add(RO[0:1, 512:1024], RO[0:1, 512:1024],
                             PS2[0:1, 0:512]).then_inc(b.h(sac), 1)
        b.inc(v.engine_nop(), scp)
    b.bump(scv, NT)
    b.bump(sac, 2 * NT)

    @blk.sync
    def _(e):
        b.wait(e, scp)
        b.inc(e.dma_start(out=out[0:1, :], in_=RO[0:1, :]), sst, 16)
        b.wait(e, sst)
    return b.finish()


F1_TE = 2688
F1_NKT = 20


def f1_tables_np(core):
    qc = core % 4
    half = np.arange(0, 64, 2, dtype=np.float64)
    inv = 1.0 / 10000.0 ** (half / 64.0)
    base = (half + 0.4 * 64.0) / (1.4 * 64.0)

    def tab(pos):
        fr = pos[:, None] * inv[None, :]
        fr = np.concatenate([fr, fr], axis=1)
        power = (pos - 4096.0) / 256.0
        sc = base[None, :] ** power[:, None]
        sc = np.concatenate([sc, sc], axis=1)
        return np.cos(fr), np.sin(fr), sc
    own = np.arange(qc * T, (qc + 1) * T, dtype=np.float64)
    c, s, sc = tab(own)
    csq = (c * sc).astype(np.float32)
    ssq = (s * sc).astype(np.float32)
    ext = np.arange(qc * T - 512, qc * T + T, dtype=np.float64)
    ext = np.clip(ext, 0, None)
    c, s, sc = tab(ext)
    csk = (c / sc).astype(np.float32)
    ssk = (s / sc).astype(np.float32)
    return csq, ssq, csk, ssk


def f1_masks_np():
    m = np.zeros((8, 128, 512), np.float32)
    jd = np.arange(128)[:, None]
    i0 = np.arange(512)[None, :]
    for r in range(8):
        if r < 4:
            m[r] = (r * 128 + jd >= i0).astype(np.float32)
        else:
            m[r] = ((r - 4) * 128 + jd <= i0).astype(np.float32)
    return np.ascontiguousarray((m - 1.0) * 60000.0)


def f1_flags_np(core):
    f = np.zeros((128, 32), np.float32)
    if core % 4 == 0:
        for r in range(4):
            f[:, 0 * 8 + r] = NEG
    return f


def mk_f1a1():
    b = B()
    xe1 = b.inp("xe1", [F1_TE, D])
    wqkv = b.inp("wqkv", [D, 1536])
    qsc = b.inp("qsc", [1, 64])
    qsr = b.inp("qsr", [1, 64])
    ksc = b.inp("ksc", [1, 64])
    ksr = b.inp("ksr", [1, 64])
    csq = b.inp("csq", [T, 64])
    ssq = b.inp("ssq", [T, 64])
    csk = b.inp("csk", [F1_NKT * 128, 64])
    ssk = b.inp("ssk", [F1_NKT * 128, 64])
    oqf = b.outp("oqf", [128, 8 * T], BF16)
    okf = b.outp("okf", [128, 8 * F1_NKT * 128], BF16)
    ovtb = b.outp("ovtb", [128, F1_NKT * 520], BF16)

    XT = b.sb("XT", [128, 2 * D])
    HF = b.sb("HF", [128, 8 * 2560], BF16)
    WQ = b.sb("WQ", [128, 8 * 1536], BF16)
    QTR = b.sb("QTR", [128, 2 * 512])
    KTR = b.sb("KTR", [128, 2 * 512])
    VTB = b.sb("VTB", [128, F1_NKT * 520], BF16)
    QF = b.sb("QF", [128, 8 * T], BF16)
    KF = b.sb("KF", [128, 8 * F1_NKT * 128], BF16)
    SSQb = b.sb("SSQb", [128, 512])
    SS = b.sb("SS", [128, 16])
    RR = b.sb("RR", [128, 16])
    TBQ = b.sb("TBQ", [128, 2 * 128])
    TBK = b.sb("TBK", [128, 2 * 128])
    QSB = b.sb("QSB", [128, 64])
    QSRB = b.sb("QSRB", [128, 64])
    KSB = b.sb("KSB", [128, 64])
    KSRB = b.sb("KSRB", [128, 64])
    QROW = b.sb("QROW", [1, 256], BF16)
    CS2 = b.sb("CS2", [128, 2 * 128])
    Q1 = b.sb("Q1", [128, 64])
    R1 = b.sb("R1", [128, 64])
    T2 = b.sb("T2", [128, 64])
    WBB = b.sb("WBB", [128, 2 * 128], BF16)
    PS = [b.ps(f"ps{i}", [128, 512], F32) for i in range(3)]
    PSB = b.ps("psb", [128, 512], F32)

    blk = b.begin()
    sini = b.sem("sini")
    sld = b.sem("sld")
    sw = b.sem("sw")
    stb = b.sem("stb")
    smm = b.sem("smm")
    sev = b.sem("sev")
    s_sq = b.sem("s_sq")
    s_v = b.sem("s_v")
    s_lf = b.sem("s_lf")
    s_rr = b.sem("s_rr")
    svm = b.sem("svm")
    str_ = b.sem("str")
    svo = b.sem("svo")
    sst = b.sem("sst")
    b.sem("cfH_c")

    ident_setup(b, blk, sini)
    xt = xe1.rearrange("(n p) d -> n p d", p=128)
    cqt = csq.rearrange("(n p) d -> n p d", p=128)
    sqt = ssq.rearrange("(n p) d -> n p d", p=128)
    ckt = csk.rearrange("(n p) d -> n p d", p=128)
    skt = ssk.rearrange("(n p) d -> n p d", p=128)

    @blk.sync
    def _(e):
        for ti in range(F1_NKT):
            et = ti + 1
            if ti >= 2:
                e.wait_ge(b.h("cfH_c"), ti - 1)
            b.inc(e.dma_start(out=XT[:, (ti % 2) * D:(ti % 2) * D + 512],
                              in_=xe1[et * 128:(et + 1) * 128, 0:512]), sld, 16)
            b.inc(e.dma_start(out=XT[:, (ti % 2) * D + 512:(ti % 2 + 1) * D],
                              in_=xe1[et * 128 - 1:(et + 1) * 128 - 1, 512:1024]),
                  sld, 16)
        for ti in range(F1_NKT):
            if ti >= 2:
                e.wait_ge(b.h(svm), 9 * (ti - 2) + 1)
            b.inc(e.dma_start(out=TBK[:, (ti % 2) * 128:(ti % 2) * 128 + 64],
                              in_=ckt[ti]), stb, 16)
            b.inc(e.dma_start(out=TBK[:, (ti % 2) * 128 + 64:(ti % 2 + 1) * 128],
                              in_=skt[ti]), stb, 16)
        for qi in range(NT):
            if qi >= 2:
                e.wait_ge(b.h(svm), 9 * (F1_NKT + qi - 2) + 1)
            b.inc(e.dma_start(out=TBQ[:, (qi % 2) * 128:(qi % 2) * 128 + 64],
                              in_=cqt[qi]), stb, 16)
            b.inc(e.dma_start(out=TBQ[:, (qi % 2) * 128 + 64:(qi % 2 + 1) * 128],
                              in_=sqt[qi]), stb, 16)

    @blk.gpsimd
    def _(g):
        g.memset(VTB[:], 1.0)
        g.memset(WBB[:], 0.0)
        b.inc(g.dma_start(out=WQ[:].rearrange("p (c o) -> p c o", o=1536),
                          in_=wqkv.rearrange("(c p) o -> p c o", p=128)), sw, 16)
        b.inc(g.dma_start(out=QROW[0:1, 0:64], in_=qsc), sw, 16)
        b.inc(g.dma_start(out=QROW[0:1, 64:128], in_=qsr), sw, 16)
        b.inc(g.dma_start(out=QROW[0:1, 128:192], in_=ksc), sw, 16)
        b.inc(g.dma_start(out=QROW[0:1, 192:256], in_=ksr), sw, 16)

    sbc = emit_bcast(b, blk, "bcs",
                     [(QROW[0:1, 0:64], QSB[:], 64), (QROW[0:1, 64:128], QSRB[:], 64),
                      (QROW[0:1, 128:192], KSB[:], 64), (QROW[0:1, 192:256], KSRB[:], 64)],
                     PSB, wait_list=[(sw, 80), (sini, None)])

    cast_fm_emit(b, blk, "cfH", F1_NKT, D,
                 lambda i: XT[:, (i % 2) * D:(i % 2 + 1) * D],
                 lambda i: [(sld, 32 * (i + 1))],
                 lambda i, c: HF[:, c * 2560 + i * 128:c * 2560 + (i + 1) * 128],
                 suf := b.sem("suf"))

    kinds = []
    for ti in range(F1_NKT):
        kinds.append(("k", ti))
        kinds.append(("v", ti))
    for qi in range(NT):
        kinds.append(("q", qi))

    def rhs_cols(kind):
        return {"q": 0, "k": 512, "v": 1024}[kind]

    @blk.tensor
    def _(t):
        b.wait(t, suf, None)
        b.wait(t, sw, 80)
        for gi, (kind, idx) in enumerate(kinds):
            ti = idx if kind != "q" else idx + 4
            if gi >= 3:
                t.wait_ge(b.h(sev), gi - 2)
            base = rhs_cols(kind)
            for ci in range(8):
                ins = t.matmul(PS[gi % 3][:, 0:512],
                               HF[:, ci * 2560 + ti * 128:ci * 2560 + (ti + 1) * 128],
                               WQ[:, ci * 1536 + base:ci * 1536 + base + 512],
                               start=(ci == 0), stop=(ci == 7))
            ins.then_inc(b.h(smm), 1)
        b.bump(smm, len(kinds))

    @blk.scalar
    def _(s):
        ntr = 0
        nlf = 0
        for gi, (kind, idx) in enumerate(kinds):
            s.wait_ge(b.h(smm), gi + 1)
            pa = PS[gi % 3][:, 0:512]
            if kind == "v":
                for j in range(8):
                    ins = s.activation(out=VTB[:, idx * 520 + j * 65:idx * 520 + j * 65 + 64],
                                       in_=pa[:, j * 64:(j + 1) * 64], func=AF.Copy)
                ins.then_inc(b.h(sev), 1)
                continue
            buf = QTR if kind == "q" else KTR
            half = idx % 2
            if idx >= 2:
                s.wait_ge(b.h(str_), 16 * 8 * (idx - 1) if kind == "k"
                          else 16 * 8 * (F1_NKT + idx - 1))
            s.activation(out=buf[:, half * 512:(half + 1) * 512], in_=pa,
                         func=AF.Copy).then_inc(b.h(sev), 1)
            s.wait_ge(b.h(sev), gi + 1)
            s.activation(out=SSQb[:], in_=buf[:, half * 512:(half + 1) * 512],
                         func=AF.Square).then_inc(b.h(s_sq), 1)
            midx = idx if kind == "k" else F1_NKT + idx
            s.wait_ge(b.h(s_v), 2 * midx + 2)
            s.activation(out=SS[:, 8:16], in_=SS[:, 0:8],
                         func=AF.Ln).then_inc(b.h(s_lf), 1)
            nlf += 1
            s.wait_ge(b.h(s_lf), nlf)
            s.activation(out=RR[:, 0:8], in_=SS[:, 8:16], func=AF.Exp,
                         scale=-0.5).then_inc(b.h(s_rr), 1)
            for h in range(8):
                s.wait_ge(b.h(svm), 9 * midx + h + 1)
                s.dma_start(out=(KF if kind == "k" else QF)
                            [:, h * (F1_NKT * 128 if kind == "k" else T) + idx * 128:
                             h * (F1_NKT * 128 if kind == "k" else T) + (idx + 1) * 128],
                            in_=WBB[:, (h % 2) * 128:(h % 2 + 1) * 128],
                            transpose=True).then_inc(b.h(str_), 16)
        b.bump(sev, len(kinds))
        b.bump(s_sq, F1_NKT + NT)
        b.bump(s_lf, F1_NKT + NT)
        b.bump(s_rr, F1_NKT + NT)
        b.bump(str_, 16 * 8 * (F1_NKT + NT))

    sfv = b.sem("sfv")

    @blk.vector
    def _(v):
        b.wait(v, sbc, None)
        nv = 0
        nsv = 0
        nfv = 0
        for mi in range(F1_NKT + NT):
            kind = "k" if mi < F1_NKT else "q"
            idx = mi if kind == "k" else mi - F1_NKT
            half = idx % 2
            buf = KTR if kind == "k" else QTR
            tbl = TBK if kind == "k" else TBQ
            sb_c = KSB if kind == "k" else QSB
            sb_r = KSRB if kind == "k" else QSRB
            tile = buf[:, half * 512:(half + 1) * 512]
            v.wait_ge(b.h(s_sq), mi + 1)
            v.reduce_sum(SS[:, 0:8], SSQb[:].rearrange("p (s d) -> p s d", d=64),
                         axis=AX.X).then_inc(b.h(s_v), 1); nv += 1
            v.wait_ge(b.h(s_v), nv)
            v.tensor_scalar_add(SS[:, 0:8], SS[:, 0:8],
                                1e-24).then_inc(b.h(s_v), 1); nv += 1
            v.wait_ge(b.h(stb), 32 * (idx + 1) if kind == "k"
                      else 32 * F1_NKT + 32 * (idx + 1))
            v.tensor_mul(CS2[:, 0:64], tbl[:, half * 128:half * 128 + 64], sb_c[:])
            v.tensor_mul(CS2[:, 64:128], tbl[:, half * 128 + 64:(half + 1) * 128],
                         sb_r[:]).then_inc(b.h(sfv), 1); nfv += 1
            v.wait_ge(b.h(sfv), nfv)
            v.wait_ge(b.h(s_rr), mi + 1)
            for h in range(8):
                hs = tile[:, h * 64:(h + 1) * 64]
                v.tensor_scalar(Q1[:], hs, RR[:, h:h + 1], None,
                                ALU.mult).then_inc(b.h(sfv), 1); nfv += 1
                v.wait_ge(b.h(sfv), nfv)
                v.tensor_scalar_mul(R1[:, 0:32], Q1[:, 32:64], -1.0)
                v.tensor_copy(R1[:, 32:64], Q1[:, 0:32]).then_inc(b.h(sfv), 1); nfv += 1
                v.wait_ge(b.h(sfv), nfv)
                v.tensor_mul(T2[:], R1[:], CS2[:, 64:128])
                v.tensor_mul(Q1[:], Q1[:], CS2[:, 0:64]).then_inc(b.h(sfv), 1); nfv += 1
                v.wait_ge(b.h(sfv), nfv)
                v.tensor_add(Q1[:], Q1[:], T2[:]).then_inc(b.h(sfv), 1); nfv += 1
                v.wait_ge(b.h(sfv), nfv)
                if mi >= 1 or h >= 2:
                    v.wait_ge(b.h(str_), 16 * (8 * mi + h - 1))
                v.tensor_copy(WBB[:, (h % 2) * 128:(h % 2) * 128 + 64],
                              Q1[:]).then_inc(b.h(svm), 1); nsv += 1
            v.engine_nop().then_inc(b.h(svm), 1); nsv += 1
        b.bump(s_v, nv)
        b.bump(svm, nsv)
        b.bump(sfv, nfv)

    @blk.sync
    def _(e):
        b.wait(e, str_, None)
        b.inc(e.dma_start(out=oqf, in_=QF[:]), sst, 16)
        b.inc(e.dma_start(out=okf, in_=KF[:]), sst, 16)
        b.wait(e, sev, None)
        b.inc(e.dma_start(out=ovtb, in_=VTB[:]), sst, 16)
        b.wait(e, sst)
    return b.finish()


def mk_f1a2(phase=2):
    b = B()
    qf = b.inp("qf", [128, 8 * T], BF16)
    kf = b.inp("kf", [128, 8 * F1_NKT * 128], BF16)
    vtb = b.inp("vtb", [128, F1_NKT * 520], BF16)
    masks = b.inp("masks", [8, 128, 512])
    flags = b.inp("flags", [128, 32])
    wo = b.inp("wo", [512, D])
    xown = b.inp("xown", [T, D])
    xout = b.outp("xout", [T, D])

    QF = b.sb("QF", [128, 8 * T], BF16)
    KF = b.sb("KF", [128, 8 * F1_NKT * 128], BF16)
    VTB = b.sb("VTB", [128, F1_NKT * 520], BF16)
    MSK = b.sb("MSK", [128, 8 * 512], F32)
    FLG = b.sb("FLG", [128, 32], F32)
    ETT = b.sb("ETT", [128, 2 * 512], F32)
    ET = b.sb("ET", [128, 8 * 512], BF16)
    OAV = b.sb("OAV", [128, 4 * 512])
    AVB = b.sb("AVB", [128, 2 * 512], BF16)
    AVFM = b.sb("AVFM", [128, 4 * T], BF16)
    WOC = b.sb("WOC", [128, 4 * D], BF16)
    XO = b.sb("XO", [128, 2 * D])
    OX = b.sb("OX", [128, 2 * D])
    LNC = b.sb("LNC", [128, 1])
    RIV = b.sb("RIV", [128, 1])
    PS = [b.ps(f"ps{i}", [128, 512], F32) for i in range(2)]
    PA = [b.ps(f"pa{i}", [128, 512], F32) for i in range(2)]
    PW = [b.ps(f"pw{i}", [128, 512], F32) for i in range(2)]

    blk = b.begin()
    sini = b.sem("sini")
    sld = b.sem("sld")
    sw = b.sem("sw")
    smm = b.sem("smm")
    sxe = b.sem("sxe")
    sxa = b.sem("sxa")
    sav = b.sem("sav")
    sfl = b.sem("sfl")
    srd = b.sem("srd")
    scv = b.sem("scv")
    satr = b.sem("satr")
    sxo = b.sem("sxo")
    swo = b.sem("swo")
    sst = b.sem("sst")

    ident_setup(b, blk, sini)

    @blk.sync
    def _(e):
        b.inc(e.dma_start(out=QF[:], in_=qf), sld, 16)
        b.inc(e.dma_start(out=KF[:], in_=kf), sld, 16)
        b.inc(e.dma_start(out=VTB[:], in_=vtb), sld, 16)
        b.inc(e.dma_start(out=MSK[:].rearrange("p (r c) -> p r c", c=512),
                          in_=masks.rearrange("r p c -> p r c")), sld, 16)
        b.inc(e.dma_start(out=FLG[:], in_=flags), sld, 16)

    @blk.gpsimd
    def _(g):
        b.inc(g.dma_start(out=WOC[:].rearrange("p (c o) -> p c o", o=D),
                          in_=wo.rearrange("(c p) o -> p c o", p=128)), sw, 16)

    @blk.tensor
    def _(t):
        b.wait(t, sld, 80)
        for w in range(4):
            for h in range(8):
                sidx = w * 8 + h
                if h >= 1 or w >= 1:
                    t.wait_ge(b.h(sav), 4 * sidx)
                for r in range(8):
                    gsc = 8 * sidx + r
                    if gsc >= 2:
                        t.wait_ge(b.h(sxa), gsc - 1)
                    t.matmul(PS[r % 2][:, 0:512],
                             KF[:, h * F1_NKT * 128 + (4 * w + r) * 128:
                                h * F1_NKT * 128 + (4 * w + r + 1) * 128],
                             QF[:, h * T + w * 512:h * T + (w + 1) * 512],
                             start=True, stop=True).then_inc(b.h(smm), 1)
                t.wait_ge(b.h(sxe), 8 * (sidx + 1))
                for qt in range(4):
                    aidx = 4 * sidx + qt
                    if aidx >= 2:
                        t.wait_ge(b.h(srd), aidx - 1)
                    for r in range(8):
                        ins = t.matmul(PA[qt % 2][:, 0:65],
                                       ET[:, r * 512 + qt * 128:r * 512 + (qt + 1) * 128],
                                       VTB[:, (4 * w + r) * 520 + h * 65:
                                           (4 * w + r) * 520 + (h + 1) * 65],
                                       start=(r == 0), stop=(r == 7))
                    ins.then_inc(b.h(sav), 1)
        b.bump(smm, 256)
        b.bump(sav, 128)

    @blk.scalar
    def _(s):
        b.wait(s, sld, 80)
        for w in range(4):
            for h in range(8):
                sidx = w * 8 + h
                for r in range(8):
                    gsc = 8 * sidx + r
                    s.wait_ge(b.h(sxa), gsc + 1)
                    if r == 0 and sidx >= 1:
                        s.wait_ge(b.h(sav), 4 * sidx)
                    s.activation(out=ET[:, r * 512:(r + 1) * 512],
                                 in_=ETT[:, (r % 2) * 512:(r % 2 + 1) * 512],
                                 func=AF.Exp, scale=8.0,
                                 bias=FLG[:, w * 8 + r:w * 8 + r + 1]).then_inc(b.h(sxe), 1)
                for qt in range(4):
                    aidx = 4 * sidx + qt
                    s.wait_ge(b.h(sav), aidx + 1)
                    if aidx >= 1:
                        s.wait_ge(b.h(srd), aidx)
                    s.activation(out=LNC[:, 0:1], in_=PA[qt % 2][:, 64:65],
                                 func=AF.Ln).then_inc(b.h(sfl), 1)
                    s.wait_ge(b.h(sfl), 2 * aidx + 1)
                    s.activation(out=RIV[:, 0:1], in_=LNC[:, 0:1], func=AF.Exp,
                                 scale=-1.0).then_inc(b.h(sfl), 1)
            for qt in range(4):
                ti = 4 * w + qt
                s.wait_ge(b.h(scv), ti + 1)
                for c in range(4):
                    s.dma_start(out=AVFM[:, c * T + ti * 128:c * T + (ti + 1) * 128],
                                in_=AVB[:, (qt % 2) * 512 + c * 128:
                                        (qt % 2) * 512 + (c + 1) * 128],
                                transpose=True).then_inc(b.h(satr), 16)
        b.bump(sxe, 256)
        b.bump(sfl, 256)
        b.bump(satr, 16 * 4 * 16)

    @blk.vector
    def _(v):
        b.wait(v, sld, 80)
        for w in range(4):
            for h in range(8):
                sidx = w * 8 + h
                for r in range(8):
                    gsc = 8 * sidx + r
                    v.wait_ge(b.h(smm), gsc + 1)
                    if gsc >= 2:
                        v.wait_ge(b.h(sxe), gsc - 1)
                    v.tensor_add(ETT[:, (r % 2) * 512:(r % 2 + 1) * 512],
                                 PS[r % 2][:, 0:512],
                                 MSK[:, r * 512:(r + 1) * 512]).then_inc(b.h(sxa), 1)
                for qt in range(4):
                    aidx = 4 * sidx + qt
                    v.wait_ge(b.h(sfl), 2 * aidx + 2)
                    if w >= 1:
                        v.wait_ge(b.h(satr), 64 * 4 * w)
                    v.tensor_scalar(OAV[:, qt * 512 + h * 64:qt * 512 + (h + 1) * 64],
                                    PA[qt % 2][:, 0:64], RIV[:, 0:1], None,
                                    ALU.mult).then_inc(b.h(srd), 1)
            for qt in range(4):
                ti = 4 * w + qt
                if ti >= 2:
                    v.wait_ge(b.h(satr), 64 * (ti - 1))
                v.tensor_copy(AVB[:, (qt % 2) * 512:(qt % 2 + 1) * 512],
                              OAV[:, qt * 512:(qt + 1) * 512]).then_inc(b.h(scv), 1)
        b.bump(sxa, 256)
        b.bump(srd, 128)
        b.bump(scv, 16)

    if phase == 1:
        odbg = b.outp("dbg", [128, 4 * T], BF16)

        @blk.sync
        def _(e):
            b.wait(e, satr, None)
            b.inc(e.dma_start(out=odbg, in_=AVFM[:]), sst, 16)
            b.wait(e, sst)
        return b.finish()

    g2 = []
    for tt in range(NT):
        for og in range(2):
            mms = [(AVFM[:, ci * T + tt * 128:ci * T + (tt + 1) * 128],
                    WOC[:, ci * D + og * 512:ci * D + (og + 1) * 512])
                   for ci in range(4)]

            def ev(e, pa, tt=tt, og=og):
                e.wait_ge(b.h(sxo), 16 * (tt + 1))
                if tt >= 2:
                    e.wait_ge(b.h(sst), 16 * (tt - 1))
                return e.tensor_add(OX[:, (tt % 2) * D + og * 512:(tt % 2) * D + (og + 1) * 512],
                                    pa, XO[:, (tt % 2) * D + og * 512:(tt % 2) * D + (og + 1) * 512])
            g2.append(dict(mms=mms, pslice=lambda p: p[:, 0:512],
                           evac_eng="vector", evac=ev,
                           waits=[("satr", 64 * (tt + 1))]))
    gemm(b, blk, "wog", g2, [PW[0], PW[1]], wait_list=[(sw, 16)], done=swo)

    xot_ = xown.rearrange("(n p) d -> n p d", p=128)
    xou = xout.rearrange("(n p) d -> n p d", p=128)

    @blk.sync
    def _(e):
        for tt in range(NT):
            if tt >= 2:
                e.wait_ge(b.h(swo), 2 * (tt - 1))
            b.inc(e.dma_start(out=XO[:, (tt % 2) * D:(tt % 2 + 1) * D], in_=xot_[tt]),
                  sxo, 16)
            if tt >= 2:
                b.inc(e.dma_start(out=xou[tt - 2],
                                  in_=OX[:, ((tt - 2) % 2) * D:((tt - 2) % 2 + 1) * D]),
                      sst, 16)
        for tt in range(NT - 2, NT):
            e.wait_ge(b.h(swo), 2 * (tt + 1))
            b.inc(e.dma_start(out=xou[tt], in_=OX[:, (tt % 2) * D:(tt % 2 + 1) * D]),
                  sst, 16)
        b.wait(e, sst)
    return b.finish()


def mk_f1b():
    b = B()
    xe2 = b.inp("xe2", [T + 1, D])
    gains = b.inp("gains", [1, 2 * D])
    w1p = b.inp("w1p", [D, 5632])
    b1c = b.inp("b1c", [128, 44])
    w2p = b.inp("w2p", [2816, D])
    b2 = b.inp("b2", [D])
    xown = b.inp("xown", [T, D])
    xout = b.outp("xout", [T, D])

    XT = b.sb("XT", [128, 2 * D])
    HF = b.sb("HF", [128, 8 * T], BF16)
    GR = b.sb("GR", [1, 2 * D], BF16)
    GB = b.sb("GB", [128, 2 * D])
    W1S = b.sb("W1S", [128, 4 * 8 * 128], BF16)
    W2C = b.sb("W2C", [128, 22 * D], BF16)
    B1C = b.sb("B1C", [128, 44], F32)
    B2R = b.sb("B2R", [1, D], BF16)
    AH = b.sb("AH", [128, 22 * 512], BF16)
    GT = b.sb("GT", [128, 2 * 512], BF16)
    HH = b.sb("HH", [128, 22 * 512], BF16)
    XO = b.sb("XO", [128, 2 * D])
    OX = b.sb("OX", [128, 2 * D])
    PS = [b.ps(f"ps{i}", [128, 512], F32) for i in range(3)]
    PSB = b.ps("psb", [128, 512], F32)

    blk = b.begin()
    sini = b.sem("sini")
    sld = b.sem("sld")
    sw = b.sem("sw")
    sw1 = b.sem("sw1")
    smm = b.sem("smm")
    s_sc = b.sem("s_sc")
    s_ve = b.sem("s_ve")
    shh = b.sem("shh")
    sxo = b.sem("sxo")
    sst = b.sem("sst")
    sgr = b.sem("sgr")
    b.sem("lnH_a")

    ident_setup(b, blk, sini)
    xot_ = xown.rearrange("(n p) d -> n p d", p=128)
    xou = xout.rearrange("(n p) d -> n p d", p=128)

    @blk.sync
    def _(e):
        for i in range(NT):
            if i >= 2:
                e.wait_ge(b.h("lnH_a"), i - 1)
            b.inc(e.dma_start(out=XT[:, (i % 2) * D:(i % 2) * D + 512],
                              in_=xe2[1 + i * 128:1 + (i + 1) * 128, 0:512]), sld, 16)
            b.inc(e.dma_start(out=XT[:, (i % 2) * D + 512:(i % 2 + 1) * D],
                              in_=xe2[i * 128:(i + 1) * 128, 512:1024]),
                  sld, 16)

    @blk.gpsimd
    def _(g):
        b.inc(g.dma_start(out=GR[0:1, :], in_=gains), sgr, 16)
        b.inc(g.dma_start(out=B1C[:], in_=b1c), sw, 16)
        b.inc(g.dma_start(out=B2R[0:1, :], in_=b2[None, :]), sw, 16)
        b.inc(g.dma_start(out=W2C[:].rearrange("p (c o) -> p c o", o=D),
                          in_=w2p.rearrange("(c p) o -> p c o", p=128)), sw, 16)
        w1r = w1p.rearrange("(c p) j -> p c j", p=128)
        for tg in range(4):
            for jc in range(44):
                widx = tg * 44 + jc
                if widx >= 4:
                    pw = widx - 4
                    g.wait_ge(b.h(smm), (pw // 44) * 52 + (pw % 44) + 1)
                b.inc(g.dma_start(out=W1S[:, (widx % 4) * 1024:(widx % 4 + 1) * 1024]
                                  .rearrange("p (c j) -> p c j", j=128),
                                  in_=w1r[:, :, jc * 128:(jc + 1) * 128]), sw1, 16)

    sgb = emit_bcast(b, blk, "gb",
                     [(GR[0:1, i * 512:(i + 1) * 512], GB[:, i * 512:(i + 1) * 512], 512)
                      for i in range(4)], PSB, wait_list=[(sgr, 16), (sini, None)])

    lnr = ln_fm_emit(b, blk, "lnH", NT, D,
                     lambda i: XT[:, (i % 2) * D:(i % 2 + 1) * D],
                     lambda i: [(sld, 32 * (i + 1)), (sgb, 4)],
                     lambda i, c: HF[:, c * T + i * 128:c * T + (i + 1) * 128],
                     shf := b.sem("shf"),
                     gamma_b=GB[:, 0:D], beta_b=GB[:, D:2 * D])

    plan = []
    for tg in range(4):
        for jc in range(44):
            plan.append(("g1", tg, jc))
        for tt in range(4):
            for og in range(2):
                plan.append(("w2", tg, (tt, og)))
    eng_of = {}
    ords = {"scalar": 0, "vector": 0}
    for gi, (kind, tg, aux) in enumerate(plan):
        eng = "scalar" if kind == "g1" else "vector"
        eng_of[gi] = (eng, ords[eng])
        ords[eng] += 1

    @blk.tensor
    def _(t):
        b.wait(t, shf, None)
        b.wait(t, sw, 48)
        for gi, (kind, tg, aux) in enumerate(plan):
            if gi >= 3:
                peng, pord = eng_of[gi - 3]
                t.wait_ge(b.h(s_sc if peng == "scalar" else s_ve), pord + 1)
            if kind == "g1":
                jc = aux
                widx = tg * 44 + jc
                t.wait_ge(b.h(sw1), 16 * (widx + 1))
                for ci in range(8):
                    ins = t.matmul(PS[gi % 3][:, 0:512],
                                   W1S[:, (widx % 4) * 1024 + ci * 128:
                                       (widx % 4) * 1024 + (ci + 1) * 128],
                                   HF[:, ci * T + tg * 512:ci * T + (tg + 1) * 512],
                                   start=(ci == 0), stop=(ci == 7))
                ins.then_inc(b.h(smm), 1)
            else:
                tt, og = aux
                t.wait_ge(b.h(shh), 22 * (tg + 1))
                mms = [(HH[:, jc2 * 512 + tt * 128:jc2 * 512 + (tt + 1) * 128],
                        W2C[:, jc2 * D + og * 512:jc2 * D + (og + 1) * 512])
                       for jc2 in range(22)]
                mms.append((b.ones_row[0:1, 0:128], B2R[0:1, og * 512:(og + 1) * 512]))
                for mi, (lh, rh) in enumerate(mms):
                    ins = t.matmul(PS[gi % 3][:, 0:512], lh, rh,
                                   start=(mi == 0), stop=(mi == len(mms) - 1))
                ins.then_inc(b.h(smm), 1)
        b.bump(smm, len(plan))

    @blk.scalar
    def _(s):
        b.wait(s, sw, 48)
        nsc = 0
        for gi, (kind, tg, aux) in enumerate(plan):
            if kind != "g1":
                continue
            jc = aux
            s.wait_ge(b.h(smm), gi + 1)
            if jc < 22:
                if tg >= 1:
                    s.wait_ge(b.h(s_ve), 8 * tg)
                s.activation(out=AH[:, jc * 512:(jc + 1) * 512],
                             in_=PS[gi % 3][:, 0:512], func=AF.Identity,
                             bias=B1C[:, jc:jc + 1]).then_inc(b.h(s_sc), 1)
            else:
                gj = jc - 22
                if tg >= 1 or gj >= 2:
                    s.wait_ge(b.h(shh), 22 * tg + gj - 1)
                s.activation(out=GT[:, (gj % 2) * 512:(gj % 2 + 1) * 512],
                             in_=PS[gi % 3][:, 0:512], func=AF.Gelu,
                             bias=B1C[:, jc:jc + 1]).then_inc(b.h(s_sc), 1)
            nsc += 1
        b.bump(s_sc, nsc)

    @blk.vector
    def _(v):
        nve = 0
        for tg in range(4):
            base_sc = tg * 44
            for gj in range(22):
                v.wait_ge(b.h(s_sc), base_sc + 22 + gj + 1)
                v.tensor_mul(HH[:, gj * 512:(gj + 1) * 512],
                             AH[:, gj * 512:(gj + 1) * 512],
                             GT[:, (gj % 2) * 512:(gj % 2 + 1) * 512]).then_inc(b.h(shh), 1)
            for tt in range(4):
                for og in range(2):
                    gi = tg * 52 + 44 + tt * 2 + og
                    gt = tg * 4 + tt
                    v.wait_ge(b.h(smm), gi + 1)
                    v.wait_ge(b.h(sxo), 16 * (gt + 1))
                    if gt >= 2:
                        v.wait_ge(b.h(sst), 16 * (gt - 1))
                    v.tensor_add(OX[:, (gt % 2) * D + og * 512:(gt % 2) * D + (og + 1) * 512],
                                 PS[gi % 3][:, 0:512],
                                 XO[:, (gt % 2) * D + og * 512:(gt % 2) * D + (og + 1) * 512]
                                 ).then_inc(b.h(s_ve), 1)
                    nve += 1
        b.bump(shh, 88)
        b.bump(s_ve, nve)

    @blk.sync
    def _(e):
        for gt in range(NT):
            if gt >= 2:
                e.wait_ge(b.h(s_ve), 2 * (gt - 1))
            b.inc(e.dma_start(out=XO[:, (gt % 2) * D:(gt % 2 + 1) * D], in_=xot_[gt]),
                  sxo, 16)
            if gt >= 2:
                b.inc(e.dma_start(out=xou[gt - 2],
                                  in_=OX[:, ((gt - 2) % 2) * D:((gt - 2) % 2 + 1) * D]),
                      sst, 16)
        for gt in range(NT - 2, NT):
            e.wait_ge(b.h(s_ve), 2 * (gt + 1))
            b.inc(e.dma_start(out=xou[gt], in_=OX[:, (gt % 2) * D:(gt % 2 + 1) * D]),
                  sst, 16)
        b.wait(e, sst)
    return b.finish()


def f1_shard_ext(xf, core):
    bidx, qc = core // 4, core % 4
    s = qc * T
    out = np.zeros((F1_TE, D), np.float32)
    lo = max(0, s - 640)
    out[640 - (s - lo):] = xf.reshape(2, 8192, D)[bidx, lo:s + T]
    return out


def f1_shard_e2(xf, core):
    bidx, qc = core // 4, core % 4
    s = qc * T
    out = np.zeros((T + 1, D), np.float32)
    lo = max(0, s - 1)
    out[1 - (s - lo):] = xf.reshape(2, 8192, D)[bidx, lo:s + T]
    return out


def run_f1(z, x3):
    ncp = mk_preln()
    imp = [dict(xin=shard_rows(x3, c)) for c in range(NCORES)]
    rp = run_bass_kernel_spmd(ncp, imp, core_ids=list(range(NCORES))).results
    xf = unshard([rp[c]["uout"] for c in range(NCORES)])

    nc1 = mk_f1a1()
    nc2 = mk_f1a2()
    nc3 = mk_f1b()
    msk = f1_masks_np()
    tabs = [f1_tables_np(c) for c in range(NCORES)]
    for layer in range(2):
        qs = z["L1_ffn_q_scale"][layer]
        ks = z["L1_ffn_k_scale"][layer]
        qsr = np.ascontiguousarray(np.concatenate([qs[32:], qs[:32]])[None, :])
        ksr = np.ascontiguousarray(np.concatenate([ks[32:], ks[:32]])[None, :])
        im1 = [dict(xe1=f1_shard_ext(xf, c), wqkv=z["L1_ffn_wqkv"][layer],
                    qsc=qs[None, :], qsr=qsr, ksc=ks[None, :], ksr=ksr,
                    csq=tabs[c][0], ssq=tabs[c][1], csk=tabs[c][2], ssk=tabs[c][3])
               for c in range(NCORES)]
        r1 = run_bass_kernel_spmd(nc1, im1, core_ids=list(range(NCORES))).results

        im2 = [dict(qf=r1[c]["oqf"], kf=r1[c]["okf"], vtb=r1[c]["ovtb"],
                    masks=msk, flags=f1_flags_np(c), wo=z["L1_ffn_wo"][layer],
                    xown=shard_rows(xf, c)) for c in range(NCORES)]
        r2 = run_bass_kernel_spmd(nc2, im2, core_ids=list(range(NCORES))).results
        xf = unshard([r2[c]["xout"] for c in range(NCORES)])

        w1 = z["L1_ffn_w1"][layer]
        w1p = np.zeros((D, 5632), np.float32)
        w1p[:, 0:2730] = w1[:, 0:2730]
        w1p[:, 2816:2816 + 2730] = w1[:, 2730:5460]
        b1 = z["L1_ffn_b1"][layer]
        b1p = np.zeros(5632, np.float32)
        b1p[0:2730] = b1[0:2730]
        b1p[2816:2816 + 2730] = b1[2730:5460]
        b1c = np.ascontiguousarray(b1p.reshape(44, 128).T)
        w2p = np.zeros((2816, D), np.float32)
        w2p[0:2730] = z["L1_ffn_w2"][layer]
        gains = np.ascontiguousarray(
            np.concatenate([z["L1_ffn_ln_g"][layer], z["L1_ffn_ln_b"][layer]])[None, :])
        im3 = [dict(xe2=f1_shard_e2(xf, c), gains=gains, w1p=w1p, b1c=b1c,
                    w2p=w2p, b2=z["L1_ffn_b2"][layer], xown=shard_rows(xf, c))
               for c in range(NCORES)]
        r3 = run_bass_kernel_spmd(nc3, im3, core_ids=list(range(NCORES))).results
        xf = unshard([r3[c]["xout"] for c in range(NCORES)])

    ncl = mk_postln()
    iml = [dict(o=shard_rows(xf, c), xprev=shard_rows(x3, c)) for c in range(NCORES)]
    rl = run_bass_kernel_spmd(ncl, iml, core_ids=list(range(NCORES))).results
    return unshard([rl[c]["xout"] for c in range(NCORES)])


A2_DN = 64.0 ** (-0.25)
A2_EPS = 0.001


def mk_a2k():
    b = B()
    uin = b.inp("uin", [T, D])
    wkT = b.inp("wkT", [D, D])
    bkr = b.inp("bkr", [D])
    wvT = b.inp("wvT", [D, D])
    bvr = b.inp("bvr", [D])
    okfm = b.outp("okfm", [128, 8 * T], BF16)
    odiag = b.outp("odiag", [128, NT * 16])
    omax = b.outp("omax", [1, 1])
    ovt = b.outp("ovt", [NT, 128, 16 * 65], BF16)
    oscr = b.outp("oscr", [128])

    UR = b.sb("UR", [128, 2 * D])
    UF = b.sb("UF", [128, 8 * T], BF16)
    WKT = b.sb("WKT", [128, 8 * D], BF16)
    WVT = b.sb("WVT", [128, 8 * D], BF16)
    BKR = b.sb("BKR", [1, D], BF16)
    BVR = b.sb("BVR", [1, D], BF16)
    KT = b.sb("KT", [128, NT * D])
    KFMB = b.sb("KFMB", [128, 8 * T], BF16)
    VT = b.sb("VT", [128, 2 * 16 * 65], BF16)
    SQ = b.sb("SQ", [128, D])
    DIAG = b.sb("DIAG", [128, NT * 16])
    MXT = b.sb("MXT", [128, NT])
    MAXP = b.sb("MAXP", [128, 1])
    MROW = b.sb("MROW", [1, 128])
    OMX = b.sb("OMX", [1, 1])
    PS = [b.ps(f"ps{i}", [128, 512], F32) for i in range(3)]

    blk = b.begin()
    sini = b.sem("sini")
    sld = b.sem("sld")
    sw = b.sem("sw")
    skv = b.sem("skv")
    s_kq = b.sem("s_kq")
    s_v = b.sem("s_v")
    s_dg = b.sem("s_dg")
    svo = b.sem("svo")
    smx1 = b.sem("smx1")
    smx2 = b.sem("smx2")
    sst = b.sem("sst")
    b.sem("cfU_c")

    ident_setup(b, blk, sini)
    uit = uin.rearrange("(n p) d -> n p d", p=128)

    @blk.sync
    def _(e):
        for i in range(NT):
            if i >= 2:
                e.wait_ge(b.h("cfU_c"), i - 1)
            b.inc(e.dma_start(out=UR[:, (i % 2) * D:(i % 2 + 1) * D], in_=uit[i]),
                  sld, 16)

    @blk.gpsimd
    def _(g):
        g.memset(VT[:], 1.0)
        b.inc(g.dma_start(out=WKT[:].rearrange("p (c o) -> p c o", o=D),
                          in_=wkT.rearrange("(c p) o -> p c o", p=128)), sw, 16)
        b.inc(g.dma_start(out=WVT[:].rearrange("p (c o) -> p c o", o=D),
                          in_=wvT.rearrange("(c p) o -> p c o", p=128)), sw, 16)
        b.inc(g.dma_start(out=BKR[0:1, :], in_=bkr[None, :]), sw, 16)
        b.inc(g.dma_start(out=BVR[0:1, :], in_=bvr[None, :]), sw, 16)

    cast_fm_emit(b, blk, "cfU", NT, D,
                 lambda i: UR[:, (i % 2) * D:(i % 2 + 1) * D],
                 lambda i: [(sld, 16 * (i + 1))],
                 lambda i, c: UF[:, c * T + i * 128:c * T + (i + 1) * 128], suf := b.sem("suf"))

    groups = []
    for tt in range(NT):
        for og in range(2):
            mms = [(UF[:, ci * T + tt * 128:(tt + 1) * 128 + ci * T],
                    WKT[:, ci * D + og * 512:ci * D + (og + 1) * 512])
                   for ci in range(8)]
            mms.append((b.ones_row[0:1, 0:128], BKR[0:1, og * 512:(og + 1) * 512]))

            def evk(e, pa, tt=tt, og=og):
                return e.tensor_copy(KT[:, tt * D + og * 512:tt * D + (og + 1) * 512], pa)
            groups.append(dict(mms=mms, pslice=lambda p: p[:, 0:512],
                               evac_eng="vector", evac=evk))
        for og in range(2):
            mms = [(UF[:, ci * T + tt * 128:(tt + 1) * 128 + ci * T],
                    WVT[:, ci * D + og * 512:ci * D + (og + 1) * 512])
                   for ci in range(8)]
            mms.append((b.ones_row[0:1, 0:128], BVR[0:1, og * 512:(og + 1) * 512]))

            def evv(e, pa, tt=tt, og=og):
                for j in range(8):
                    h = og * 8 + j
                    ins = e.tensor_copy(
                        VT[:, (tt % 2) * 1040 + h * 65:(tt % 2) * 1040 + h * 65 + 64],
                        pa[:, j * 64:(j + 1) * 64])
                return ins
            wts = [("svo", 16 * (tt - 1))] if tt >= 2 else []
            groups.append(dict(mms=mms, pslice=lambda p: p[:, 0:512],
                               evac_eng="vector", evac=evv, waits=wts))
    gemm(b, blk, "kv", groups, PS, wait_list=[(suf, None), (sw, 64), (sini, None)],
         done=skv)

    @blk.scalar
    def _(s):
        for i in range(NT):
            s.wait_ge(b.h(skv), 4 * i + 2)
            if i >= 1:
                s.wait_ge(b.h(s_v), 2 * i)
            s.activation(out=SQ[:, 0:512], in_=KT[:, i * D:i * D + 512],
                         func=AF.Square).then_inc(b.h(s_kq), 1)
            s.wait_ge(b.h(s_v), 2 * i + 1)
            s.activation(out=SQ[:, 512:1024], in_=KT[:, i * D + 512:(i + 1) * D],
                         func=AF.Square).then_inc(b.h(s_kq), 1)
        b.bump(s_kq, 2 * NT)

    @blk.vector
    def _(v):
        nv = 0
        for i in range(NT):
            v.wait_ge(b.h(s_kq), 2 * i + 1)
            v.reduce_sum(DIAG[:, i * 16:i * 16 + 8],
                         SQ[:, 0:512].rearrange("p (s d) -> p s d", d=64),
                         axis=AX.X).then_inc(b.h(s_v), 1); nv += 1
            v.wait_ge(b.h(s_kq), 2 * i + 2)
            v.reduce_sum(DIAG[:, i * 16 + 8:(i + 1) * 16],
                         SQ[:, 512:1024].rearrange("p (s d) -> p s d", d=64),
                         axis=AX.X).then_inc(b.h(s_v), 1); nv += 1
            v.wait_ge(b.h(s_v), nv)
            v.tensor_scalar_mul(DIAG[:, i * 16:(i + 1) * 16],
                                DIAG[:, i * 16:(i + 1) * 16],
                                -0.5).then_inc(b.h(s_dg), 1)
            v.wait_ge(b.h(s_dg), i + 1)
            v.reduce_max(MXT[:, i:i + 1],
                         DIAG[:, i * 16:(i + 1) * 16].rearrange("p (s c) -> p s c", s=1),
                         axis=AX.X).then_inc(b.h(s_v), 1); nv += 1
        v.wait_ge(b.h(s_v), nv)
        v.reduce_max(MAXP[:, 0:1], MXT[:].rearrange("p (s c) -> p s c", s=1),
                     axis=AX.X).then_inc(b.h(s_v), 1); nv += 1
        v.wait_ge(b.h(smx2), 16)
        v.reduce_max(OMX[0:1, 0:1], MROW[0:1, :].rearrange("p (s c) -> p s c", s=1),
                     axis=AX.X).then_inc(b.h(s_v), 1); nv += 1
        b.bump(s_v, nv)
        b.bump(s_dg, NT)

    cast_fm_emit(b, blk, "cfK", NT, D,
                 lambda i: KT[:, i * D:(i + 1) * D],
                 lambda i: [(skv, 4 * i + 2)],
                 lambda i, c: KFMB[:, c * T + i * 128:c * T + (i + 1) * 128],
                 skf := b.sem("skf"))

    vtt = ovt

    @blk.sync
    def _(e):
        for tt in range(NT):
            b.wait(e, skv, 4 * (tt + 1))
            b.inc(e.dma_start(out=vtt[tt],
                              in_=VT[:, (tt % 2) * 1040:(tt % 2 + 1) * 1040]), svo, 16)
        SVN = b.val(s_v)
        e.wait_ge(b.h(s_v), SVN - 1)
        b.inc(e.dma_start(out=oscr[:, None], in_=MAXP[:]), smx1, 16)
        e.wait_ge(b.h(smx1), 16)
        b.inc(e.dma_start(out=MROW[0:1, :], in_=oscr[None, :]), smx2, 16)
        e.wait_ge(b.h(s_v), SVN)
        b.inc(e.dma_start(out=omax, in_=OMX[0:1, 0:1]), sst, 16)
        b.wait(e, s_dg, NT)
        b.inc(e.dma_start(out=odiag, in_=DIAG[:]), sst, 16)
        b.wait(e, skf)
        b.inc(e.dma_start(out=okfm, in_=KFMB[:]), sst, 16)
        b.wait(e, sst)
    return b.finish()


def mk_a2c():
    b = B()
    kfm = b.inp("kfm", [128, 8 * T], BF16)
    vti = b.inp("vti", [NT, 128, 16 * 65], BF16)
    diag = b.inp("diag", [128, NT * 16])
    mst = b.inp("mst", [1, 512])
    projc = b.inp("projc", [64, 256])
    octx = b.outp("octx", [32, 128, 65])

    KFM = b.sb("KFM", [128, 8 * T], BF16)
    VT = b.sb("VT", [128, 2 * 1040], BF16)
    KF = b.sb("KF", [128, 2 * 16 * 256], BF16)
    PJ2 = b.sb("PJ2", [128, 512], BF16)
    DIAG = b.sb("DIAG", [128, 256])
    DGS = b.sb("DGS", [128, 256])
    MAXC = b.sb("MAXC", [128, 1])
    MST = b.sb("MST", [1, 512], BF16)
    CTXE = b.sb("CTXE", [128, 32 * 65])
    PSD = [b.ps(f"psd{i}", [128, 512], F32) for i in range(2)]
    PCT = [b.ps(f"pct{i}", [128, 512], F32) for i in range(5)]

    blk = b.begin()
    sini = b.sem("sini")
    sld = b.sem("sld")
    spj = b.sem("spj")
    sbm = b.sem("sbm")
    svf = b.sem("svf")
    sdgs = b.sem("sdgs")
    sda = b.sem("sda")
    sexa = b.sem("sexa")
    sexp = b.sem("sexp")
    sctx = b.sem("sctx")
    svt = b.sem("svt")
    sev = b.sem("sev")
    sst = b.sem("sst")

    ident_setup(b, blk, sini)

    @blk.sync
    def _(e):
        b.inc(e.dma_start(out=KFM[:], in_=kfm), sld, 16)
        b.inc(e.dma_start(out=DIAG[:], in_=diag), sld, 16)

    @blk.gpsimd
    def _(g):
        b.inc(g.dma_start(out=MST[0:1, :], in_=mst), sld, 16)
        g.memset(PJ2[:], 0.0)
        b.inc(g.dma_start(out=PJ2[0:64, 0:256], in_=projc), spj, 16)
        b.inc(g.dma_start(out=PJ2[64:128, 256:512], in_=projc), spj, 16)
        for i in range(NT):
            if i >= 2:
                g.wait_ge(b.h(sctx), 16 * (i - 1))
            b.inc(g.dma_start(out=VT[:, (i % 2) * 1040:(i % 2 + 1) * 1040],
                              in_=vti[i]), svt, 16)

    @blk.vector
    def _(v):
        v.wait_ge(b.h(sbm), 1)
        v.reduce_max(MAXC[:, 0:1], PSD[0][:, 0:512].rearrange("p (s c) -> p s c", s=1),
                     axis=AX.X).then_inc(b.h(svf), 1)
        v.wait_ge(b.h(svf), 1)
        v.tensor_scalar(DGS[:], DIAG[:], MAXC[:, 0:1], None,
                        ALU.subtract).then_inc(b.h(sdgs), 1)
        b.bump(svf, 1)
        b.bump(sdgs, 1)

    @blk.tensor
    def _(t):
        b.wait(t, sini)
        b.wait(t, sld, 48)
        t.matmul(PSD[0][:, 0:512], b.ones_row[0:1, 0:128], MST[0:1, 0:512],
                 start=True, stop=True).then_inc(b.h(sbm), 1)
        t.wait_ge(b.h(svf), 1)
        t.wait_ge(b.h(spj), 32)
        for i in range(NT):
            for h in range(16):
                idx = 16 * i + h
                if idx >= 2:
                    t.wait_ge(b.h(sexa), idx - 1)
                t.matmul(PSD[h % 2][:, 0:256],
                         KFM[:, (h // 2) * T + i * 128:(h // 2) * T + (i + 1) * 128],
                         PJ2[:, (h % 2) * 256:(h % 2 + 1) * 256],
                         start=True, stop=True).then_inc(b.h(sda), 1)
            t.wait_ge(b.h(svt), 16 * (i + 1))
            for h in range(16):
                idx = 16 * i + h
                t.wait_ge(b.h(sexp), idx + 1)
                for mc in range(2):
                    g = h * 2 + mc
                    ins = t.matmul(
                        PCT[g // 7][:, (g % 7) * 65:(g % 7) * 65 + 65],
                        KF[:, (i % 2) * 4096 + h * 256 + mc * 128:
                           (i % 2) * 4096 + h * 256 + (mc + 1) * 128],
                        VT[:, (i % 2) * 1040 + h * 65:(i % 2) * 1040 + (h + 1) * 65],
                        start=(i == 0), stop=(i == NT - 1), skip_group_check=True)
                ins.then_inc(b.h(sctx), 1)
        b.bump(sbm, 1)
        b.bump(sda, 256)
        b.bump(sctx, 256)

    @blk.scalar
    def _(s):
        s.wait_ge(b.h(sdgs), 1)
        for i in range(NT):
            for h in range(16):
                idx = 16 * i + h
                s.wait_ge(b.h(sda), idx + 1)
                if i >= 2:
                    s.wait_ge(b.h(sctx), 16 * (i - 1))
                s.activation(out=KF[:, (i % 2) * 4096 + h * 256:
                                    (i % 2) * 4096 + (h + 1) * 256],
                             in_=PSD[h % 2][:, 0:256], func=AF.Exp,
                             scale=A2_DN,
                             bias=DGS[:, i * 16 + h:i * 16 + h + 1]).then_inc(b.h(sexa), 1)
        b.bump(sexa, 256)

    @blk.vector
    def _(v):
        for i in range(NT):
            for h in range(16):
                idx = 16 * i + h
                v.wait_ge(b.h(sexa), idx + 1)
                sl = KF[:, (i % 2) * 4096 + h * 256:(i % 2) * 4096 + (h + 1) * 256]
                v.tensor_scalar_add(sl, sl, A2_EPS).then_inc(b.h(sexp), 1)
        b.bump(sexp, 256)

    @blk.vector
    def _(v):
        v.wait_ge(b.h(sctx), 256)
        for g in range(32):
            v.tensor_copy(CTXE[:, g * 65:(g + 1) * 65],
                          PCT[g // 7][:, (g % 7) * 65:(g % 7) * 65 + 65]).then_inc(b.h(sev), 1)
        b.bump(sev, 32)

    @blk.sync
    def _(e):
        b.wait(e, sev, 32)
        b.inc(e.dma_start(out=octx.rearrange("g p e -> p g e"),
                          in_=CTXE[:].rearrange("p (g e) -> p g e", e=65)), sst, 16)
        b.wait(e, sst)
    return b.finish()


def mk_a2d():
    b = B()
    uin = b.inp("uin", [T, D])
    wqT = b.inp("wqT", [D, D])
    qbc = b.inp("qbc", [128, 8])
    projc = b.inp("projc", [64, 256])
    ctxp = b.inp("ctxp", [4, 32, 128, 65])
    oav = b.outp("oav", [T, D])

    UR = b.sb("UR", [128, 2 * D])
    UF = b.sb("UF", [128, 8 * T], BF16)
    WQT = b.sb("WQT", [128, 8 * D], BF16)
    QBC = b.sb("QBC", [128, 8], F32)
    QFF = b.sb("QFF", [128, 8 * T], BF16)
    PJ2 = b.sb("PJ2", [128, 512], BF16)
    CTXP = b.sb("CTXP", [128, 2 * 2080])
    CTXS = b.sb("CTXS", [128, 2080])
    CTXB = b.sb("CTXB", [128, 2080], BF16)
    QFM = b.sb("QFM", [128, 2 * 2 * T], BF16)
    LNC = b.sb("LNC", [128, 1])
    RIV = b.sb("RIV", [128, 1])
    OAVS = b.sb("OAVS", [128, 4 * 64])
    PS = [b.ps(f"ps{i}", [128, 512], F32) for i in range(4)]

    blk = b.begin()
    sini = b.sem("sini")
    sld = b.sem("sld")
    sw = b.sem("sw")
    sqp = b.sem("sqp")
    spj = b.sem("spj")
    sldc = b.sem("sldc")
    scmb = b.sem("scmb")
    sdq = b.sem("sdq")
    seva = b.sem("seva")
    seps = b.sem("seps")
    sav = b.sem("sav")
    sfl = b.sem("sfl")
    srd = b.sem("srd")
    sst = b.sem("sst")
    b.sem("cfU_c")

    ident_setup(b, blk, sini)
    uit = uin.rearrange("(n p) d -> n p d", p=128)

    @blk.sync
    def _(e):
        b.inc(e.dma_start(out=QBC[:], in_=qbc), sld, 16)
        for i in range(NT):
            if i >= 2:
                e.wait_ge(b.h("cfU_c"), i - 1)
            b.inc(e.dma_start(out=UR[:, (i % 2) * D:(i % 2 + 1) * D], in_=uit[i]),
                  sld, 16)

    @blk.gpsimd
    def _(g):
        g.memset(PJ2[:], 0.0)
        b.inc(g.dma_start(out=PJ2[0:64, 0:256], in_=projc), spj, 16)
        b.inc(g.dma_start(out=PJ2[64:128, 256:512], in_=projc), spj, 16)
        b.inc(g.dma_start(out=WQT[:].rearrange("p (c o) -> p c o", o=D),
                          in_=wqT.rearrange("(c p) o -> p c o", p=128)), sw, 16)
        for r in range(4):
            if r >= 2:
                g.wait_ge(b.h(scmb), r - 1)
            b.inc(g.dma_start(out=CTXP[:, (r % 2) * 2080:(r % 2 + 1) * 2080]
                              .rearrange("p (q e) -> p q e", e=65),
                              in_=ctxp[r].rearrange("q p e -> p q e")), sldc, 16)

    cast_fm_emit(b, blk, "cfU", NT, D,
                 lambda i: UR[:, (i % 2) * D:(i % 2 + 1) * D],
                 lambda i: [(sld, 16 + 16 * (i + 1))],
                 lambda i, c: UF[:, c * T + i * 128:c * T + (i + 1) * 128],
                 suf := b.sem("suf"))

    g1 = []
    for co in range(8):
        for tg in range(4):
            mms = [(WQT[:, ci * D + co * 128:ci * D + (co + 1) * 128],
                    UF[:, ci * T + tg * 512:ci * T + (tg + 1) * 512])
                   for ci in range(8)]

            def ev(e, pa, co=co, tg=tg):
                return e.activation(out=QFF[:, co * T + tg * 512:co * T + (tg + 1) * 512],
                                    in_=pa, func=AF.Identity, bias=QBC[:, co:co + 1])
            g1.append(dict(mms=mms, pslice=lambda p: p[:, 0:512],
                           evac_eng="scalar", evac=ev))
    gemm(b, blk, "qp", g1, PS[0:2], wait_list=[(suf, None), (sw, 16), (sld, 16)],
         done=sqp)

    @blk.vector
    def _(v):
        v.wait_ge(b.h(sldc), 32)
        v.tensor_add(CTXS[:], CTXP[:, 0:2080],
                     CTXP[:, 2080:4160]).then_inc(b.h(scmb), 1)
        for r in range(2, 4):
            v.wait_ge(b.h(sldc), 16 * (r + 1))
            v.wait_ge(b.h(scmb), r - 1)
            v.tensor_add(CTXS[:], CTXS[:],
                         CTXP[:, (r % 2) * 2080:(r % 2 + 1) * 2080]).then_inc(b.h(scmb), 1)
        v.wait_ge(b.h(scmb), 3)
        v.tensor_copy(CTXB[:], CTXS[:]).then_inc(b.h(scmb), 1)
        b.bump(scmb, 4)

    @blk.tensor
    def _(t):
        b.wait(t, sqp, 32)
        b.wait(t, scmb, 4)
        b.wait(t, spj, 32)
        for h in range(16):
            for mc in range(2):
                for tg in range(4):
                    idx = h * 8 + mc * 4 + tg
                    if idx >= 2:
                        t.wait_ge(b.h(seva), idx - 1)
                    t.matmul(PS[idx % 2][:, 0:512],
                             PJ2[:, (h % 2) * 256 + mc * 128:(h % 2) * 256 + (mc + 1) * 128],
                             QFF[:, (h // 2) * T + tg * 512:(h // 2) * T + (tg + 1) * 512],
                             start=True, stop=True).then_inc(b.h(sdq), 1)
            t.wait_ge(b.h(seps), 2 * (h + 1))
            for tt in range(NT):
                gidx = h * 16 + tt
                if gidx >= 2:
                    t.wait_ge(b.h(srd), gidx - 1)
                for mc in range(2):
                    ins = t.matmul(PS[2 + tt % 2][:, 0:65],
                                   QFM[:, (h % 2) * 2 * T + mc * T + tt * 128:
                                       (h % 2) * 2 * T + mc * T + (tt + 1) * 128],
                                   CTXB[:, (h * 2 + mc) * 65:(h * 2 + mc + 1) * 65],
                                   start=(mc == 0), stop=(mc == 1))
                ins.then_inc(b.h(sav), 1)
        b.bump(sdq, 128)
        b.bump(sav, 256)

    @blk.scalar
    def _(s):
        for h in range(16):
            for mc in range(2):
                for tg in range(4):
                    idx = h * 8 + mc * 4 + tg
                    s.wait_ge(b.h(sdq), idx + 1)
                    if h >= 2:
                        s.wait_ge(b.h(sav), 16 * (h - 1))
                    s.activation(out=QFM[:, (h % 2) * 2 * T + mc * T + tg * 512:
                                        (h % 2) * 2 * T + mc * T + (tg + 1) * 512],
                                 in_=PS[idx % 2][:, 0:512], func=AF.Exp,
                                 scale=A2_DN).then_inc(b.h(seva), 1)
            for tt in range(NT):
                gidx = h * 16 + tt
                s.wait_ge(b.h(sav), gidx + 1)
                if gidx >= 1:
                    s.wait_ge(b.h(srd), gidx)
                s.activation(out=LNC[:, 0:1], in_=PS[2 + tt % 2][:, 64:65],
                             func=AF.Ln).then_inc(b.h(sfl), 1)
                s.wait_ge(b.h(sfl), 2 * gidx + 1)
                s.activation(out=RIV[:, 0:1], in_=LNC[:, 0:1], func=AF.Exp,
                             scale=-1.0).then_inc(b.h(sfl), 1)
        b.bump(seva, 128)
        b.bump(sfl, 512)

    @blk.vector
    def _(v):
        for h in range(16):
            for mc in range(2):
                v.wait_ge(b.h(seva), h * 8 + (mc + 1) * 4)
                sl = QFM[:, (h % 2) * 2 * T + mc * T:(h % 2) * 2 * T + (mc + 1) * T]
                v.tensor_scalar_add(sl, sl, A2_EPS).then_inc(b.h(seps), 1)
            for tt in range(NT):
                gidx = h * 16 + tt
                v.wait_ge(b.h(sfl), 2 * gidx + 2)
                if gidx >= 4:
                    v.wait_ge(b.h(sst), 16 * (gidx - 3))
                v.tensor_scalar(OAVS[:, (gidx % 4) * 64:(gidx % 4 + 1) * 64],
                                PS[2 + tt % 2][:, 0:64], RIV[:, 0:1], None,
                                ALU.mult).then_inc(b.h(srd), 1)
        b.bump(seps, 32)
        b.bump(srd, 256)

    oavt = oav.rearrange("(n p) d -> n p d", p=128)

    @blk.sync
    def _(e):
        for h in range(16):
            for tt in range(NT):
                gidx = h * 16 + tt
                e.wait_ge(b.h(srd), gidx + 1)
                b.inc(e.dma_start(out=oavt[tt][:, h * 64:(h + 1) * 64],
                                  in_=OAVS[:, (gidx % 4) * 64:(gidx % 4 + 1) * 64]),
                      sst, 16)
        b.wait(e, sst)
    return b.finish()


def mk_a2e():
    b = B()
    avin = b.inp("avin", [T, D])
    woT = b.inp("woT", [D, D])
    bo = b.inp("bo", [D])
    xprev = b.inp("xprev", [T, D])
    xout = b.outp("xout", [T, D])

    AVR = b.sb("AVR", [128, 2 * D])
    AVF = b.sb("AVF", [128, 8 * T], BF16)
    WOT = b.sb("WOT", [128, 8 * D], BF16)
    BOR = b.sb("BOR", [1, D], BF16)
    OX = b.sb("OX", [128, NT * D])
    XR = b.sb("XR", [128, 2 * D])
    TMP = b.sb("TMP", [128, 2 * D])
    PS = [b.ps(f"ps{i}", [128, 512], F32) for i in range(3)]

    blk = b.begin()
    sini = b.sem("sini")
    sld = b.sem("sld")
    slx = b.sem("slx")
    sw = b.sem("sw")
    swo = b.sem("swo")
    sfin = b.sem("sfin")
    sst = b.sem("sst")
    b.sem("cfA_c")

    ident_setup(b, blk, sini)
    avt = avin.rearrange("(n p) d -> n p d", p=128)

    @blk.sync
    def _(e):
        for i in range(NT):
            if i >= 2:
                e.wait_ge(b.h("cfA_c"), i - 1)
            b.inc(e.dma_start(out=AVR[:, (i % 2) * D:(i % 2 + 1) * D], in_=avt[i]),
                  sld, 16)

    @blk.gpsimd
    def _(g):
        b.inc(g.dma_start(out=WOT[:].rearrange("p (c o) -> p c o", o=D),
                          in_=woT.rearrange("(c p) o -> p c o", p=128)), sw, 16)
        b.inc(g.dma_start(out=BOR[0:1, :], in_=bo[None, :]), sw, 16)

    cast_fm_emit(b, blk, "cfA", NT, D,
                 lambda i: AVR[:, (i % 2) * D:(i % 2 + 1) * D],
                 lambda i: [(sld, 16 * (i + 1))],
                 lambda i, c: AVF[:, c * T + i * 128:c * T + (i + 1) * 128],
                 suf := b.sem("suf"))

    g2 = []
    for tt in range(NT):
        for og in range(2):
            mms = [(AVF[:, ci * T + tt * 128:ci * T + (tt + 1) * 128],
                    WOT[:, ci * D + og * 512:ci * D + (og + 1) * 512])
                   for ci in range(8)]
            mms.append((b.ones_row[0:1, 0:128], BOR[0:1, og * 512:(og + 1) * 512]))

            def ev(e, pa, tt=tt, og=og):
                return e.tensor_copy(OX[:, tt * D + og * 512:tt * D + (og + 1) * 512], pa)
            g2.append(dict(mms=mms, pslice=lambda p: p[:, 0:512],
                           evac_eng="vector", evac=ev))
    gemm(b, blk, "wo", g2, PS, wait_list=[(suf, None), (sw, 32), (sini, None)],
         done=swo)

    def get_o(i):
        return OX[:, i * D:(i + 1) * D]

    def post(v, i, st):
        tmp = TMP[:, (i % 2) * D:(i % 2 + 1) * D]
        if i >= 2:
            v.wait_ge(b.h(sst), 16 * (i - 1))
        ln_apply(v, get_o(i), tmp, st)
        v.wait_ge(b.h(slx), 16 * (i + 1))
        return v.tensor_add(tmp, tmp, XR[:, (i % 2) * D:(i % 2 + 1) * D])

    ln_emit(b, blk, "lnO", NT, D, get_o,
            lambda i: [(swo, 2 * NT)], sfin, post=post)

    xpt = xprev.rearrange("(n p) d -> n p d", p=128)
    xot = xout.rearrange("(n p) d -> n p d", p=128)

    @blk.sync
    def _(e):
        for i in range(NT):
            if i >= 2:
                b.wait(e, sfin, i - 1)
            b.inc(e.dma_start(out=XR[:, (i % 2) * D:(i % 2 + 1) * D], in_=xpt[i]),
                  slx, 16)
            if i >= 2:
                b.inc(e.dma_start(out=xot[i - 2],
                                  in_=TMP[:, ((i - 2) % 2) * D:((i - 2) % 2 + 1) * D]),
                      sst, 16)
        for i in range(NT - 2, NT):
            b.wait(e, sfin, i + 1)
            b.inc(e.dma_start(out=xot[i], in_=TMP[:, (i % 2) * D:(i % 2 + 1) * D]),
                  sst, 16)
        b.wait(e, sst)
    return b.finish()


def run_a2(z, x4):
    ncp = mk_preln()
    imp = [dict(xin=shard_rows(x4, c)) for c in range(NCORES)]
    rp = run_bass_kernel_spmd(ncp, imp, core_ids=list(range(NCORES))).results

    nck = mk_a2k()
    wkT = np.ascontiguousarray(z["L2_attn_Wk"].T)
    wvT = np.ascontiguousarray(z["L2_attn_Wv"].T)
    imk = [dict(uin=rp[c]["uout"], wkT=wkT, bkr=z["L2_attn_bk"],
                wvT=wvT, bvr=z["L2_attn_bv"]) for c in range(NCORES)]
    rk = run_bass_kernel_spmd(nck, imk, core_ids=list(range(NCORES))).results

    mst = np.tile(np.concatenate([np.asarray(rk[c]["omax"]).reshape(1)
                                  for c in range(NCORES)]), 64)[None, :]
    mst = np.ascontiguousarray(mst.astype(np.float32))
    projc = np.ascontiguousarray(z["L2_attn_proj"].T)
    ncc = mk_a2c()
    imc = [dict(kfm=rk[c]["okfm"], vti=rk[c]["ovt"], diag=rk[c]["odiag"],
                mst=mst, projc=projc) for c in range(NCORES)]
    rc = run_bass_kernel_spmd(ncc, imc, core_ids=list(range(NCORES))).results

    ncd = mk_a2d()
    wqT = np.ascontiguousarray(z["L2_attn_Wq"].T)
    qbc = np.ascontiguousarray(z["L2_attn_bq"].reshape(8, 128).T)
    imd = []
    for c in range(NCORES):
        bidx = c // 4
        ctxp = np.stack([np.asarray(rc[bidx * 4 + q]["octx"]) for q in range(4)])
        imd.append(dict(uin=rp[c]["uout"], wqT=wqT, qbc=qbc, projc=projc,
                        ctxp=ctxp))
    rd = run_bass_kernel_spmd(ncd, imd, core_ids=list(range(NCORES))).results

    nce = mk_a2e()
    woT = np.ascontiguousarray(z["L2_attn_Wo"].T)
    ime = [dict(avin=rd[c]["oav"], woT=woT, bo=z["L2_attn_bo"],
                xprev=shard_rows(x4, c)) for c in range(NCORES)]
    re_ = run_bass_kernel_spmd(nce, ime, core_ids=list(range(NCORES))).results
    return unshard([re_[c]["xout"] for c in range(NCORES)])


def run_f2(z, x5):
    ncp = mk_preln()
    imp = [dict(xin=shard_rows(x5, c)) for c in range(NCORES)]
    rp = run_bass_kernel_spmd(ncp, imp, core_ids=list(range(NCORES))).results

    ew1 = np.concatenate([z["L2_ffn_ew1"], z["L2_ffn_sh_w1"][None]], axis=0)
    ew2 = np.concatenate([z["L2_ffn_ew2"], z["L2_ffn_sh_w2"][None]], axis=0)
    eb1 = np.concatenate([z["L2_ffn_eb1"], z["L2_ffn_sh_b1"][None]], axis=0)
    eb2 = np.concatenate([z["L2_ffn_eb2"], z["L2_ffn_sh_b2"][None]], axis=0)
    eb1t = np.ascontiguousarray(
        eb1.reshape(9, 32, 128).transpose(2, 0, 1).reshape(128, 9 * 32))
    eb2r = np.ascontiguousarray(eb2.reshape(9 * 1024))
    ncm = mk_f2moe()
    imm = [dict(u=rp[c]["uout"], x6=shard_rows(x5, c),
                gate_w=z["L2_ffn_gate_w"], gate_b=z["L2_ffn_gate_b"],
                ew1=ew1, eb1t=eb1t, ew2=ew2, eb2r=eb2r) for c in range(NCORES)]
    rm = run_bass_kernel_spmd(ncm, imm, core_ids=list(range(NCORES))).results
    return unshard([rm[c]["x7"] for c in range(NCORES)])


def run_a3(z, x6):
    ncp = mk_preln()
    imp = [dict(xin=shard_rows(x6, c)) for c in range(NCORES)]
    rp = run_bass_kernel_spmd(ncp, imp, core_ids=list(range(NCORES))).results

    ncq = mk_a3qkv()
    bq = z["L3_attn_b_qkv"]
    qkb = np.zeros((128, 32), np.float32)
    for h in range(16):
        for kind in range(2):
            qkb[(h % 2) * 64:(h % 2) * 64 + 64, kind * 16 + h] = \
                bq[192 * h + 64 * kind:192 * h + 64 * kind + 64]
    wqT = np.ascontiguousarray(z["L3_attn_W_qkv"].T)
    imq = [dict(uin=rp[c]["uout"], wqkvT=wqT,
                bqkv=bq, qkb=qkb) for c in range(NCORES)]
    rq = run_bass_kernel_spmd(ncq, imq, core_ids=list(range(NCORES))).results

    nca = mk_gattn(16, 1, False, has_bias=True)
    woT = np.ascontiguousarray(z["L3_attn_W_out"].T)
    zeros_u = np.zeros((T, D), np.float32)
    flags0 = np.zeros((128, 4 * 64), np.float32)
    ima = []
    for c in range(NCORES):
        bidx = c // 4
        ks = [rq[bidx * 4 + q]["ok"] for q in range(4)]
        kglob = np.ascontiguousarray(np.concatenate(ks, axis=2))
        vs = np.concatenate([rq[bidx * 4 + q]["vt"] for q in range(4)], axis=0)
        vglob = np.ascontiguousarray(vs.reshape(8192, 16, 65).transpose(1, 0, 2))
        ima.append(dict(qh=rq[c]["oq"], kglob=kglob, vglob=vglob, uin=zeros_u,
                        wo=woT, flags=flags0, obias=z["L3_attn_b_out"]))
    ra = run_bass_kernel_spmd(nca, ima, core_ids=list(range(NCORES))).results

    ncl = mk_postln()
    iml = [dict(o=ra[c]["u1"], xprev=shard_rows(x6, c)) for c in range(NCORES)]
    rl = run_bass_kernel_spmd(ncl, iml, core_ids=list(range(NCORES))).results
    return unshard([rl[c]["xout"] for c in range(NCORES)])


def run_f3(z, x7):
    ncp = mk_preln()
    imp = [dict(xin=shard_rows(x7, c)) for c in range(NCORES)]
    rp = run_bass_kernel_spmd(ncp, imp, core_ids=list(range(NCORES))).results

    ncc = mk_colsum()
    imc = [dict(uin=rp[c]["uout"]) for c in range(NCORES)]
    rc = run_bass_kernel_spmd(ncc, imc, core_ids=list(range(NCORES))).results

    ncg = mk_f3gf()
    img = []
    for c in range(NCORES):
        bidx = c // 4
        gcp = np.concatenate([rc[bidx * 4 + q]["out"] for q in range(4)], axis=0)
        img.append(dict(
            gcp=gcp,
            gw1=z["L3_ffn_gc_w1"], gw2=z["L3_ffn_gc_w2"], gw3=z["L3_ffn_gc_w3"],
            gb1r=z["L3_ffn_gc_b1"], gb2r=z["L3_ffn_gc_b2"], gb3=z["L3_ffn_gc_b3"],
            ar=z["L3_ffn_adaptive_range"]))
    rg = run_bass_kernel_spmd(ncg, img, core_ids=list(range(NCORES))).results

    def cols(bvec, nch):
        return np.ascontiguousarray(
            bvec.reshape(nch, 128).T.astype(np.float32))
    ncy = mk_f3cyc()
    state = [rp[c]["uout"] for c in range(NCORES)]
    for cyc in range(3):
        imy = [dict(sin=state[c],
                    cw1=z["L3_ffn_cn_w1"], cw2=z["L3_ffn_cn_w2"],
                    cw3=z["L3_ffn_cn_w3"], cw4=z["L3_ffn_cn_w4"],
                    cb1c=cols(z["L3_ffn_cn_b1"], 8), cb2c=cols(z["L3_ffn_cn_b2"], 4),
                    cb3c=cols(z["L3_ffn_cn_b3"], 2), cb4=z["L3_ffn_cn_b4"],
                    c01=rg[c]["oc01"]) for c in range(NCORES)]
        ry = run_bass_kernel_spmd(ncy, imy, core_ids=list(range(NCORES))).results
        state = [ry[c]["sout"] for c in range(NCORES)]

    ncl = mk_postln()
    iml = [dict(o=state[c], xprev=shard_rows(x7, c)) for c in range(NCORES)]
    rl = run_bass_kernel_spmd(ncl, iml, core_ids=list(range(NCORES))).results
    return unshard([rl[c]["xout"] for c in range(NCORES)])


def mk_a1qkv():
    b = B()
    uin = b.inp("uin", [T, D])
    wT = b.inp("wT", [D, 3072])
    bqkv = b.inp("bqkv", [3072])
    qbc = b.inp("qbc", [128, 16])
    gains = b.inp("gains", [1, 4 * 1024])
    oqf = b.outp("qf", [8, 128, T], BF16)
    okvp = b.outp("kvp", [128, 8 * 64], BF16)

    UR = b.sb("UR", [128, 2 * D])
    UF = b.sb("UF", [128, 8 * T], BF16)
    WT = b.sb("WT", [128, 8 * 3072], BF16)
    QF = b.sb("QF", [128, 8 * T], BF16)
    KB = b.sb("KB", [128, NT * D], BF16)
    VB = b.sb("VB", [128, NT * D], BF16)
    BR = b.sb("BR", [1, 3072], BF16)
    QB = b.sb("QB", [128, 16], F32)
    GRR = b.sb("GRR", [1, 512], BF16)
    GB = b.sb("GB", [128, 4 * 1024], BF16)
    SQ2 = b.sb("SQ2", [128, 512], BF16)
    ST2 = b.sb("ST2", [128, 96], F32)
    KVP = b.sb("KVP", [128, 8 * 64], BF16)
    PS = [b.ps(f"ps{i}", [128, 512], F32) for i in range(2)]
    PKV = [b.ps(f"pkv{i}", [128, 128], F32) for i in range(2)]
    PSB = b.ps("psb", [128, 512], F32)

    blk = b.begin()
    sini = b.sem("sini")
    sld = b.sem("sld")
    sw = b.sem("sw")
    suf = b.sem("suf")
    sproj = b.sem("sproj")
    s_sq = b.sem("s_sq")
    s_v = b.sem("s_v")
    s_var = b.sem("s_var")
    s_rstd = b.sem("s_rstd")
    s_app = b.sem("s_app")
    skv = b.sem("skv")
    skve = b.sem("skve")
    sst = b.sem("sst")
    b.sem("cfU_c")

    ident_setup(b, blk, sini)
    uit = uin.rearrange("(n p) d -> n p d", p=128)

    @blk.sync
    def _(e):
        b.inc(e.dma_start(out=QB[:], in_=qbc[:, :]), sld, 16)
        for i in range(NT):
            if i >= 2:
                e.wait_ge(b.h("cfU_c"), i - 1)
            b.inc(e.dma_start(out=UR[:, (i % 2) * D:(i % 2 + 1) * D], in_=uit[i]),
                  sld, 16)

    sgl = b.sem("sgl")
    bc_sems = []

    @blk.gpsimd
    def _(g):
        b.inc(g.dma_start(out=WT[:].rearrange("p (c o) -> p c o", o=3072),
                          in_=wT.rearrange("(c p) o -> p c o", p=128)), sw, 16)
        b.inc(g.dma_start(out=BR[0:1, :], in_=bqkv[None, :]), sw, 16)

    prev = None
    for jj in range(8):
        def mkload(jj=jj, prev=prev):
            @blk.gpsimd
            def _(g):
                if prev is not None:
                    g.wait_ge(b.h(prev), 1)
                b.inc(g.dma_start(out=GRR[0:1, :],
                                  in_=gains[0:1, jj * 512:(jj + 1) * 512]), sgl, 16)
        mkload()
        prev = emit_bcast(b, blk, f"gb{jj}",
                          [(GRR[0:1, :], GB[:, jj * 512:(jj + 1) * 512], 512)], PSB,
                          wait_list=[(sgl, 16 * (jj + 1)), (sini, None)])
    sgb = prev

    cast_fm_emit(b, blk, "cfU", NT, D,
                 lambda i: UR[:, (i % 2) * D:(i % 2 + 1) * D],
                 lambda i: [(sld, 16 + 16 * (i + 1))],
                 lambda i, c: UF[:, c * T + i * 128:c * T + (i + 1) * 128], suf)

    groups = []
    for h in range(16):
        ro = (h % 2) * 64
        for g4 in range(4):
            mms = [(WT[:, ci * 3072 + h * 64:ci * 3072 + h * 64 + 64],
                    UF[:, ci * T + g4 * 512:ci * T + (g4 + 1) * 512])
                   for ci in range(8)]

            def ev(e, pa, h=h, g4=g4, ro=ro):
                return e.activation(
                    out=QF[ro:ro + 64, (h // 2) * T + g4 * 512:(h // 2) * T + (g4 + 1) * 512],
                    in_=pa, func=AF.Identity, bias=QB[ro:ro + 64, h:h + 1])
            groups.append(dict(mms=mms, pslice=lambda p: p[0:64, 0:512],
                               evac_eng="scalar", evac=ev))
    NQG = len(groups)
    for tt in range(NT):
        for h in range(16):
            for kind in range(2):
                base_o = 1024 + kind * 1024 + h * 64
                mms = [(UF[:, ci * T + tt * 128:ci * T + (tt + 1) * 128],
                        WT[:, ci * 3072 + base_o:ci * 3072 + base_o + 64])
                       for ci in range(8)]
                mms.append((b.ones_row[0:1, 0:128], BR[0:1, base_o:base_o + 64]))

                def ev(e, pa, tt=tt, h=h, kind=kind):
                    dstbuf = KB if kind == 0 else VB
                    return e.tensor_copy(
                        dstbuf[:, tt * D + h * 64:tt * D + (h + 1) * 64], pa)
                groups.append(dict(mms=mms, pslice=lambda p: p[:, 0:64],
                                   evac_eng="vector", evac=ev))
    gemm(b, blk, "proj", groups, PS, wait_list=[(suf, None), (sw, 32), (sld, 16)],
         done=sproj)

    NSEG = 16

    @blk.scalar
    def _(s):
        cnt = 0
        for i in range(NT):
            for kind in range(2):
                buf = KB if kind == 0 else VB
                idx = 2 * i + kind
                s.wait_ge(b.h(sproj), NQG + (i + 1) * 32)
                if idx >= 1:
                    s.wait_ge(b.h(s_v), 4 * idx)
                s.activation(out=SQ2[:], in_=buf[:, i * D:i * D + 512],
                             func=AF.Square).then_inc(b.h(s_sq), 1)
                s.wait_ge(b.h(s_v), 4 * idx + 1)
                s.activation(out=SQ2[:], in_=buf[:, i * D + 512:(i + 1) * D],
                             func=AF.Square).then_inc(b.h(s_sq), 1)
                s.wait_ge(b.h(s_var), idx + 1)
                s.activation(out=ST2[:, 80:96], in_=ST2[:, 32:48],
                             func=AF.Ln).then_inc(b.h(s_sq), 1)
                s.wait_ge(b.h(s_sq), 3 * idx + 3)
                s.activation(out=ST2[:, 48:64], in_=ST2[:, 80:96], func=AF.Exp,
                             scale=-0.5).then_inc(b.h(s_rstd), 1)
        b.bump(s_sq, NT * 6)
        b.bump(s_rstd, NT * 2)

    @blk.vector
    def _(v):
        nv = 0
        for i in range(NT):
            for kind in range(2):
                buf = KB if kind == 0 else VB
                tile = buf[:, i * D:(i + 1) * D]
                idx = 2 * i + kind
                gofs = kind * 2048
                v.wait_ge(b.h(s_sq), 3 * idx + 1)
                v.reduce_sum(ST2[:, 0:16], tile.rearrange("p (s d) -> p s d", d=64),
                             axis=AX.X)
                v.reduce_sum(ST2[:, 16:24], SQ2[:].rearrange("p (s d) -> p s d", d=64),
                             axis=AX.X).then_inc(b.h(s_v), 1); nv += 1
                v.wait_ge(b.h(s_sq), 3 * idx + 2)
                v.reduce_sum(ST2[:, 24:32], SQ2[:].rearrange("p (s d) -> p s d", d=64),
                             axis=AX.X).then_inc(b.h(s_v), 1); nv += 1
                v.wait_ge(b.h(s_v), nv)
                v.tensor_scalar_mul(ST2[:, 0:16], ST2[:, 0:16], 1.0 / 64).then_inc(b.h(s_v), 1); nv += 1
                v.wait_ge(b.h(s_v), nv)
                v.tensor_mul(ST2[:, 64:80], ST2[:, 0:16], ST2[:, 0:16])
                v.tensor_scalar(ST2[:, 32:48], ST2[:, 16:32], 1.0 / 64, 1e-5,
                                ALU.mult, ALU.add).then_inc(b.h(s_v), 1); nv += 1
                v.wait_ge(b.h(s_v), nv)
                v.tensor_sub(ST2[:, 32:48], ST2[:, 32:48],
                             ST2[:, 64:80]).then_inc(b.h(s_var), 1)
                v.wait_ge(b.h(s_rstd), idx + 1)
                for sgi in range(NSEG):
                    v.tensor_scalar(tile[:, sgi * 64:(sgi + 1) * 64],
                                    tile[:, sgi * 64:(sgi + 1) * 64],
                                    ST2[:, sgi:sgi + 1], ST2[:, 48 + sgi:49 + sgi],
                                    ALU.subtract, ALU.mult)
                for half in range(2):
                    sl = tile[:, half * 512:(half + 1) * 512]
                    v.tensor_mul(sl, sl, GB[:, gofs + half * 512:gofs + (half + 1) * 512])
                    ins = v.tensor_add(sl, sl, GB[:, 1024 + gofs + half * 512:
                                                  1024 + gofs + (half + 1) * 512])
                ins.then_inc(b.h(s_app), 1)
        b.bump(s_v, NT * 2 * 4)
        b.bump(s_var, NT * 2)
        b.bump(s_app, NT * 2)

    @blk.tensor
    def _(t):
        b.wait(t, s_app, NT * 2)
        for j in range(8):
            if j >= 2:
                t.wait_ge(b.h(skve), 2 * (j - 1))
            for sub in range(2):
                h = 2 * j + sub
                for i in range(NT):
                    ins = t.matmul(PKV[j % 2][:, sub * 64:(sub + 1) * 64],
                                   KB[:, i * D + 2 * j * 64:i * D + 2 * j * 64 + 128],
                                   VB[:, i * D + h * 64:i * D + (h + 1) * 64],
                                   start=(i == 0), stop=(i == NT - 1),
                                   skip_group_check=True)
            ins.then_inc(b.h(skv), 1)
        b.bump(skv, 8)

    @blk.vector
    def _(v):
        for j in range(8):
            v.wait_ge(b.h(skv), j + 1)
            v.tensor_copy(KVP[0:64, j * 64:(j + 1) * 64],
                          PKV[j % 2][0:64, 0:64]).then_inc(b.h(skve), 1)
            v.tensor_copy(KVP[64:128, j * 64:(j + 1) * 64],
                          PKV[j % 2][64:128, 64:128]).then_inc(b.h(skve), 1)
        b.bump(skve, 16)

    @blk.sync
    def _(e):
        b.wait(e, sproj)
        b.inc(e.dma_start(out=oqf.rearrange("a p t -> p a t"),
                          in_=QF[:].rearrange("p (a t) -> p a t", t=T)), sst, 16)
        b.wait(e, skve, 16)
        b.inc(e.dma_start(out=okvp, in_=KVP[:]), sst, 16)
        b.wait(e, sst)
    return b.finish()


def mk_a1att(phase=5):
    b = B()
    uin = b.inp("uin", [T, D])
    x2 = b.inp("x2", [T, D])
    qf = b.inp("qf", [8, 128, T], BF16)
    kvp = b.inp("kvp", [4, 128, 8 * 64], BF16)
    tempc = b.inp("tempc", [128, 8])
    abias = b.inp("abias", [128, 8 * 64])
    p1T = b.inp("p1T", [D, D])
    p1bc = b.inp("p1bc", [128, 8])
    p2T = b.inp("p2T", [D, D])
    p2b = b.inp("p2b", [D])
    x3out = b.outp("x3", [T, D])

    U = b.sb("U", [128, NT * D])
    ATF = b.sb("ATF", [128, 8 * T], BF16)
    H1F = b.sb("H1F", [128, 8 * T], BF16)
    QFS = b.sb("QFS", [128, 2 * 1024], BF16)
    W1T = b.sb("W1T", [128, 8 * D], BF16)
    W2T = b.sb("W2T", [128, 8 * D], BF16)
    KVS = b.sb("KVS", [128, 2 * 512], F32)
    KV2 = b.sb("KV2", [128, 16 * 64], BF16)
    TMPC = b.sb("TMPC", [128, 8], F32)
    ABI = b.sb("ABI", [128, 512], F32)
    P1B = b.sb("P1B", [128, 8], F32)
    P2R = b.sb("P2R", [1, D], BF16)
    XR = b.sb("XR", [128, 2 * D])
    TMP = b.sb("TMP", [128, 2 * D])
    PS = [b.ps(f"ps{i}", [128, 512], F32) for i in range(3)]

    blk = b.begin()
    sini = b.sem("sini")
    sld = b.sem("sld")
    slu = b.sem("slu")
    sw = b.sem("sw")
    skvr = b.sem("skvr")
    satt = b.sem("satt")
    satte = b.sem("satte")
    satf = b.sem("satf")
    sp1 = b.sem("sp1")
    sfin = b.sem("sfin")
    sst = b.sem("sst")
    sqf = b.sem("sqf")

    ident_setup(b, blk, sini)

    @blk.sync
    def _(e):
        b.inc(e.dma_start(out=U[:].rearrange("p (n d) -> p n d", d=D),
                          in_=uin.rearrange("(n p) d -> p n d", p=128)), slu, 16)
        b.inc(e.dma_start(out=TMPC[:], in_=tempc[:, :]), sld, 16)
        b.inc(e.dma_start(out=ABI[:], in_=abias[:, :]), sld, 16)
        b.inc(e.dma_start(out=P1B[:], in_=p1bc[:, :]), sld, 16)
        for tt in range(NT if phase >= 2 else 2):
            if tt >= 2:
                e.wait_ge(b.h(satt), 2 * (tt - 1))
            b.inc(e.dma_start(
                out=QFS[:, (tt % 2) * 1024:(tt % 2 + 1) * 1024]
                    .rearrange("p (a t) -> p a t", t=128),
                in_=qf[:, :, tt * 128:(tt + 1) * 128].rearrange("a p t -> p a t")),
                sqf, 16)

    skl = b.sem("skl")

    @blk.gpsimd
    def _(g):
        b.inc(g.dma_start(out=KVS[:, 0:512], in_=kvp[0]), skl, 16)
        b.inc(g.dma_start(out=KVS[:, 512:1024], in_=kvp[1]), skl, 16)
        for r in (2, 3):
            g.wait_ge(b.h("skvr"), r - 1)
            b.inc(g.dma_start(out=KVS[:, 512:1024], in_=kvp[r]), skl, 16)
        b.inc(g.dma_start(out=W1T[:].rearrange("p (c o) -> p c o", o=D),
                          in_=p1T.rearrange("(c p) o -> p c o", p=128)), sw, 16)
        b.inc(g.dma_start(out=W2T[:].rearrange("p (c o) -> p c o", o=D),
                          in_=p2T.rearrange("(c p) o -> p c o", p=128)), sw, 16)
        b.inc(g.dma_start(out=P2R[0:1, :], in_=p2b[None, :]), sw, 16)

    @blk.vector
    def _(v):
        v.memset(KV2[:], 0.0)
        b.wait(v, sld, b.val(sld))
        for r in range(3):
            v.wait_ge(b.h(skl), 32 + 16 * r)
            v.tensor_add(KVS[:, 0:512], KVS[:, 0:512],
                         KVS[:, 512:1024]).then_inc(b.h(skvr), 1)
        v.tensor_scalar_mul(KVS[:, 0:512], KVS[:, 0:512], 1.0 / 8192.0)
        for j in range(8):
            v.tensor_scalar(KVS[:, j * 64:(j + 1) * 64],
                            KVS[:, j * 64:(j + 1) * 64],
                            TMPC[:, j:j + 1], None, ALU.mult)
        v.tensor_add(KVS[:, 0:512], KVS[:, 0:512], ABI[:])
        for j in range(8):
            v.tensor_copy(KV2[0:64, (2 * j) * 64:(2 * j + 1) * 64],
                          KVS[0:64, j * 64:(j + 1) * 64])
            v.tensor_copy(KV2[64:128, (2 * j + 1) * 64:(2 * j + 2) * 64],
                          KVS[64:128, j * 64:(j + 1) * 64])
        v.engine_nop().then_inc(b.h(skvr), 1)
        b.bump(skvr, 4)

    if phase == 1:
        odbg = b.outp("dbg", [128, 1024], BF16)

        @blk.sync
        def _(e):
            b.wait(e, skvr, 4)
            b.inc(e.dma_start(out=odbg, in_=KV2[:]), sst, 16)
            b.wait(e, sst)
        return b.finish()

    @blk.tensor
    def _(t):
        b.wait(t, skvr, 4)
        b.wait(t, sld, b.val(sld))
        for tt in range(NT):
            if tt >= 1:
                t.wait_ge(b.h(satte), 2 * tt)
            t.wait_ge(b.h(sqf), 16 * (tt + 1))
            for half in range(2):
                for hh in range(8):
                    h = half * 8 + hh
                    ro = (h % 2) * 64
                    ins = t.matmul(PS[half][:, hh * 64:(hh + 1) * 64],
                                   QFS[:, (tt % 2) * 1024 + (h // 2) * 128:
                                       (tt % 2) * 1024 + (h // 2 + 1) * 128],
                                   KV2[:, h * 64:(h + 1) * 64],
                                   start=True, stop=True)
                ins.then_inc(b.h(satt), 1)
        b.bump(satt, NT * 2)

    @blk.vector
    def _(v):
        b.wait(v, slu, 16)
        for tt in range(NT):
            for half in range(2):
                v.wait_ge(b.h(satt), 2 * tt + half + 1)
                sl = U[:, tt * D + half * 512:tt * D + (half + 1) * 512]
                v.tensor_add(sl, PS[half][:, 0:512], sl).then_inc(b.h(satte), 1)
        b.bump(satte, NT * 2)

    if phase == 15:
        odbg = b.outp("dbg", [128, NT * D], F32)

        @blk.sync
        def _(e):
            b.wait(e, satte)
            b.inc(e.dma_start(out=odbg, in_=U[:]), sst, 16)
            b.wait(e, sst)
        return b.finish()

    cast_fm_emit(b, blk, "cfA", NT, D,
                 lambda i: U[:, i * D:(i + 1) * D],
                 lambda i: [(satte, 2 * (i + 1))],
                 lambda i, c: ATF[:, c * T + i * 128:c * T + (i + 1) * 128], satf)

    if phase == 2:
        odbg = b.outp("dbg", [128, 8 * T], BF16)

        @blk.sync
        def _(e):
            b.wait(e, satf)
            b.inc(e.dma_start(out=odbg, in_=ATF[:]), sst, 16)
            b.wait(e, sst)
        return b.finish()

    g1 = []
    for co in range(8):
        for g4 in range(4):
            mms = [(W1T[:, ci * D + co * 128:ci * D + (co + 1) * 128],
                    ATF[:, ci * T + g4 * 512:ci * T + (g4 + 1) * 512])
                   for ci in range(8)]

            def ev(e, pa, co=co, g4=g4):
                return e.activation(out=H1F[:, co * T + g4 * 512:co * T + (g4 + 1) * 512],
                                    in_=pa, func=AF.Gelu, bias=P1B[:, co:co + 1])
            g1.append(dict(mms=mms, pslice=lambda p: p[:, 0:512],
                           evac_eng="scalar", evac=ev))
    gemm(b, blk, "p1g", g1, PS, wait_list=[(satf, None), (sw, None), (sld, None)],
         done=sp1)

    if phase == 3:
        odbg = b.outp("dbg", [128, 8 * T], BF16)

        @blk.sync
        def _(e):
            b.wait(e, sp1)
            b.inc(e.dma_start(out=odbg, in_=H1F[:]), sst, 16)
            b.wait(e, sst)
        return b.finish()

    sue = b.sem("sue")
    g2 = []
    for tt in range(NT):
        for og in range(2):
            mms = [(H1F[:, ci * T + tt * 128:ci * T + (tt + 1) * 128],
                    W2T[:, ci * D + og * 512:ci * D + (og + 1) * 512])
                   for ci in range(8)]
            mms.append((b.ones_row[0:1, 0:128], P2R[0:1, og * 512:(og + 1) * 512]))

            def ev(e, pa, tt=tt, og=og):
                e.wait_ge(b.h(slu), 16 + 16 * (tt + 1))
                dst = U[:, tt * D + og * 512:tt * D + (og + 1) * 512]
                return e.tensor_add(dst, pa, dst)
            g2.append(dict(mms=mms, pslice=lambda p: p[:, 0:512],
                           evac_eng="vector", evac=ev))
    gemm(b, blk, "p2g", g2, PS, wait_list=[(sp1, None)], done=sue)

    if phase == 4:
        odbg = b.outp("dbg", [128, NT * D], F32)
        uit2p = uin.rearrange("(n p) d -> n p d", p=128)

        @blk.sync
        def _(e):
            for i in range(NT):
                e.wait_ge(b.h("cfA_c"), i + 1)
                b.inc(e.dma_start(out=U[:, i * D:(i + 1) * D], in_=uit2p[i]),
                      slu, 16)
            b.wait(e, sue)
            b.inc(e.dma_start(out=odbg, in_=U[:]), sst, 16)
            b.wait(e, sst)
        return b.finish()

    def get_o(i):
        return U[:, i * D:(i + 1) * D]

    def post(v, i, st):
        tmp = TMP[:, (i % 2) * D:(i % 2 + 1) * D]
        if i >= 2:
            v.wait_ge(b.h(sst), 16 * (i - 1))
        ln_apply(v, get_o(i), tmp, st)
        v.wait_ge(b.h(sld), b.val(sld) + 16 * (i + 1))
        return v.tensor_add(tmp, tmp, XR[:, (i % 2) * D:(i % 2 + 1) * D])

    SLDF = b.val(sld)
    ln_emit(b, blk, "lnO", NT, D, get_o,
            lambda i: [(sue, 2 * NT)], sfin, post=post)

    uit2 = uin.rearrange("(n p) d -> n p d", p=128)
    x2t = x2.rearrange("(n p) d -> n p d", p=128)
    xot = x3out.rearrange("(n p) d -> n p d", p=128)

    @blk.sync
    def _(e):
        for i in range(NT):
            e.wait_ge(b.h("cfA_c"), i + 1)
            b.inc(e.dma_start(out=U[:, i * D:(i + 1) * D], in_=uit2[i]),
                  slu, 16)
        for i in range(NT):
            if i >= 2:
                b.wait(e, sfin, i - 1)
            b.inc(e.dma_start(out=XR[:, (i % 2) * D:(i % 2 + 1) * D], in_=x2t[i]),
                  sld, 16)
            if i >= 2:
                b.inc(e.dma_start(out=xot[i - 2],
                                  in_=TMP[:, ((i - 2) % 2) * D:((i - 2) % 2 + 1) * D]),
                      sst, 16)
        for i in range(NT - 2, NT):
            b.wait(e, sfin, i + 1)
            b.inc(e.dma_start(out=xot[i], in_=TMP[:, (i % 2) * D:(i % 2 + 1) * D]),
                  sst, 16)
        b.wait(e, sst)
    return b.finish()


def run_a1(z, x2, phase=5):
    ncp = mk_preln()
    imp = [dict(xin=shard_rows(x2, c)) for c in range(NCORES)]
    rp = run_bass_kernel_spmd(ncp, imp, core_ids=list(range(NCORES))).results

    wT = np.ascontiguousarray(z["L1_attn_qkv_w"].T)
    bq = z["L1_attn_qkv_b"]
    qbc = np.zeros((128, 16), np.float32)
    for h in range(16):
        qbc[(h % 2) * 64:(h % 2) * 64 + 64, h] = bq[h * 64:(h + 1) * 64]
    gains = np.concatenate([np.tile(z["L1_attn_kn_g"], 16),
                            np.tile(z["L1_attn_kn_b"], 16),
                            np.tile(z["L1_attn_vn_g"], 16),
                            np.tile(z["L1_attn_vn_b"], 16)])[None, :].astype(np.float32)
    ncq = mk_a1qkv()
    imq = [dict(uin=rp[c]["uout"], wT=wT, bqkv=bq, qbc=qbc, gains=gains)
           for c in range(NCORES)]
    rq = run_bass_kernel_spmd(ncq, imq, core_ids=list(range(NCORES))).results

    nca = mk_a1att(phase)
    tmp = z["L1_attn_temperature"].reshape(16)
    tempc = np.zeros((128, 8), np.float32)
    ab = z["L1_attn_attn_bias"]
    abias = np.zeros((128, 8 * 64), np.float32)
    for h in range(16):
        ro = (h % 2) * 64
        tempc[ro:ro + 64, h // 2] = tmp[h]
        abias[ro:ro + 64, (h // 2) * 64:(h // 2 + 1) * 64] = ab[h]
    p1T = np.ascontiguousarray(z["L1_attn_p1_w"].T)
    p2T = np.ascontiguousarray(z["L1_attn_p2_w"].T)
    p1bc = np.ascontiguousarray(z["L1_attn_p1_b"].reshape(8, 128).T)
    ima = []
    for c in range(NCORES):
        bidx = c // 4
        kvparts = np.stack([rq[bidx * 4 + q]["kvp"] for q in range(4)])
        ima.append(dict(uin=rp[c]["uout"], x2=shard_rows(x2, c), qf=rq[c]["qf"],
                        kvp=kvparts, tempc=tempc, abias=abias,
                        p1T=p1T, p1bc=p1bc, p2T=p2T, p2b=z["L1_attn_p2_b"]))
    ra = run_bass_kernel_spmd(nca, ima, core_ids=list(range(NCORES))).results
    if phase < 5:
        return [np.asarray(ra[c]["dbg"]) for c in range(NCORES)], rq, rp
    return unshard([ra[c]["x3"] for c in range(NCORES)])


def run_a0(z, x0):
    nc1 = mk_a0qkv()
    im1 = [dict(xin=shard_rows_halo(x0, c, HALO0),
                wq=z["L0_attn_Wq"], wk=z["L0_attn_Wk"], wv=z["L0_attn_Wv"],
                bqv=z["L0_attn_bq"].reshape(8, 128),
                bkv=z["L0_attn_bk"].reshape(2, 128),
                bvv=z["L0_attn_bv"]) for c in range(NCORES)]
    r1 = run_bass_kernel_spmd(nc1, im1, core_ids=list(range(NCORES))).results

    nc2 = mk_a0at()
    msk = a0_masks_np()
    im2 = [dict(qf=r1[c]["qf"], kf=r1[c]["kf"], vt=r1[c]["vt"],
                masks=msk,
                flagb=np.ascontiguousarray(
                    a0_flagbias_np(c % 4).transpose(2, 0, 1).reshape(128, 24)),
                sinkb=np.asarray([z["L0_attn_sink_bias"]], np.float32).reshape(1),
                wo=z["L0_attn_Wo"], bov=z["L0_attn_bo"],
                xown=shard_rows(x0, c)) for c in range(NCORES)]
    r2 = run_bass_kernel_spmd(nc2, im2, core_ids=list(range(NCORES))).results
    return unshard([r2[c]["xout"] for c in range(NCORES)])


def kernel(**inputs):
    z = {k: np.asarray(v) for k, v in inputs.items()}
    x = np.asarray(z["L0_attn_x"], np.float32).reshape(2, 8192, 1024)
    x = run_a0(z, x)
    x = run_f0(z, x)
    x = run_a1(z, x)
    x = run_f1(z, x)
    x = run_a2(z, x)
    x = run_f2(z, x)
    x = run_a3(z, x)
    x = run_f3(z, x)
    return np.ascontiguousarray(x.astype(np.float32))


